# version 12 (final)
# baseline (speedup 1.0000x reference)
import numpy as np
import ml_dtypes
import concourse.bass as bass
import concourse.mybir as mybir
from concourse.bass_utils import run_bass_kernel_spmd

F32 = mybir.dt.float32
BF16 = mybir.dt.bfloat16
I32 = mybir.dt.int32
U8 = mybir.dt.uint8
AF = mybir.ActivationFunctionType
ALU = mybir.AluOpType

DM = 2048
DI = 4096
NH = 64
HD = 64
DIP = 8512
CH = 128
KC_IN = 16
NOC_IN = 67
KC_OUT = 32
NOC_OUT = 16
HID = 5632
NOC_G = 44
NOC_D = 16
EPS = 1e-5
BPP = 202752
TTMAX = 1152


def _f_tiles(tt):
    out, o = [], 0
    while o < tt:
        w = min(512, tt - o)
        out.append((o, w))
        o += w
    return out


class Mk:
    RX = 102656

    def __init__(self, nc, group_tiles, dev_taps=(), prepass_tiles=()):
        self.nc = nc
        self.gt = group_tiles
        self.ng = len(group_tiles)
        self.pt = list(prepass_tiles)
        self.np_ = len(self.pt)
        self.nslots = self.np_ + self.ng
        self.nt = sum(group_tiles)
        self.ntok_real = (self.nt - 1) * 128
        self.dev = set(dev_taps)
        self.c = {}
        self.emitter = {}
        self.waited = {}
        self.dram = {}
        self.ev = {}

    def inc(self, inst, sem, eng, n=1):
        inst.then_inc(self.S[sem], n)
        self.c[sem] += n
        ev = (sem, self.c[sem])
        self.emitter[ev] = eng
        return ev

    def wait(self, eng, engname, *evs):
        for ev in evs:
            if ev is None:
                continue
            sem, val = ev
            if sem in ("pe", "dv", "ac", "po") and self.emitter.get(ev) == engname:
                continue
            w = self.waited.setdefault(engname, {})
            if w.get(sem, -1) >= val:
                continue
            eng.wait_ge(self.S[sem], val)
            w[sem] = val

    def din(self, name, shape, dt):
        self.dram[name] = self.nc.dram_tensor(name, shape, dt, kind="ExternalInput").ap()

    def dout(self, name, shape, dt):
        self.dram[name] = self.nc.dram_tensor(name, shape, dt, kind="ExternalOutput").ap()

    def dscr(self, name, shape, dt):
        kind = "ExternalOutput" if name in self.dev else "Internal"
        self.dram[name] = self.nc.dram_tensor(name, shape, dt, kind=kind).ap()

    def reg(self, off, nbytes, dt, parts=128):
        assert off % 4 == 0 and off + nbytes <= BPP, (off, nbytes)
        v = self.pool[0:parts, off:off + nbytes]
        if dt is not U8:
            v = v.bitcast(dt)
        return v

    def bank(self, i, n=1):
        return self.ps[:, i * 512:(i + n) * 512]

    def build(self):
        nc = self.nc
        self.din("u", [self.nt * 128, DM], F32)
        self.din("win_main", [66, 128, KC_IN, 128], F32)
        self.din("win_tail", [128, KC_IN, 64], F32)
        self.din("wout", [NOC_OUT, 128, KC_OUT, 128], F32)
        self.din("wg", [NOC_G, 128, KC_IN, 128], I32)
        self.din("wu", [NOC_G, 128, KC_IN, 128], I32)
        self.din("wd", [NOC_D, 128, NOC_G, 128], I32)
        self.din("convw", [128, 34, 4], F32)
        self.din("convb", [128, 34], F32)
        self.din("dtb", [64, 1], F32)
        self.din("alog", [64, 1], F32)
        self.din("dpar", [128, 32], F32)
        self.din("normw", [128, 32], F32)
        self.din("sg", [128, NOC_G], F32)
        self.din("su", [128, NOC_G], F32)
        self.din("sd", [128, NOC_D], F32)
        self.din("ident_bf", [128, 128], BF16)
        self.din("ident_f", [128, 128], F32)
        self.din("tri_bf", [128, 128], BF16)
        self.din("reset", [64, TTMAX], F32)
        self.din("ones_bf", [128, 1], BF16)
        self.din("cvals", [128, 2], F32)
        if self.np_:
            self.din("u2", [sum(self.pt) * 128, DM], F32)
            self.din("smask", [128, 1], F32)

        self.dout("out", [self.ntok_real, DM], F32)

        ns = self.nslots
        self.dscr("xn_sc", [ns, TTMAX, DM], BF16)
        self.dscr("z_sc", [ns, DI, TTMAX], BF16)
        self.dscr("xb_sc", [ns, DI, TTMAX], BF16)
        self.dscr("y_sc", [ns, DI, 1024], BF16)
        self.dscr("acs_sc", [ns, 8, 64, CH], F32)
        self.dscr("rstd_sc", [ns, 1, 1024], F32)
        self.dscr("attn_sc", [ns, DM, 1024], BF16)
        self.dscr("x1_sc", [ns, 1024, DM], F32)
        self.dscr("xn2_sc", [ns, 1024, DM], BF16)
        self.dscr("h_sc", [ns, HID, 1024], BF16)
        self.dscr("ffn_sc", [ns, DM, 1024], BF16)
        if self.dev:
            self.dout("dv_dt", [ns, 64, TTMAX], F32)
            self.dout("dv_acs", [ns, 64, TTMAX], F32)
            self.dout("dv_S", [ns, 128, DI], F32)
            self.dout("dv_Sp", [128, DI], F32)
            self.dout("dv_mt", [ns, 128, 8192], BF16)
            self.dout("dv_mt2", [ns, 128, 8192], BF16)
            self.dout("dv_xch", [ns, 128, 4096], BF16)
            self.dout("dv_cbt", [ns, 128, 128], BF16)

        import contextlib
        with contextlib.ExitStack() as st:
            pool_t = st.enter_context(nc.sbuf_tensor("pool", [128, BPP], U8))
            self.pool = pool_t[:]
            names = [("cw", [128, 34 * 4], F32), ("cb", [128, 34], F32),
                     ("s_dtb", [64, 1], F32), ("s_alog", [64, 1], F32),
                     ("acol", [64, 1], F32), ("s_dpar", [128, 32], F32),
                     ("nw", [128, 32], F32), ("s_sg", [128, NOC_G], F32),
                     ("s_su", [128, NOC_G], F32), ("s_sd", [128, NOC_D], F32),
                     ("idb", [128, 128], BF16), ("idf", [128, 128], F32),
                     ("trib", [128, 128], BF16), ("rmask", [64, TTMAX], F32),
                     ("onesb", [128, 1], BF16), ("scol", [128, 24], F32),
                     ("cval", [128, 2], F32), ("smk", [128, 1], F32)]
            self.sb = {}
            for nm, shape, dt in names:
                self.sb[nm] = st.enter_context(nc.sbuf_tensor(nm, shape, dt))[:]
            nc.const_aps.register(F32, EPS, self.sb["cval"][:, 0:1])
            nc.const_aps.register(F32, 1.0, self.sb["cval"][:, 1:2])
            self.ps = st.enter_context(nc.psum_tensor("ps", [128, 4096], F32))[:]
            self.S = {}
            for s in ("qs", "qg", "qc", "pe", "dv", "ac", "po"):
                self.S[s] = st.enter_context(nc.semaphore(s))
                self.c[s] = 0
            self.blk = st.enter_context(nc.Block())
            self.emit_all()
        return nc

    def emit_all(self):
        self.prologue()
        tb = 0
        for s in range(self.np_):
            nt = self.pt[s]
            gA = gW = gPE = gDVE = self.ev.get(("slot_done", s - 1))
            self.phaseA(s, tb, nt, nt * 128, uname="u2", gateA=gA)
            self.phaseB(s, nt * 128, prepass=True, gateW=gW, gatePE=gPE, gateDVE=gDVE)
            self.phaseC(s, nt * 128, 0, prepass=True)
            tb += nt
        if self.np_:
            self.mask_block()
        tb = 0
        for g in range(self.ng):
            s = self.np_ + g
            nt = self.gt[g]
            tt = nt * 128
            c0 = 1 if g == 0 else 0
            STRICT = True
            if g == 0:
                gA = self.ev.get(("slot_mask",)) if STRICT else None
                gW = self.ev.get(("slot_mask",)) if STRICT else self.ev.get(("b_mm", s - 1, 66))
                gPE = self.ev.get(("slot_mask",)) if STRICT else self.ev.get(("c_sadd", s - 1, 7, 1))
                gDVE = self.ev.get(("slot_mask",)) if STRICT else self.ev.get(("c_asum", s - 1, 7, 1))
            elif STRICT:
                gA = gW = gPE = gDVE = self.ev.get(("slot_done", s - 1))
            else:
                gA = self.ev.get(("f_umm", s - 1, NOC_G - 1))
                gW = self.ev.get(("f_umm", s - 1, NOC_G - 1))
                gPE = self.ev.get(("f_fev", s - 1, 1, NOC_D - 1))
                gDVE = self.ev.get(("f_out", s - 1, 7))
            self.phaseA(s, tb, nt, tt, uname="u", gateA=gA)
            self.phaseB(s, tt, gateW=gW, gatePE=gPE, gateDVE=gDVE)
            self.phaseC(s, tt, c0)
            self.phaseE(s, g, tb, c0)
            self.phaseF(s, g)
            tb += nt
        self.epilogue()

    def mask_block(self):
        S = self.reg(0, 16384, F32)
        Sbf = self.reg(16384, 8192, BF16)
        hist = self.reg(24576, 204, BF16)
        last_p = self.np_ - 1

        @self.blk.sync
        def _(sync):
            self.wait(sync, "sync", self.ev["consts"])
            ev = self.inc(sync.dma_start(out=self.sb["smk"], in_=self.dram["smask"]),
                          "qs", "sync", 16)
            self.ev["smk"] = ev

        if self.dev:
            @self.blk.sync
            def _(sync):
                self.wait(sync, "sync", self.ev[("c_sadd", last_p, 7, 1)])
                ev = self.inc(sync.dma_start(out=self.dram["dv_Sp"], in_=S),
                              "qs", "sync", 16)
                self.ev["sptap"] = ev

        @self.blk.vector
        def _(v):
            self.wait(v, "vec", self.ev["smk"],
                      self.ev[("c_sadd", last_p, 7, 1)],
                      self.ev.get("sptap"))
            v.tensor_scalar(out=S, in0=S, scalar1=self.sb["smk"][:],
                            scalar2=None, op0=ALU.mult)
            v.memset(hist, 0.0)
            ev = self.inc(v.tensor_copy(out=Sbf, in_=S), "dv", "vec")
            self.ev[("c_sbf", self.np_, -1)] = ev
            self.ev[("slot_mask",)] = ev

    def prologue(self):
        d, sb = self.dram, self.sb

        @self.blk.sync
        def _(sync):
            last = None
            for nm, dst in [("convw", sb["cw"]), ("convb", sb["cb"]),
                            ("dtb", sb["s_dtb"]), ("alog", sb["s_alog"]),
                            ("dpar", sb["s_dpar"]), ("normw", sb["nw"]),
                            ("sg", sb["s_sg"]), ("su", sb["s_su"]), ("sd", sb["s_sd"]),
                            ("ident_bf", sb["idb"]), ("ident_f", sb["idf"]),
                            ("tri_bf", sb["trib"]), ("reset", sb["rmask"]),
                            ("ones_bf", sb["onesb"]), ("cvals", sb["cval"])]:
                last = self.inc(sync.dma_start(out=dst, in_=d[nm]), "qs", "sync", 16)
            self.ev["consts"] = last

        S = self.reg(0, 16384, F32)
        Sbf = self.reg(16384, 8192, BF16)
        hist = self.reg(24576, 204, BF16)

        @self.blk.scalar
        def _(a):
            self.wait(a, "act", self.ev["consts"])
            ev = self.inc(a.activation(out=sb["acol"], in_=sb["s_alog"], func=AF.Exp),
                          "ac", "act")
            self.ev["aexp"] = ev

        @self.blk.vector
        def _(v):
            self.wait(v, "vec", self.ev["consts"], self.ev["aexp"])
            v.memset(S, 0.0)
            v.memset(Sbf, 0.0)
            v.memset(hist, 0.0)
            ev = self.inc(v.tensor_scalar(out=sb["acol"], in0=sb["acol"],
                                          scalar1=-1.0, scalar2=None, op0=ALU.mult),
                          "dv", "vec")
            self.ev["init"] = ev
            self.ev["acol"] = ev

    def phaseA(self, g, tb, nt, tt, uname="u", gateA=None):
        d, sb = self.dram, self.sb
        XFA = self.reg(24832, 36864, BF16).rearrange("p (kc t) -> p kc t", kc=16)
        u_t = [self.reg(self.RX + 20000, 8192, F32),
               self.reg(self.RX + 28192, 8192, F32)]
        xn_t = [self.reg(self.RX + 36384, 4096, BF16),
                self.reg(self.RX + 40480, 4096, BF16)]
        sq = self.reg(self.RX + 44576, 8192, F32)
        scol = sb["scol"]

        for i in range(nt):
            p = i % 2
            row0 = (tb + i) * 128

            @self.blk.sync
            def _(sync, i=i, p=p, row0=row0):
                gate = gateA if i < 2 else None
                gate2 = self.ev.get(("a_xfa", g - 1)) if i < 2 else None
                self.wait(sync, "sync", gate, gate2, self.ev.get(("a_xn", g, i - 2)))
                ev = self.inc(sync.dma_start(out=u_t[p], in_=d[uname][row0:row0 + 128, :]),
                              "qs", "sync", 16)
                self.ev[("a_u", g, i)] = ev

            @self.blk.scalar
            def _(a, i=i, p=p):
                self.wait(a, "act", self.ev[("a_u", g, i)])
                ev = self.inc(a.activation(out=sq, in_=u_t[p], func=AF.Square,
                                           accum_out=scol[:, 8 * p + 1:8 * p + 2]),
                              "ac", "act")
                self.ev[("a_sq", g, i)] = ev

            @self.blk.vector
            def _(v, i=i, p=p):
                self.wait(v, "vec", self.ev[("a_u", g, i)])
                rv = self.inc(v.tensor_reduce(out=scol[:, 8 * p:8 * p + 1], in_=u_t[p],
                                              axis=mybir.AxisListType.X, op=ALU.add),
                              "dv", "vec")
                v.wait_ge(self.S["dv"], rv[1])
                v.tensor_scalar(out=scol[:, 8 * p + 2:8 * p + 3],
                                in0=scol[:, 8 * p:8 * p + 1],
                                scalar1=1.0 / DM, scalar2=None, op0=ALU.mult)
                self.wait(v, "vec", self.ev[("a_sq", g, i)])
                v.tensor_scalar(out=scol[:, 8 * p + 1:8 * p + 2],
                                in0=scol[:, 8 * p + 1:8 * p + 2],
                                scalar1=1.0 / DM, scalar2=None, op0=ALU.mult)
                ev = self.inc(v.scalar_tensor_tensor(
                    out=scol[:, 8 * p + 3:8 * p + 4],
                    in0=scol[:, 8 * p + 2:8 * p + 3],
                    scalar=scol[:, 8 * p + 2:8 * p + 3],
                    in1=scol[:, 8 * p + 1:8 * p + 2],
                    op0=ALU.mult, op1=ALU.subtract), "dv", "vec")
                self.ev[("a_var", g, i)] = ev

            @self.blk.scalar
            def _(a, i=i, p=p):
                self.wait(a, "act", self.ev[("a_var", g, i)])
                a.activation(out=scol[:, 8 * p + 4:8 * p + 5],
                             in_=scol[:, 8 * p + 3:8 * p + 4],
                             func=AF.Ln, scale=-1.0, bias=EPS)
                ev = self.inc(a.activation(out=scol[:, 8 * p + 4:8 * p + 5],
                                           in_=scol[:, 8 * p + 4:8 * p + 5],
                                           func=AF.Exp, scale=-0.5), "ac", "act")
                self.ev[("a_stf", g, i)] = ev

            @self.blk.vector
            def _(v, i=i, p=p):
                self.wait(v, "vec", self.ev[("a_stf", g, i)],
                          self.ev.get(("a_xndma", g, i - 2)))
                ev = self.inc(v.tensor_scalar(out=xn_t[p], in0=u_t[p],
                                              scalar1=scol[:, 8 * p + 2:8 * p + 3],
                                              scalar2=scol[:, 8 * p + 4:8 * p + 5],
                                              op0=ALU.subtract, op1=ALU.mult),
                              "dv", "vec")
                self.ev[("a_xn", g, i)] = ev

            @self.blk.sync
            def _(sync, i=i, p=p):
                self.wait(sync, "sync", self.ev[("a_xn", g, i)])
                ev = self.inc(sync.dma_start(
                    out=d["xn_sc"][g, i * 128:(i + 1) * 128, :], in_=xn_t[p]),
                    "qs", "sync", 16)
                self.ev[("a_xndma", g, i)] = ev

        @self.blk.sync
        def _(sync):
            self.wait(sync, "sync", self.ev[("a_xndma", g, nt - 1)],
                      self.ev.get(("b_mm", g - 1, 66)))
            last = None
            for kc in range(16):
                last = self.inc(sync.dma_start(
                    out=XFA[:, kc, 0:tt],
                    in_=d["xn_sc"][g, 0:tt, kc * 128:(kc + 1) * 128],
                    transpose=True), "qs", "sync", 16)
            self.ev[("a_xfa", g)] = last

    def phaseB(self, g, tt, prepass=False, gateW=None, gatePE=None, gateDVE=None):
        d, sb = self.dram, self.sb
        XFA = self.reg(24832, 36864, BF16).rearrange("p (kc t) -> p kc t", kc=16)
        wbuf = [self.reg(61696, 11264, BF16), self.reg(61696 + 11264, 11264, BF16)]
        dtF = self.reg(84224, 4608, F32, parts=64)
        aF = self.reg(88832, 4608, F32, parts=64)
        acsF = self.reg(93440, 4608, F32, parts=64)
        BCF = self.reg(98048, 4608, BF16).rearrange("p (a t) -> p a t", a=2)
        zbuf = [self.reg(self.RX, 2304, BF16), self.reg(self.RX + 2304, 2304, BF16)]
        raw = [self.reg(self.RX + 4608, 2312, BF16), self.reg(self.RX + 6920, 2312, BF16)]
        acc = [self.reg(self.RX + 9232, 4608, F32), self.reg(self.RX + 13840, 4608, F32)]
        cout = [self.reg(self.RX + 18448, 2304, BF16),
                self.reg(self.RX + 18448 + 2304, 2304, BF16)]
        hist = self.reg(24576, 204, BF16).rearrange("p (a b) -> p a b", a=34)
        ftiles = _f_tiles(tt)
        cwv = sb["cw"].rearrange("p (a b) -> p a b", a=34)

        oclist = (list(range(32, 65)) + [66]) if prepass else list(range(NOC_IN))
        for oci, oc in enumerate(oclist):
            p = oc % 2
            mwid = 64 if oc == 66 else 128

            @self.blk.gpsimd
            def _(gp, oc=oc, p=p, mwid=mwid, oci=oci):
                gate = gateW if oci < 2 else None
                self.wait(gp, "gp", gate, self.ev.get(("b_mm", g, oc - 2)))
                src = d["win_main"][oc] if oc < 66 else d["win_tail"]
                dst = wbuf[p][:, 0:16 * mwid].rearrange("p (kc o) -> p kc o", kc=16)
                ev = self.inc(gp.dma_start(out=dst, in_=src), "qg", "gp", 16)
                self.ev[("b_w", g, oc)] = ev

            @self.blk.tensor
            def _(t, oc=oc, p=p, mwid=mwid, oci=oci):
                self.wait(t, "pe", self.ev[("b_w", g, oc)], self.ev[("a_xfa", g)],
                          gatePE if oci < 2 else None,
                          self.ev.get(("b_evac", g, oc - 2)))
                wv = wbuf[p][:, 0:16 * mwid].rearrange("p (kc o) -> p kc o", kc=16)
                last = None
                for fi, (fo, fw) in enumerate(ftiles):
                    pb = self.bank(3 * p + fi)[0:mwid, 0:fw]
                    for kc in range(16):
                        last = t.matmul(pb, lhsT=wv[:, kc, :], rhs=XFA[:, kc, fo:fo + fw],
                                        start=(kc == 0), stop=(kc == 15))
                ev = self.inc(last, "pe", "pe")
                self.ev[("b_mm", g, oc)] = ev

            if oc < 32:
                @self.blk.vector
                def _(v, oc=oc, p=p, oci=oci):
                    self.wait(v, "vec", self.ev[("b_mm", g, oc)],
                              gateDVE if oci < 2 else None,
                              self.ev.get(("b_zdma", g, oc - 2)))
                    last = None
                    for fi, (fo, fw) in enumerate(ftiles):
                        last = v.tensor_copy(out=zbuf[p][:, fo:fo + fw],
                                             in_=self.bank(3 * p + fi)[:, 0:fw])
                    ev = self.inc(last, "dv", "vec")
                    self.ev[("b_evac", g, oc)] = ev

                @self.blk.sync
                def _(sync, oc=oc, p=p):
                    self.wait(sync, "sync", self.ev[("b_evac", g, oc)])
                    ev = self.inc(sync.dma_start(
                        out=d["z_sc"][g, oc * 128:(oc + 1) * 128, 0:tt], in_=zbuf[p][:, 0:tt]),
                        "qs", "sync", 16)
                    self.ev[("b_zdma", g, oc)] = ev

            elif oc < 66:
                hp = oc - 32

                @self.blk.vector
                def _(v, oc=oc, p=p, hp=hp, oci=oci):
                    self.wait(v, "vec", self.ev[("b_mm", g, oc)],
                              gateDVE if oci < 2 else None)
                    v.tensor_copy(out=raw[p][:, 0:3], in_=hist[:, hp, :])
                    last = None
                    for fi, (fo, fw) in enumerate(ftiles):
                        last = v.tensor_copy(out=raw[p][:, 3 + fo:3 + fo + fw],
                                             in_=self.bank(3 * p + fi)[:, 0:fw])
                    bv = self.inc(last, "dv", "vec")
                    v.wait_ge(self.S["dv"], bv[1])
                    ev = self.inc(v.tensor_copy(out=hist[:, hp, :],
                                                in_=raw[p][:, tt:tt + 3]), "dv", "vec")
                    self.ev[("b_evac", g, oc)] = ev
                    self.wait(v, "vec", self.ev.get(("b_silu", g, oc - 2)))
                    v.tensor_scalar(out=acc[p][:, 0:tt], in0=raw[p][:, 0:tt],
                                    scalar1=cwv[:, hp, 0:1], scalar2=None, op0=ALU.mult)
                    last = None
                    for j in (1, 2, 3):
                        last = v.scalar_tensor_tensor(out=acc[p][:, 0:tt],
                                                      in0=raw[p][:, j:j + tt],
                                                      scalar=cwv[:, hp, j:j + 1],
                                                      in1=acc[p][:, 0:tt],
                                                      op0=ALU.mult, op1=ALU.add)
                    ev = self.inc(last, "dv", "vec")
                    self.ev[("b_conv", g, oc)] = ev

                @self.blk.scalar
                def _(a, oc=oc, p=p, hp=hp):
                    self.wait(a, "act", self.ev[("b_conv", g, oc)],
                              self.ev.get(("b_xdma", g, oc - 2)))
                    dst = cout[p][:, 0:tt] if hp < 32 else BCF[:, hp - 32, 0:tt]
                    ev = self.inc(a.activation(out=dst, in_=acc[p][:, 0:tt], func=AF.Silu,
                                               bias=sb["cb"][:, hp:hp + 1]), "ac", "act")
                    self.ev[("b_silu", g, oc)] = ev

                if hp < 32:
                    @self.blk.sync
                    def _(sync, oc=oc, p=p, hp=hp):
                        self.wait(sync, "sync", self.ev[("b_silu", g, oc)])
                        ev = self.inc(sync.dma_start(
                            out=d["xb_sc"][g, hp * 128:(hp + 1) * 128, 0:tt],
                            in_=cout[p][:, 0:tt]), "qs", "sync", 16)
                        self.ev[("b_xdma", g, oc)] = ev

            else:
                @self.blk.vector
                def _(v, p=p):
                    self.wait(v, "vec", self.ev[("b_mm", g, 66)])
                    for fi, (fo, fw) in enumerate(ftiles):
                        v.tensor_scalar(out=dtF[:, fo:fo + fw],
                                        in0=self.bank(3 * p + fi)[0:64, 0:fw],
                                        scalar1=sb["s_dtb"][:], scalar2=None, op0=ALU.add)
                    ev = self.inc(v.scalar_tensor_tensor(
                        out=aF[:, 0:tt], in0=dtF[:, 0:tt], scalar=-1.0,
                        in1=dtF[:, 0:tt], op0=ALU.mult, op1=ALU.max), "dv", "vec")
                    self.ev[("b_evac", g, 66)] = ev

                @self.blk.scalar
                def _(a):
                    self.wait(a, "act", self.ev[("b_evac", g, 66)])
                    a.activation(out=aF[:, 0:tt], in_=aF[:, 0:tt], func=AF.Exp, scale=-1.0)
                    ev = self.inc(a.activation(out=aF[:, 0:tt], in_=aF[:, 0:tt],
                                               func=AF.Ln, bias=1.0), "ac", "act")
                    self.ev[("b_sp1", g)] = ev

                @self.blk.vector
                def _(v):
                    self.wait(v, "vec", self.ev[("b_sp1", g)], self.ev["acol"])
                    v.tensor_scalar(out=dtF[:, 0:tt], in0=dtF[:, 0:tt],
                                    scalar1=0.0, scalar2=None, op0=ALU.max)
                    v.tensor_tensor(out=dtF[:, 0:tt], in0=dtF[:, 0:tt],
                                    in1=aF[:, 0:tt], op=ALU.add)
                    v.tensor_scalar(out=aF[:, 0:tt], in0=dtF[:, 0:tt],
                                    scalar1=sb["acol"][:], scalar2=None, op0=ALU.mult)
                    ev = self.inc(v.tensor_tensor_scan(
                        out=acsF[:, 0:tt], data0=sb["rmask"][:, 0:tt],
                        data1=aF[:, 0:tt], initial=0.0, op0=ALU.mult, op1=ALU.add),
                        "dv", "vec")
                    self.ev[("b_dt", g)] = ev

        if self.dev:
            @self.blk.sync
            def _(sync):
                self.wait(sync, "sync", self.ev[("b_dt", g)])
                self.inc(sync.dma_start(out=d["dv_dt"][g, :, 0:tt], in_=dtF[:, 0:tt]),
                         "qs", "sync", 16)
                self.inc(sync.dma_start(out=d["dv_acs"][g, :, 0:tt], in_=acsF[:, 0:tt]),
                         "qs", "sync", 16)

    def phaseC(self, g, tt, c0, prepass=False):
        d, sb = self.dram, self.sb
        S = self.reg(0, 16384, F32)
        Sbf = self.reg(16384, 8192, BF16)
        dtF = self.reg(84224, 4608, F32, parts=64)
        acsF = self.reg(93440, 4608, F32, parts=64)
        BCF = self.reg(98048, 4608, BF16).rearrange("p (a t) -> p a t", a=2)
        term1s = [self.reg(self.RX, 16384, F32),
                  self.reg(24832, 16384, F32)]
        MT = self.reg(self.RX + 16384, 16384, BF16)
        MT2 = self.reg(self.RX + 73728, 16384, BF16)
        MTv = MT.rearrange("p (h l) -> p h l", h=64)
        MT2v = MT2.rearrange("p (h l) -> p h l", h=64)
        xdr = [self.reg(self.RX + 32768, 8192, BF16),
               self.reg(61696, 8192, BF16)]
        xch = [self.reg(self.RX + 49152, 8192, BF16),
               self.reg(61696 + 8192, 8192, BF16)]
        xdd = self.reg(self.RX + 40960, 8192, BF16)
        ych = [self.reg(self.RX + 65536, 8192, BF16),
               self.reg(24832 + 24576, 8192, BF16)]
        zch = self.reg(self.RX + 57344, 8192, BF16)
        sqch = self.reg(24832 + 16384, 8192, BF16)
        smts = [self.reg(self.RX + 90112, 2048, F32),
                self.reg(61696 + 16384, 2048, F32)]
        CBts = [self.reg(self.RX + 92160, 256, BF16),
                self.reg(61696 + 18432, 256, BF16)]
        Btoks = [self.reg(self.RX + 92416, 256, BF16),
                 self.reg(61696 + 18688, 256, BF16)]
        rrepc = self.reg(self.RX + 92672, 512, F32)
        rstbf = self.reg(self.RX + 93184, 256, BF16)
        nwv = self.sb["nw"][:].unsqueeze(2)
        zoff = 128 if g == self.np_ else 0
        nch = tt // 128

        def smf(png):
            smt = smts[png]
            return dict(acs_tok=smt[:, 0:64], dt_tok=smt[:, 64:128],
                        asum=smt[:, 128:192], dd=smt[:, 192:256],
                        etile=smt[:, 256:320], ddt=smt[:, 320:384],
                        rstrow=smt[:, 384:512])

        def FEa(c):
            r = c - c0
            cs = c * 128
            png = r % 2
            sm = smf(png)
            CBt, Btok = CBts[png], Btoks[png]

            @self.blk.tensor
            def _(t, c=c, r=r, cs=cs):
                self.wait(t, "pe", self.ev[("b_dt", g)],
                          self.ev[("b_silu", g, 64 if prepass else 65)],
                          self.ev.get(("c_sm", g, r - 1)))
                p4 = self.bank(4)
                t.transpose(p4[0:128, 0:64], in_=acsF[:, cs:cs + 128],
                            identity=sb["idf"][0:64, 0:64])
                t.transpose(p4[0:128, 64:128], in_=dtF[:, cs:cs + 128],
                            identity=sb["idf"][0:64, 0:64])
                last = t.transpose(p4.bitcast(BF16)[:, 512:640],
                                   in_=BCF[:, 0, cs:cs + 128], identity=sb["idb"])
                if not prepass:
                    last = t.matmul(p4[:, 128:256], lhsT=BCF[:, 0, cs:cs + 128],
                                    rhs=BCF[:, 1, cs:cs + 128], start=True, stop=True)
                ev = self.inc(last, "pe", "pe")
                self.ev[("c_p4", g, r)] = ev

            @self.blk.vector
            def _(v, r=r, sm=sm, CBt=CBt, Btok=Btok):
                self.wait(v, "vec", self.ev[("c_p4", g, r)],
                          self.ev.get(("c_xd", g, r - 2)))
                p4 = self.bank(4)
                v.tensor_copy(out=sm["acs_tok"], in_=p4[:, 0:64])
                v.tensor_copy(out=sm["dt_tok"], in_=p4[:, 64:128])
                bv = self.inc(v.tensor_copy(out=Btok, in_=p4.bitcast(BF16)[:, 512:640]),
                              "dv", "vec")
                v.wait_ge(self.S["dv"], bv[1])
                if prepass:
                    self.ev[("c_sm", g, r)] = bv
                else:
                    ev = self.inc(v.tensor_tensor(out=CBt, in0=p4[:, 128:256],
                                                  in1=sb["trib"], op=ALU.mult), "dv", "vec")
                    self.ev[("c_sm", g, r)] = ev

            @self.blk.sync
            def _(sync, r=r, cs=cs):
                self.wait(sync, "sync", self.ev[("b_dt", g)])
                ev = self.inc(sync.dma_start(out=d["acs_sc"][g, r],
                                             in_=acsF[:, cs:cs + 128]), "qs", "sync", 16)
                self.ev[("c_acssc", g, r)] = ev

            @self.blk.sync
            def _(sync, r=r, cs=cs, png=png):
                war = self.ev.get(("c_xd", g, r - 1)) if prepass else \
                    self.ev.get(("c_ylast", g, r - 2))
                bgate = None if prepass else self.ev[("b_dt", g)]
                bgate2 = None if prepass else self.ev[("b_silu", g, 65)]
                self.wait(sync, "sync", self.ev[("b_xdma", g, 63)], war, bgate, bgate2)
                ev = self.inc(sync.dma_start(out=xdr[png],
                                             in_=d["xb_sc"][g, :, cs:cs + 128],
                                             transpose=True), "qs", "sync", 16)
                self.ev[("c_xdr", g, r)] = ev

            @self.blk.gpsimd
            def _(gp, r=r, cs=cs, png=png):
                if prepass:
                    return
                self.wait(gp, "gp", self.ev[("b_xdma", g, 63)],
                          self.ev.get(("c_yev", g, r - 2, 15)),
                          self.ev[("b_dt", g)], self.ev[("b_silu", g, 65)])
                src = d["xb_sc"][g].rearrange("(hp p) t -> p hp t", p=128)[:, :, cs:cs + 128]
                ev = self.inc(gp.dma_start(
                    out=xch[png].rearrange("p (hp t) -> p hp t", hp=32), in_=src),
                    "qg", "gp", 16)
                self.ev[("c_xch", g, r)] = ev

            for H in (0, 1):
                @self.blk.gpsimd
                def _(gp, r=r, H=H):
                    if prepass:
                        free = self.ev.get(("c_asum", g, r - 1, H))
                    else:
                        free = self.ev.get(("c_exp", g, r - 1, H))
                    self.wait(gp, "gp", self.ev[("c_acssc", g, r)], free)
                    srcp = d["acs_sc"][g, r, 32 * H:32 * (H + 1), :]
                    srcp = srcp.rearrange("h l -> (h l)").unsqueeze(0).broadcast_to([128, 4096])
                    ev = self.inc(gp.dma_start(out=term1s[H], in_=srcp), "qg", "gp", 16)
                    self.ev[("c_rep", g, r, H)] = ev

        def FEb(c):
            r = c - c0
            cs = c * 128
            png = r % 2
            sm = smf(png)
            CBt = CBts[png]

            if prepass:
                for H in (0, 1):
                    @self.blk.vector
                    def _(v, r=r, H=H, sm=sm):
                        self.wait(v, "vec", self.ev[("c_rep", g, r, H)],
                                  self.ev[("c_sm", g, r)])
                        t1v = term1s[H].rearrange("p (h l) -> p h l", h=32)
                        ev = self.inc(v.tensor_copy(
                            out=sm["asum"][:, 32 * H:32 * (H + 1)].unsqueeze(2),
                            in_=t1v[:, :, 127:128]), "dv", "vec")
                        self.ev[("c_asum", g, r, H)] = ev

                @self.blk.vector
                def _(v, r=r, sm=sm):
                    self.wait(v, "vec", self.ev[("c_asum", g, r, 1)])
                    ev = self.inc(v.tensor_tensor(out=sm["dd"], in0=sm["asum"],
                                                  in1=sm["acs_tok"],
                                                  op=ALU.subtract), "dv", "vec")
                    self.ev[("c_ddraw", g, r)] = ev

                @self.blk.scalar
                def _(a, r=r, sm=sm):
                    self.wait(a, "act", self.ev[("c_ddraw", g, r)])
                    a.activation(out=sm["dd"], in_=sm["dd"], func=AF.Exp)
                    ev = self.inc(a.activation(out=sm["etile"], in_=sm["asum"],
                                               func=AF.Exp), "ac", "act")
                    self.ev[("c_dde", g, r)] = ev

                @self.blk.vector
                def _(v, r=r, sm=sm, png=png):
                    self.wait(v, "vec", self.ev[("c_dde", g, r)],
                              self.ev[("c_xdr", g, r)],
                              self.ev.get(("c_stmm", g, r - 1, 1)))
                    bv = self.inc(v.tensor_tensor(out=sm["ddt"], in0=sm["dd"],
                                                  in1=sm["dt_tok"], op=ALU.mult),
                                  "dv", "vec")
                    v.wait_ge(self.S["dv"], bv[1])
                    xv = xdr[png].rearrange("p (h q) -> p h q", h=64)
                    ev = self.inc(v.tensor_tensor(
                        out=xdd.rearrange("p (h q) -> p h q", h=64), in0=xv,
                        in1=sm["ddt"].unsqueeze(2).broadcast_to([128, 64, 64]),
                        op=ALU.mult), "dv", "vec")
                    self.ev[("c_xd", g, r)] = ev
                return

            @self.blk.gpsimd
            def _(gp, r=r, sm=sm):
                self.wait(gp, "gp", self.ev[("c_rep", g, r, 0)],
                          self.ev[("c_rep", g, r, 1)], self.ev[("c_sm", g, r)],
                          self.ev.get(("c_dde", g, r - 2)))
                for H in (0, 1):
                    t1v = term1s[H].rearrange("p (h l) -> p h l", h=32)
                    gp.tensor_copy(out=sm["asum"][:, 32 * H:32 * (H + 1)].unsqueeze(2),
                                   in_=t1v[:, :, 127:128])
                ev = self.inc(gp.tensor_tensor(out=sm["dd"], in0=sm["asum"],
                                               in1=sm["acs_tok"], op=ALU.subtract),
                              "po", "gp")
                self.ev[("c_ddraw", g, r)] = ev

            @self.blk.scalar
            def _(a, r=r, sm=sm):
                self.wait(a, "act", self.ev[("c_ddraw", g, r)])
                a.activation(out=sm["dd"], in_=sm["dd"], func=AF.Exp)
                ev = self.inc(a.activation(out=sm["etile"], in_=sm["asum"],
                                           func=AF.Exp), "ac", "act")
                self.ev[("c_dde", g, r)] = ev

            @self.blk.gpsimd
            def _(gp, r=r, sm=sm, png=png):
                self.wait(gp, "gp", self.ev[("c_dde", g, r)], self.ev[("c_xdr", g, r)],
                          self.ev.get(("c_stmm", g, r - 1, 1)))
                bv = self.inc(gp.tensor_tensor(out=sm["ddt"], in0=sm["dd"],
                                               in1=sm["dt_tok"], op=ALU.mult),
                              "po", "gp")
                gp.wait_ge(self.S["po"], bv[1])
                xv = xdr[png].rearrange("p (h q) -> p h q", h=64)
                gp.tensor_tensor(out=xdd.rearrange("p (h q) -> p h q", h=64),
                                 in0=xv,
                                 in1=sm["ddt"].unsqueeze(2).broadcast_to([128, 64, 64]),
                                 op=ALU.mult)
                ev = self.inc(gp.tensor_tensor(
                    out=xv, in0=xv,
                    in1=sm["dt_tok"].unsqueeze(2).broadcast_to([128, 64, 64]),
                    op=ALU.mult), "po", "gp")
                self.ev[("c_xd", g, r)] = ev

            for H in (0, 1):
                @self.blk.scalar
                def _(a, r=r, H=H):
                    war = self.ev.get(("c_y", g, r - 1, 7)) if H == 0 else \
                        self.ev.get(("c_ylast", g, r - 1))
                    self.wait(a, "act", self.ev[("c_rep", g, r, H)], war,
                              self.ev.get(("c_devtap", g)) if (self.dev and r == 2) else None)
                    ev = self.inc(a.activation(out=MT2[:, 4096 * H:4096 * (H + 1)],
                                               in_=term1s[H], func=AF.Exp), "ac", "act")
                    self.ev[("c_e2", g, r, H)] = ev

            for H in (0, 1):
                @self.blk.gpsimd
                def _(gp, r=r, H=H, sm=sm):
                    self.wait(gp, "gp", self.ev[("c_e2", g, r, H)],
                              self.ev[("c_sm", g, r)])
                    t1v = term1s[H].rearrange("p (h l) -> p h l", h=32)
                    view = sm["acs_tok"][:, 32 * H:32 * (H + 1)].unsqueeze(2) \
                        .broadcast_to([128, 32, 128])
                    ev = self.inc(gp.tensor_tensor(out=t1v, in0=view, in1=t1v,
                                                   op=ALU.subtract), "po", "gp")
                    self.ev[("c_dneg", g, r, H)] = ev

            for H in (0, 1):
                @self.blk.scalar
                def _(a, r=r, H=H):
                    war = self.ev.get(("c_y", g, r - 1, 7)) if H == 0 else \
                        self.ev.get(("c_ylast", g, r - 1))
                    self.wait(a, "act", self.ev[("c_dneg", g, r, H)], war)
                    a.activation(out=term1s[H], in_=term1s[H], func=AF.Relu)
                    ev = self.inc(a.activation(out=MT[:, 4096 * H:4096 * (H + 1)],
                                               in_=term1s[H], func=AF.Exp, scale=-1.0),
                                  "ac", "act")
                    self.ev[("c_exp", g, r, H)] = ev

            for H in (0, 1):
                @self.blk.vector
                def _(v, r=r, H=H, cs=cs):
                    self.wait(v, "vec", self.ev[("c_e2", g, r, H)],
                              self.ev[("b_silu", g, 65)])
                    m2v = MT2v[:, 32 * H:32 * (H + 1), :]
                    cview = BCF[:, 1, cs:cs + 128].unsqueeze(1).broadcast_to([128, 32, 128])
                    ev = self.inc(v.tensor_tensor(out=m2v, in0=m2v, in1=cview,
                                                  op=ALU.mult), "dv", "vec")
                    self.ev[("c_mt2", g, r, H)] = ev

            for H in (0, 1):
                @self.blk.gpsimd
                def _(gp, r=r, H=H, CBt=CBt):
                    self.wait(gp, "gp", self.ev[("c_exp", g, r, H)])
                    mtv_ = MTv[:, 32 * H:32 * (H + 1), :]
                    cbv = CBt.unsqueeze(1).broadcast_to([128, 32, 128])
                    ev = self.inc(gp.tensor_tensor(out=mtv_, in0=mtv_, in1=cbv,
                                                   op=ALU.mult), "po", "gp")
                    self.ev[("c_mt", g, r, H)] = ev

        def BE1(c):
            r = c - c0
            png = r % 2
            sm = smf(png)
            Btok = Btoks[png]

            @self.blk.tensor
            def _(t, r=r, Btok=Btok):
                self.wait(t, "pe", self.ev[("c_xd", g, r)], self.ev[("c_sm", g, r)],
                          self.ev.get(("c_sadd", g, r - 1, 1)))
                last = None
                for i in range(4):
                    last = t.matmul(self.bank(i), lhsT=Btok,
                                    rhs=xdd[:, i * 512:(i + 1) * 512],
                                    start=True, stop=True)
                ev = self.inc(last, "pe", "pe")
                self.ev[("c_stmm", g, r, 0)] = ev

            @self.blk.vector
            def _(v, r=r, sm=sm):
                self.wait(v, "vec", self.ev[("c_dde", g, r)])
                v.tensor_tensor(out=S.rearrange("p (h q) -> p h q", h=64),
                                in0=S.rearrange("p (h q) -> p h q", h=64),
                                in1=sm["etile"].unsqueeze(2).broadcast_to([128, 64, 64]),
                                op=ALU.mult)
                self.wait(v, "vec", self.ev[("c_stmm", g, r, 0)])
                last = None
                for i in range(4):
                    last = v.tensor_tensor(out=S[:, i * 512:(i + 1) * 512],
                                           in0=S[:, i * 512:(i + 1) * 512],
                                           in1=self.bank(i), op=ALU.add)
                ev = self.inc(last, "dv", "vec")
                self.ev[("c_sadd", g, r, 0)] = ev

            @self.blk.tensor
            def _(t, r=r, Btok=Btok):
                self.wait(t, "pe", self.ev[("c_sadd", g, r, 0)])
                last = None
                for i in range(4):
                    last = t.matmul(self.bank(i), lhsT=Btok,
                                    rhs=xdd[:, 2048 + i * 512:2048 + (i + 1) * 512],
                                    start=True, stop=True)
                ev = self.inc(last, "pe", "pe")
                self.ev[("c_stmm", g, r, 1)] = ev

            @self.blk.vector
            def _(v, r=r):
                self.wait(v, "vec", self.ev[("c_stmm", g, r, 1)])
                last = None
                for i in range(4):
                    last = v.tensor_tensor(out=S[:, 2048 + i * 512:2048 + (i + 1) * 512],
                                           in0=S[:, 2048 + i * 512:2048 + (i + 1) * 512],
                                           in1=self.bank(i), op=ALU.add)
                ev = self.inc(last, "dv", "vec")
                self.ev[("c_sadd", g, r, 1)] = ev

        def BE2(c):
            r = c - c0
            png = r % 2
            sm = smf(png)
            for j in range(16):
                q = j % 2
                bk = 5 + q
                heads = [8 * (j // 2) + 2 * i + q for i in range(4)]

                @self.blk.tensor
                def _(t, r=r, j=j, q=q, bk=bk, heads=heads, png=png):
                    bank_war = self.ev.get(("c_yev", g, r - 1, 14 + q)) if j < 2 \
                        else self.ev[("c_yev", g, r, j - 2)]
                    self.wait(t, "pe",
                              self.ev[("c_mt", g, r, 0 if j < 8 else 1)],
                              self.ev[("c_mt2", g, r, 0 if j < 8 else 1)],
                              self.ev[("c_xd", g, r)],
                              self.ev.get(("c_sbf", g, r - 1)),
                              self.ev.get(("c_rcp", g, r - 1)) if bk == 6 else None,
                              bank_war)
                    pb = self.bank(bk)
                    first = True
                    last = None
                    for i, h in enumerate(heads):
                        outp = pb[0:64, i * 128:(i + 1) * 128]
                        last = t.matmul(outp, lhsT=xdr[png][:, h * 64:h * 64 + 64],
                                        rhs=MTv[:, h, :], start=first, stop=False)
                        first = False
                        last = t.matmul(outp, lhsT=Sbf[:, h * 64:h * 64 + 64],
                                        rhs=MT2v[:, h, :],
                                        start=False, stop=(i == 3))
                    ev = self.inc(last, "pe", "pe")
                    self.ev[("c_y", g, r, j)] = ev
                    if j == 15:
                        self.ev[("c_ylast", g, r)] = ev

                @self.blk.vector
                def _(v, r=r, j=j, q=q, bk=bk, heads=heads, png=png):
                    self.wait(v, "vec", self.ev[("c_y", g, r, j)],
                              self.ev[("c_xch", g, r)],
                              self.ev.get(("c_ydma", g, r - 2)))
                    pb = self.bank(bk)
                    b0 = q * 64
                    xv = xch[png].rearrange("p (hp t) -> p hp t", hp=32)
                    yv = ych[png].rearrange("p (hp t) -> p hp t", hp=32)
                    last = None
                    for i, h in enumerate(heads):
                        hp = h // 2
                        last = v.scalar_tensor_tensor(
                            out=yv[b0:b0 + 64, hp, :], in0=xv[b0:b0 + 64, hp, :],
                            scalar=sb["s_dpar"][b0:b0 + 64, hp:hp + 1],
                            in1=pb[0:64, i * 128:(i + 1) * 128],
                            op0=ALU.mult, op1=ALU.add)
                    ev = self.inc(last, "dv", "vec")
                    self.ev[("c_yev", g, r, j)] = ev

            @self.blk.scalar
            def _(a, r=r):
                self.wait(a, "act", self.ev[("c_ylast", g, r)],
                          self.ev[("c_sadd", g, r, 1)])
                ev = self.inc(a.activation(out=Sbf, in_=S, func=AF.Copy), "ac", "act")
                self.ev[("c_sbf", g, r)] = ev

            @self.blk.sync
            def _(sync, r=r):
                self.wait(sync, "sync", self.ev[("b_zdma", g, 31)],
                          self.ev.get(("c_yg", g, r - 1)))
                ev = self.inc(sync.dma_start(
                    out=zch.rearrange("p (hp t) -> p hp t", hp=32),
                    in_=d["z_sc"][g].rearrange("(hp p) t -> p hp t", p=128)[:, :, zoff + r * 128:zoff + (r + 1) * 128]),
                    "qs", "sync", 16)
                self.ev[("c_zld", g, r)] = ev

            @self.blk.scalar
            def _(a, r=r):
                self.wait(a, "act", self.ev[("c_zld", g, r)])
                ev = self.inc(a.activation(out=zch, in_=zch, func=AF.Silu), "ac", "act")
                self.ev[("c_zsilu", g, r)] = ev

            @self.blk.vector
            def _(v, r=r, png=png):
                self.wait(v, "vec", self.ev[("c_zsilu", g, r)],
                          self.ev[("c_yev", g, r, 15)])
                bv = self.inc(v.tensor_tensor(out=ych[png], in0=ych[png], in1=zch,
                                              op=ALU.mult), "dv", "vec")
                self.ev[("c_yg", g, r)] = bv
                v.wait_ge(self.S["dv"], bv[1])
                ev = self.inc(v.tensor_tensor(out=sqch, in0=ych[png], in1=ych[png],
                                              op=ALU.mult), "dv", "vec")
                self.ev[("c_sq", g, r)] = ev

            @self.blk.tensor
            def _(t, r=r):
                self.wait(t, "pe", self.ev[("c_sq", g, r)],
                          self.ev.get(("c_rstd", g, r - 1)))
                sv = sqch.rearrange("p (hp t) -> p hp t", hp=32)
                last = None
                for hp in range(32):
                    last = t.matmul(self.bank(7)[0:1, 0:128], lhsT=sb["onesb"][:],
                                    rhs=sv[:, hp, :], start=(hp == 0), stop=(hp == 31))
                ev = self.inc(last, "pe", "pe")
                self.ev[("c_rms", g, r)] = ev

            @self.blk.scalar
            def _(a, r=r, sm=sm):
                self.wait(a, "act", self.ev[("c_rms", g, r)])
                a.activation(out=sm["rstrow"][0:1, 0:128], in_=self.bank(7)[0:1, 0:128],
                             func=AF.Ln, scale=1.0 / DI, bias=EPS)
                a.activation(out=sm["rstrow"][0:1, 0:128], in_=sm["rstrow"][0:1, 0:128],
                             func=AF.Exp, scale=-0.5)
                ev = self.inc(a.activation(out=rstbf[0:1, 0:128],
                                           in_=sm["rstrow"][0:1, 0:128], func=AF.Copy),
                              "ac", "act")
                self.ev[("c_rstd", g, r)] = ev

            @self.blk.tensor
            def _(t, r=r):
                self.wait(t, "pe", self.ev[("c_rstd", g, r)],
                          self.ev.get(("c_rcp", g, r - 1)))
                ev = self.inc(t.matmul(self.bank(6)[:, 0:128],
                                       lhsT=sb["trib"][0:1, :],
                                       rhs=rstbf[0:1, 0:128],
                                       start=True, stop=True), "pe", "pe")
                self.ev[("c_rrep", g, r)] = ev

            @self.blk.vector
            def _(v, r=r, png=png):
                self.wait(v, "vec", self.ev[("c_rrep", g, r)])
                bv = self.inc(v.tensor_copy(out=rrepc, in_=self.bank(6)[:, 0:128]),
                              "dv", "vec")
                self.ev[("c_rcp", g, r)] = bv
                v.wait_ge(self.S["dv"], bv[1])
                y3 = ych[png].rearrange("p (hp t) -> p hp t", hp=32)
                v.tensor_tensor(out=y3, in0=y3, in1=nwv.broadcast_to([128, 32, 128]),
                                op=ALU.mult)
                ev = self.inc(v.tensor_tensor(
                    out=y3, in0=y3,
                    in1=rrepc.unsqueeze(1).broadcast_to([128, 32, 128]),
                    op=ALU.mult), "dv", "vec")
                self.ev[("c_yn", g, r)] = ev

            @self.blk.sync
            def _(sync, r=r, png=png):
                self.wait(sync, "sync", self.ev[("c_yn", g, r)])
                dst = d["y_sc"][g].rearrange("(hp p) t -> p hp t", p=128)[:, :, r * 128:(r + 1) * 128]
                ev = self.inc(sync.dma_start(
                    out=dst, in_=ych[png].rearrange("p (hp t) -> p hp t", hp=32)),
                    "qs", "sync", 16)
                self.ev[("c_ydma", g, r)] = ev

            if self.dev and r == 1:
                @self.blk.sync
                def _(sync, r=r, png=png):
                    self.wait(sync, "sync", self.ev[("c_ylast", g, r)],
                              self.ev[("c_yev", g, r, 15)])
                    self.inc(sync.dma_start(out=d["dv_mt"][g], in_=MT), "qs", "sync", 16)
                    self.inc(sync.dma_start(out=d["dv_mt2"][g], in_=MT2), "qs", "sync", 16)
                    self.inc(sync.dma_start(out=d["dv_xch"][g], in_=xch[png]), "qs", "sync", 16)
                    ev = self.inc(sync.dma_start(out=d["dv_cbt"][g], in_=CBts[png]), "qs", "sync", 16)
                    self.ev[("c_devtap", g)] = ev

        if prepass:
            for c in range(c0, nch):
                FEa(c)
                FEb(c)
                BE1(c)
            self.ev[("slot_done", g)] = self.ev[("c_sadd", g, nch - c0 - 1, 1)]
            return

        FEa(c0)
        FEb(c0)
        for c in range(c0, nch):
            BE1(c)
            if c + 1 < nch:
                FEa(c + 1)
            BE2(c)
            if c + 1 < nch:
                FEb(c + 1)

        self.ev[("c_done", g)] = self.ev[("c_ydma", g, nch - c0 - 1)]
        self.ev[("c_yevlast", g)] = self.ev[("c_yev", g, nch - c0 - 1, 15)]
        if self.dev:
            @self.blk.sync
            def _(sync):
                self.wait(sync, "sync", self.ev[("c_sbf", g, nch - c0 - 1)])
                self.inc(sync.dma_start(out=d["dv_S"][g], in_=S), "qs", "sync", 16)

    def phaseD(self, g):
        d, sb = self.dram, self.sb
        YN = self.reg(self.RX, 65536, BF16)
        ysl = self.reg(self.RX + 65536, 8192, BF16)
        zsl = self.reg(self.RX + 73728, 8192, BF16)
        qsl = self.reg(self.RX + 81920, 8192, BF16)
        rrep = self.reg(self.RX + 90112, 4096, F32)
        rsum = self.reg(self.RX + 94208, 4096, F32)
        zoff = 128 if g == self.np_ else 0

        for s8 in range(8):
            @self.blk.sync
            def _(sync, s8=s8):
                self.wait(sync, "sync", self.ev[("c_done", g)],
                          self.ev.get(("d_mv", g, s8 - 1)))
                sync.dma_start(
                    out=ysl.rearrange("p (hp t) -> p hp t", hp=4),
                    in_=d["y_sc"][g].rearrange("(hp p) t -> p hp t", p=128)[:, 4 * s8:4 * s8 + 4, :]
                ).then_inc(self.S["qs"], 16)
                self.c["qs"] += 16
                self.wait(sync, "sync", self.ev.get(("d_yg", g, s8 - 1)))
                ev = self.inc(sync.dma_start(
                    out=zsl.rearrange("p (hp t) -> p hp t", hp=4),
                    in_=d["z_sc"][g].rearrange("(hp p) t -> p hp t", p=128)[:, 4 * s8:4 * s8 + 4, zoff:zoff + 1024]),
                    "qs", "sync", 16)
                self.ev[("d_ld", g, s8)] = ev

            @self.blk.scalar
            def _(a, s8=s8):
                self.wait(a, "act", self.ev[("d_ld", g, s8)])
                ev = self.inc(a.activation(out=zsl, in_=zsl, func=AF.Silu), "ac", "act")
                self.ev[("d_silu", g, s8)] = ev

            @self.blk.vector
            def _(v, s8=s8):
                self.wait(v, "vec", self.ev[("d_silu", g, s8)])
                ev = self.inc(v.tensor_tensor(out=ysl, in0=ysl, in1=zsl, op=ALU.mult),
                              "dv", "vec")
                self.ev[("d_yg", g, s8)] = ev

            @self.blk.scalar
            def _(a, s8=s8):
                self.wait(a, "act", self.ev[("d_yg", g, s8)],
                          self.ev.get(("d_rmsmm", g, s8 - 1)))
                ev = self.inc(a.activation(out=qsl, in_=ysl, func=AF.Square), "ac", "act")
                self.ev[("d_sq", g, s8)] = ev

            @self.blk.vector
            def _(v, s8=s8):
                self.wait(v, "vec", self.ev[("d_sq", g, s8)])
                ev = self.inc(v.tensor_copy(out=YN[:, s8 * 4096:(s8 + 1) * 4096],
                                            in_=ysl), "dv", "vec")
                self.ev[("d_mv", g, s8)] = ev

            @self.blk.tensor
            def _(t, s8=s8):
                self.wait(t, "pe", self.ev[("d_sq", g, s8)],
                          self.ev.get(("d_racc", g, s8 - 1)))
                qv = qsl.rearrange("p (hp t) -> p hp t", hp=4)
                last = None
                for fi in range(2):
                    for hp in range(4):
                        last = t.matmul(self.bank(7 - fi)[0:1, 0:512],
                                        lhsT=sb["onesb"][:],
                                        rhs=qv[:, hp, fi * 512:(fi + 1) * 512],
                                        start=(hp == 0), stop=(hp == 3))
                ev = self.inc(last, "pe", "pe")
                self.ev[("d_rmsmm", g, s8)] = ev

            @self.blk.vector
            def _(v, s8=s8):
                self.wait(v, "vec", self.ev[("d_rmsmm", g, s8)])
                last = None
                for fi in range(2):
                    if s8 == 0:
                        last = v.tensor_copy(out=rsum[0:1, fi * 512:(fi + 1) * 512],
                                             in_=self.bank(7 - fi)[0:1, 0:512])
                    else:
                        last = v.tensor_tensor(out=rsum[0:1, fi * 512:(fi + 1) * 512],
                                               in0=rsum[0:1, fi * 512:(fi + 1) * 512],
                                               in1=self.bank(7 - fi)[0:1, 0:512],
                                               op=ALU.add)
                ev = self.inc(last, "dv", "vec")
                self.ev[("d_racc", g, s8)] = ev

        @self.blk.scalar
        def _(a):
            self.wait(a, "act", self.ev[("d_racc", g, 7)])
            rr = rrep[0:1, 0:1024]
            a.activation(out=rr, in_=rsum[0:1, 0:1024], func=AF.Ln,
                         scale=1.0 / DI, bias=EPS)
            ev = self.inc(a.activation(out=rr, in_=rr, func=AF.Exp, scale=-0.5),
                          "ac", "act")
            self.ev[("d_rsb", g)] = ev

        @self.blk.sync
        def _(sync):
            self.wait(sync, "sync", self.ev[("d_rsb", g)])
            ev = self.inc(sync.dma_start(out=d["rstd_sc"][g], in_=rrep[0:1, :]),
                          "qs", "sync", 16)
            self.ev[("d_rdma", g)] = ev

        @self.blk.gpsimd
        def _(gp):
            self.wait(gp, "gp", self.ev[("d_rdma", g)])
            src = d["rstd_sc"][g, 0].unsqueeze(0).broadcast_to([128, 1024])
            ev = self.inc(gp.dma_start(out=rrep, in_=src), "qg", "gp", 16)
            self.ev[("d_rrep", g)] = ev

        @self.blk.vector
        def _(v):
            self.wait(v, "vec", self.ev[("d_rrep", g)], self.ev[("d_mv", g, 7)])
            ynv = YN.rearrange("p (hp t) -> p hp t", hp=32)
            last = None
            for hp in range(32):
                last = v.scalar_tensor_tensor(out=ynv[:, hp, :], in0=ynv[:, hp, :],
                                              scalar=sb["nw"][:, hp:hp + 1], in1=rrep,
                                              op0=ALU.mult, op1=ALU.mult)
            ev = self.inc(last, "dv", "vec")
            self.ev[("d_yn", g)] = ev

    def phaseE(self, g, gmain, tb, c0):
        d, sb = self.dram, self.sb
        YN = self.reg(self.RX, 65536, BF16).rearrange("p (kc t) -> p kc t", kc=32)
        wbuf = [self.reg(61696, 8192, BF16), self.reg(61696 + 8192, 8192, BF16)]
        at_t = [self.reg(self.RX + 65536, 2048, BF16),
                self.reg(self.RX + 67584, 2048, BF16)]

        @self.blk.sync
        def _(sync):
            self.wait(sync, "sync", self.ev[("c_done", g)])
            ev = self.inc(sync.dma_start(
                out=YN, in_=d["y_sc"][g].rearrange("(kc p) t -> p kc t", p=128)),
                "qs", "sync", 16)
            self.ev[("e_yn", g)] = ev

        for oc in range(NOC_OUT):
            p = oc % 2

            @self.blk.gpsimd
            def _(gp, oc=oc, p=p):
                gate = self.ev[("c_yevlast", g)] if oc < 2 else None
                self.wait(gp, "gp", gate, self.ev.get(("e_mm", g, oc - 2)))
                dst = wbuf[p].rearrange("p (kc o) -> p kc o", kc=KC_OUT)
                ev = self.inc(gp.dma_start(out=dst, in_=d["wout"][oc]), "qg", "gp", 16)
                self.ev[("e_w", g, oc)] = ev

            @self.blk.tensor
            def _(t, oc=oc, p=p):
                self.wait(t, "pe", self.ev[("e_w", g, oc)], self.ev[("e_yn", g)],
                          self.ev.get(("e_evac", g, oc - 2)))
                wv = wbuf[p].rearrange("p (kc o) -> p kc o", kc=KC_OUT)
                last = None
                for fi in range(2):
                    pb = self.bank(2 * p + fi)
                    for kc in range(KC_OUT):
                        last = t.matmul(pb, lhsT=wv[:, kc, :],
                                        rhs=YN[:, kc, fi * 512:(fi + 1) * 512],
                                        start=(kc == 0), stop=(kc == KC_OUT - 1))
                ev = self.inc(last, "pe", "pe")
                self.ev[("e_mm", g, oc)] = ev

            @self.blk.vector
            def _(v, oc=oc, p=p):
                self.wait(v, "vec", self.ev[("e_mm", g, oc)],
                          self.ev.get(("e_atdma", g, oc - 2)))
                last = None
                for fi in range(2):
                    last = v.tensor_copy(out=at_t[p][:, fi * 512:(fi + 1) * 512],
                                         in_=self.bank(2 * p + fi))
                ev = self.inc(last, "dv", "vec")
                self.ev[("e_evac", g, oc)] = ev

            @self.blk.sync
            def _(sync, oc=oc, p=p):
                self.wait(sync, "sync", self.ev[("e_evac", g, oc)])
                ev = self.inc(sync.dma_start(
                    out=d["attn_sc"][g, oc * 128:(oc + 1) * 128, :], in_=at_t[p]),
                    "qs", "sync", 16)
                self.ev[("e_atdma", g, oc)] = ev

        at_tok = [self.reg(self.RX, 4096, BF16), self.reg(self.RX + 4096, 4096, BF16),
                  self.reg(61696, 4096, BF16)]
        u2 = [self.reg(self.RX + 8192, 8192, F32), self.reg(self.RX + 16384, 8192, F32),
              self.reg(61696 + 4096, 8192, F32)]
        x1 = [self.reg(self.RX + 24576, 8192, F32), self.reg(self.RX + 32768, 8192, F32),
              self.reg(61696 + 12288, 8192, F32)]
        xn2 = [self.reg(self.RX + 40960, 4096, BF16), self.reg(self.RX + 45056, 4096, BF16),
               self.reg(self.RX + 57344, 4096, BF16)]
        sq2 = self.reg(self.RX + 49152, 8192, F32)
        scol = sb["scol"]
        XFB = self.reg(165888, 32768, BF16)

        for i in range(8):
            p = i % 3
            urow = (tb + c0 + i) * 128

            @self.blk.sync
            def _(sync, i=i, p=p, urow=urow):
                self.wait(sync, "sync", self.ev[("e_atdma", g, NOC_OUT - 1)],
                          self.ev.get(("e_ln2", g, i - 3)))
                sync.dma_start(out=at_tok[p],
                               in_=d["attn_sc"][g, :, i * 128:(i + 1) * 128],
                               transpose=True).then_inc(self.S["qs"], 16)
                self.c["qs"] += 16
                self.wait(sync, "sync", self.ev.get(("e_sq", g, i - 3)))
                ev = self.inc(sync.dma_start(out=u2[p], in_=d["u"][urow:urow + 128, :]),
                              "qs", "sync", 16)
                self.ev[("e_ld", g, i)] = ev

            @self.blk.vector
            def _(v, i=i, p=p):
                self.wait(v, "vec", self.ev[("e_ld", g, i)],
                          self.ev.get(("e_x1dma", g, i - 3)))
                ev = self.inc(v.tensor_tensor(out=x1[p], in0=u2[p], in1=at_tok[p],
                                              op=ALU.add), "dv", "vec")
                self.ev[("e_x1", g, i)] = ev

            @self.blk.sync
            def _(sync, i=i, p=p):
                self.wait(sync, "sync", self.ev[("e_x1", g, i)])
                ev = self.inc(sync.dma_start(out=d["x1_sc"][g, i * 128:(i + 1) * 128, :],
                                             in_=x1[p]), "qs", "sync", 16)
                self.ev[("e_x1dma", g, i)] = ev

            @self.blk.scalar
            def _(a, i=i, p=p):
                self.wait(a, "act", self.ev[("e_x1", g, i)])
                ev = self.inc(a.activation(out=sq2, in_=x1[p], func=AF.Square,
                                           accum_out=scol[:, 8 * p + 1:8 * p + 2]),
                              "ac", "act")
                self.ev[("e_sq", g, i)] = ev

            @self.blk.vector
            def _(v, i=i, p=p):
                self.wait(v, "vec", self.ev[("e_sq", g, i)])
                rv = self.inc(v.tensor_reduce(out=scol[:, 8 * p:8 * p + 1], in_=x1[p],
                                              axis=mybir.AxisListType.X, op=ALU.add),
                              "dv", "vec")
                v.wait_ge(self.S["dv"], rv[1])
                v.tensor_scalar(out=scol[:, 8 * p + 2:8 * p + 3],
                                in0=scol[:, 8 * p:8 * p + 1],
                                scalar1=1.0 / DM, scalar2=None, op0=ALU.mult)
                v.tensor_scalar(out=scol[:, 8 * p + 1:8 * p + 2],
                                in0=scol[:, 8 * p + 1:8 * p + 2],
                                scalar1=1.0 / DM, scalar2=None, op0=ALU.mult)
                ev = self.inc(v.scalar_tensor_tensor(
                    out=scol[:, 8 * p + 3:8 * p + 4],
                    in0=scol[:, 8 * p + 2:8 * p + 3],
                    scalar=scol[:, 8 * p + 2:8 * p + 3],
                    in1=scol[:, 8 * p + 1:8 * p + 2],
                    op0=ALU.mult, op1=ALU.subtract), "dv", "vec")
                self.ev[("e_var", g, i)] = ev

            @self.blk.scalar
            def _(a, i=i, p=p):
                self.wait(a, "act", self.ev[("e_var", g, i)])
                a.activation(out=scol[:, 8 * p + 4:8 * p + 5],
                             in_=scol[:, 8 * p + 3:8 * p + 4],
                             func=AF.Ln, scale=-1.0, bias=EPS)
                ev = self.inc(a.activation(out=scol[:, 8 * p + 4:8 * p + 5],
                                           in_=scol[:, 8 * p + 4:8 * p + 5],
                                           func=AF.Exp, scale=-0.5), "ac", "act")
                self.ev[("e_stf", g, i)] = ev

            @self.blk.vector
            def _(v, i=i, p=p):
                self.wait(v, "vec", self.ev[("e_stf", g, i)],
                          self.ev.get(("e_xn2dma", g, i - 3)))
                ev = self.inc(v.tensor_scalar(out=xn2[p], in0=x1[p],
                                              scalar1=scol[:, 8 * p + 2:8 * p + 3],
                                              scalar2=scol[:, 8 * p + 4:8 * p + 5],
                                              op0=ALU.subtract, op1=ALU.mult),
                              "dv", "vec")
                self.ev[("e_ln2", g, i)] = ev

            @self.blk.sync
            def _(sync, i=i, p=p):
                self.wait(sync, "sync", self.ev[("e_ln2", g, i)])
                ev = self.inc(sync.dma_start(out=d["xn2_sc"][g, i * 128:(i + 1) * 128, :],
                                             in_=xn2[p]), "qs", "sync", 16)
                self.ev[("e_xn2dma", g, i)] = ev

        for hh in (0, 1):
            @self.blk.scalar
            def _(a, hh=hh):
                self.wait(a, "act", self.ev[("e_xn2dma", g, 4 * hh + 3)])
                XFBv = XFB.rearrange("p (kc t) -> p kc t", kc=16)
                last = None
                for kc in range(16):
                    last = self.inc(a.dma_start(
                        out=XFBv[:, kc, hh * 512:(hh + 1) * 512],
                        in_=d["xn2_sc"][g, hh * 512:(hh + 1) * 512,
                                        kc * 128:(kc + 1) * 128], transpose=True),
                        "qc", "act", 16)
                if hh == 1:
                    self.ev[("e_xfb", g)] = last

    def phaseF(self, g, gmain):
        d, sb = self.dram, self.sb
        XFB = self.reg(165888, 32768, BF16).rearrange("p (kc t) -> p kc t", kc=16)
        wbuf = [self.reg(61696 + 4096 * j, 4096, BF16) for j in range(4)]
        wtouch = self.reg(61696 + 16384, 16, BF16)
        gate_t = [self.reg(self.RX, 2048, BF16), self.reg(self.RX + 2048, 2048, BF16)]
        h_t = [self.reg(self.RX + 4096, 2048, BF16), self.reg(self.RX + 6144, 2048, BF16)]

        for oc in range(NOC_G):
            p = oc % 2

            @self.blk.gpsimd
            def _(gp, oc=oc, p=p):
                gate = self.ev[("e_xfb", g)] if oc < 2 else None
                self.wait(gp, "gp", gate, self.ev.get(("f_gmm", g, oc - 2)))
                ev = self.inc(gp.dma_start(
                    out=wbuf[2 * p].rearrange("p (kc o) -> p kc o", kc=16),
                    in_=d["wg"][oc]), "qg", "gp", 16)
                self.ev[("f_wgld", g, oc)] = ev
                self.wait(gp, "gp", self.ev.get(("f_umm", g, oc - 2)))
                ev2 = self.inc(gp.dma_start(
                    out=wbuf[2 * p + 1].rearrange("p (kc o) -> p kc o", kc=16),
                    in_=d["wu"][oc]), "qg", "gp", 16)
                self.ev[("f_wuld", g, oc)] = ev2

            @self.blk.vector
            def _(v, oc=oc, p=p):
                self.wait(v, "vec", self.ev[("f_wgld", g, oc)])
                v.tensor_scalar(out=wbuf[2 * p], in0=wbuf[2 * p],
                                scalar1=-128.0, scalar2=None, op0=ALU.add)
                ev = self.inc(v.tensor_copy(out=wtouch[:, 0:8],
                                            in_=wbuf[2 * p][:, 2040:2048]), "dv", "vec")
                self.ev[("f_wgs", g, oc)] = ev
                self.wait(v, "vec", self.ev[("f_wuld", g, oc)])
                v.tensor_scalar(out=wbuf[2 * p + 1], in0=wbuf[2 * p + 1],
                                scalar1=-128.0, scalar2=None, op0=ALU.add)
                ev2 = self.inc(v.tensor_copy(out=wtouch[:, 0:8],
                                             in_=wbuf[2 * p + 1][:, 2040:2048]), "dv", "vec")
                self.ev[("f_wus", g, oc)] = ev2

            @self.blk.tensor
            def _(t, oc=oc, p=p):
                self.wait(t, "pe", self.ev[("f_wgs", g, oc)], self.ev[("e_xfb", g)],
                          self.ev.get(("f_gev", g, oc - 1)))
                wv = wbuf[2 * p].rearrange("p (kc o) -> p kc o", kc=16)
                last = None
                for fi in range(2):
                    for kc in range(16):
                        last = t.matmul(self.bank(fi), lhsT=wv[:, kc, :],
                                        rhs=XFB[:, kc, fi * 512:(fi + 1) * 512],
                                        start=(kc == 0), stop=(kc == 15))
                ev = self.inc(last, "pe", "pe")
                self.ev[("f_gmm", g, oc)] = ev

            @self.blk.scalar
            def _(a, oc=oc, p=p):
                self.wait(a, "act", self.ev[("f_gmm", g, oc)],
                          self.ev.get(("f_h", g, oc - 2)))
                last = None
                for fi in range(2):
                    last = a.activation(out=gate_t[p][:, fi * 512:(fi + 1) * 512],
                                        in_=self.bank(fi), func=AF.Silu,
                                        scale=sb["s_sg"][:, oc:oc + 1])
                ev = self.inc(last, "ac", "act")
                self.ev[("f_gev", g, oc)] = ev

            @self.blk.tensor
            def _(t, oc=oc, p=p):
                self.wait(t, "pe", self.ev[("f_wus", g, oc)],
                          self.ev.get(("f_h", g, oc - 1)))
                wv = wbuf[2 * p + 1].rearrange("p (kc o) -> p kc o", kc=16)
                last = None
                for fi in range(2):
                    for kc in range(16):
                        last = t.matmul(self.bank(2 + fi), lhsT=wv[:, kc, :],
                                        rhs=XFB[:, kc, fi * 512:(fi + 1) * 512],
                                        start=(kc == 0), stop=(kc == 15))
                ev = self.inc(last, "pe", "pe")
                self.ev[("f_umm", g, oc)] = ev

            @self.blk.vector
            def _(v, oc=oc, p=p):
                self.wait(v, "vec", self.ev[("f_umm", g, oc)], self.ev[("f_gev", g, oc)],
                          self.ev.get(("f_hdma", g, oc - 2)))
                last = None
                for fi in range(2):
                    last = v.scalar_tensor_tensor(
                        out=h_t[p][:, fi * 512:(fi + 1) * 512],
                        in0=self.bank(2 + fi), scalar=sb["s_su"][:, oc:oc + 1],
                        in1=gate_t[p][:, fi * 512:(fi + 1) * 512],
                        op0=ALU.mult, op1=ALU.mult)
                ev = self.inc(last, "dv", "vec")
                self.ev[("f_h", g, oc)] = ev

            @self.blk.sync
            def _(sync, oc=oc, p=p):
                self.wait(sync, "sync", self.ev[("f_h", g, oc)])
                ev = self.inc(sync.dma_start(out=d["h_sc"][g, oc * 128:(oc + 1) * 128, :],
                                             in_=h_t[p]), "qs", "sync", 16)
                self.ev[("f_hdma", g, oc)] = ev

        self.ev[("f_gu", g)] = self.ev[("f_hdma", g, NOC_G - 1)]

        Hb = [self.reg(self.RX + 8192, 45056, BF16)]
        wdb = [self.reg(self.RX + 53248, 11264, BF16),
               self.reg(self.RX + 64512, 11264, BF16)]
        fb = [self.reg(self.RX + 75776, 1024, BF16),
              self.reg(self.RX + 76800, 1024, BF16)]

        for fh in range(2):
            hp_ = 0

            @self.blk.sync
            def _(sync, fh=fh, hp_=hp_):
                self.wait(sync, "sync", self.ev[("f_gu", g)],
                          self.ev.get(("f_dmm", g, fh - 1, NOC_D - 1)))
                e = self.inc(sync.dma_start(
                    out=Hb[hp_].rearrange("p (kc t) -> p kc t", kc=NOC_G),
                    in_=d["h_sc"][g].rearrange("(kc p) t -> p kc t", p=128)[:, :, fh * 512:(fh + 1) * 512]),
                    "qs", "sync", 16)
                self.ev[("f_hload", g, fh)] = e

            for oc in range(NOC_D):
                p = oc % 2

                @self.blk.gpsimd
                def _(gp, fh=fh, oc=oc, p=p):
                    self.wait(gp, "gp", self.ev.get(("f_dmm", g, fh, oc - 2)),
                              self.ev.get(("f_dmm", g, fh - 1, 14 + oc)) if (oc < 2 and fh > 0) else None,
                              self.ev[("f_gu", g)] if (oc < 2 and fh == 0) else None)
                    e = self.inc(gp.dma_start(
                        out=wdb[p].rearrange("p (kc o) -> p kc o", kc=NOC_G),
                        in_=d["wd"][oc]), "qg", "gp", 16)
                    self.ev[("f_wdld", g, fh, oc)] = e

                @self.blk.vector
                def _(v, fh=fh, oc=oc, p=p):
                    self.wait(v, "vec", self.ev[("f_wdld", g, fh, oc)])
                    e = self.inc(v.tensor_scalar(out=wdb[p], in0=wdb[p], scalar1=-128.0,
                                                 scalar2=None, op0=ALU.add), "dv", "vec")
                    self.ev[("f_wds", g, fh, oc)] = e

                @self.blk.tensor
                def _(t, fh=fh, oc=oc, p=p, hp_=hp_):
                    self.wait(t, "pe", self.ev[("f_wds", g, fh, oc)],
                              self.ev[("f_hload", g, fh)],
                              self.ev[("f_gev", g, NOC_G - 1)] if (fh == 0 and oc == 0) else None,
                              self.ev.get(("f_fev", g, fh, oc - 2)))
                    wv = wdb[p].rearrange("p (kc o) -> p kc o", kc=NOC_G)
                    Hv = Hb[hp_].rearrange("p (kc t) -> p kc t", kc=NOC_G)
                    last = None
                    for kc in range(NOC_G):
                        last = t.matmul(self.bank(p), lhsT=wv[:, kc, :], rhs=Hv[:, kc, :],
                                        start=(kc == 0), stop=(kc == NOC_G - 1))
                    e = self.inc(last, "pe", "pe")
                    self.ev[("f_dmm", g, fh, oc)] = e

                @self.blk.vector
                def _(v, fh=fh, oc=oc, p=p):
                    self.wait(v, "vec", self.ev[("f_dmm", g, fh, oc)],
                              self.ev.get(("f_fdma", g, fh, oc - 2)))
                    e = self.inc(v.tensor_scalar(out=fb[p], in0=self.bank(p),
                                                 scalar1=sb["s_sd"][:, oc:oc + 1],
                                                 scalar2=None, op0=ALU.mult), "dv", "vec")
                    self.ev[("f_fev", g, fh, oc)] = e

                @self.blk.sync
                def _(sync, fh=fh, oc=oc, p=p):
                    self.wait(sync, "sync", self.ev[("f_fev", g, fh, oc)])
                    e = self.inc(sync.dma_start(
                        out=d["ffn_sc"][g, oc * 128:(oc + 1) * 128,
                                        fh * 512:(fh + 1) * 512], in_=fb[p]),
                        "qs", "sync", 16)
                    self.ev[("f_fdma", g, fh, oc)] = e

        f_tok = [self.reg(self.RX + 8192, 4096, BF16),
                 self.reg(self.RX + 12288, 4096, BF16)]
        x1b = [self.reg(self.RX + 16384, 8192, F32), self.reg(self.RX + 24576, 8192, F32)]
        ot = [self.reg(self.RX + 32768, 8192, F32), self.reg(self.RX + 40960, 8192, F32)]

        for i in range(8):
            p = i % 2
            orow = gmain * 1024 + i * 128

            @self.blk.sync
            def _(sync, i=i, p=p):
                self.wait(sync, "sync", self.ev[("f_fdma", g, 1, NOC_D - 1)],
                          self.ev.get(("f_o", g, i - 2)))
                sync.dma_start(out=f_tok[p], in_=d["ffn_sc"][g, :, i * 128:(i + 1) * 128],
                               transpose=True).then_inc(self.S["qs"], 16)
                self.c["qs"] += 16
                ev = self.inc(sync.dma_start(out=x1b[p],
                                             in_=d["x1_sc"][g, i * 128:(i + 1) * 128, :]),
                              "qs", "sync", 16)
                self.ev[("f_old", g, i)] = ev

            @self.blk.vector
            def _(v, i=i, p=p):
                self.wait(v, "vec", self.ev[("f_old", g, i)],
                          self.ev.get(("f_out", g, i - 2)))
                ev = self.inc(v.tensor_tensor(out=ot[p], in0=x1b[p], in1=f_tok[p],
                                              op=ALU.add), "dv", "vec")
                self.ev[("f_o", g, i)] = ev

            @self.blk.sync
            def _(sync, i=i, p=p, orow=orow):
                self.wait(sync, "sync", self.ev[("f_o", g, i)])
                ev = self.inc(sync.dma_start(out=d["out"][orow:orow + 128, :], in_=ot[p]),
                              "qs", "sync", 16)
                self.ev[("f_out", g, i)] = ev

        self.ev[("slot_done", g)] = self.ev[("f_out", g, 7)]

    def epilogue(self):
        @self.blk.sync
        def _(sync):
            self.wait(sync, "sync", self.ev[("slot_done", self.nslots - 1)])


def host_prep(inputs):
    f32 = np.float32
    W_in = inputs["L0_attn_W_in"]
    A = np.ascontiguousarray(W_in.T).reshape(KC_IN, 128, DIP)
    win_main = np.ascontiguousarray(
        A[:, :, :66 * 128].reshape(KC_IN, 128, 66, 128).transpose(2, 1, 0, 3))
    win_tail = np.ascontiguousarray(A[:, :, 66 * 128:].transpose(1, 0, 2))
    W_out = inputs["L0_attn_W_out"]
    B = np.ascontiguousarray(W_out.T).reshape(KC_OUT, 128, DM)
    wout = np.ascontiguousarray(
        B.reshape(KC_OUT, 128, NOC_OUT, 128).transpose(2, 1, 0, 3))

    def blk_gate(wq):
        T = np.ascontiguousarray(wq.T).reshape(KC_IN, 128, HID)
        return np.ascontiguousarray(
            T.reshape(KC_IN, 128, NOC_G, 128).transpose(2, 1, 0, 3))

    wg = blk_gate(inputs["L0_ffn_wq_gate"])
    wu = blk_gate(inputs["L0_ffn_wq_up"])
    Td = np.ascontiguousarray(inputs["L0_ffn_wq_down"].T)
    wd = np.ascontiguousarray(
        Td.reshape(NOC_G, 128, NOC_D, 128).transpose(2, 1, 0, 3))

    convw = np.ascontiguousarray(
        inputs["L0_attn_conv_w"][:, 0, :].reshape(34, 128, 4).transpose(1, 0, 2))
    convb = np.ascontiguousarray(inputs["L0_attn_conv_b"].reshape(34, 128).T)
    shared = dict(
        win_main=win_main.astype(f32), win_tail=win_tail.astype(f32),
        wout=wout.astype(f32), wg=wg.astype(np.int32), wu=wu.astype(np.int32),
        wd=wd.astype(np.int32),
        convw=convw.reshape(128, 34, 4).astype(f32), convb=convb.astype(f32),
        dtb=inputs["L0_attn_dt_bias"].reshape(64, 1).astype(f32),
        alog=inputs["L0_attn_A_log"].reshape(64, 1).astype(f32),
        dpar=np.ascontiguousarray(
            np.repeat(inputs["L0_attn_D_param"], 64).reshape(32, 128).T).astype(f32),
        normw=np.ascontiguousarray(
            inputs["L0_attn_norm_w"].reshape(32, 128).T).astype(f32),
        sg=np.ascontiguousarray(inputs["L0_ffn_s_gate"].reshape(NOC_G, 128).T).astype(f32),
        su=np.ascontiguousarray(inputs["L0_ffn_s_up"].reshape(NOC_G, 128).T).astype(f32),
        sd=np.ascontiguousarray(inputs["L0_ffn_s_down"].reshape(NOC_D, 128).T).astype(f32),
        ident_bf=np.eye(128, dtype=f32).astype(ml_dtypes.bfloat16),
        ident_f=np.eye(128, dtype=f32),
        tri_bf=np.triu(np.ones((128, 128), f32)).astype(ml_dtypes.bfloat16),
        reset=np.ones((64, TTMAX), f32),
        ones_bf=np.ones((128, 1), f32).astype(ml_dtypes.bfloat16),
        cvals=np.concatenate([np.full((128, 1), EPS, f32), np.ones((128, 1), f32)], 1),
    )
    shared["reset"][:, 0::128] = 0.0
    return shared


def kernel(**inputs):
    u_full = np.asarray(inputs["L0_attn_u"])
    shared = host_prep(inputs)
    nc = bass.Bass("TRN2", target_bir_lowering=False)
    mk = Mk(nc, [9, 8], prepass_tiles=(8, 8))
    nc = mk.build()
    in_maps = []
    for c in range(8):
        b, h = c // 2, c % 2
        pad = np.zeros((128, DM), np.float32)
        if h == 1:
            pad[125:128] = u_full[b][2045:2048]
        m = dict(shared)
        m["u"] = np.ascontiguousarray(
            np.vstack([pad, u_full[b][h * 2048:(h + 1) * 2048]]))
        m["u2"] = np.ascontiguousarray(u_full[b][0:2048])
        m["smask"] = np.full((128, 1), float(h), np.float32)
        in_maps.append(m)
    res = run_bass_kernel_spmd(nc, in_maps, core_ids=list(range(8)))
    out = np.stack(
        [np.vstack([np.asarray(res.results[2 * b]["out"]),
                    np.asarray(res.results[2 * b + 1]["out"])]) for b in range(4)],
        axis=0)
    return out.astype(np.float32)


# version 13
# speedup vs baseline: 1.1679x; 1.1679x over previous
import numpy as np
import ml_dtypes
import concourse.bass as bass
import concourse.mybir as mybir
from concourse.bass_utils import run_bass_kernel_spmd

F32 = mybir.dt.float32
BF16 = mybir.dt.bfloat16
I32 = mybir.dt.int32
U8 = mybir.dt.uint8
AF = mybir.ActivationFunctionType
ALU = mybir.AluOpType

DM = 2048
DI = 4096
NH = 64
HD = 64
DIP = 8512
CH = 128
KC_IN = 16
NOC_IN = 67
KC_OUT = 32
NOC_OUT = 16
HID = 5632
NOC_G = 44
NOC_D = 16
EPS = 1e-5
BPP = 202752
TTMAX = 1152


def _f_tiles(tt):
    out, o = [], 0
    while o < tt:
        w = min(512, tt - o)
        out.append((o, w))
        o += w
    return out


class Mk:
    RX = 102656

    def __init__(self, nc, group_tiles, dev_taps=(), prepass_tiles=()):
        self.nc = nc
        self.gt = group_tiles
        self.ng = len(group_tiles)
        self.pt = list(prepass_tiles)
        self.np_ = len(self.pt)
        self.nslots = self.np_ + self.ng
        self.nt = sum(group_tiles)
        self.ntok_real = (self.nt - 1) * 128
        self.dev = set(dev_taps)
        self.c = {}
        self.emitter = {}
        self.waited = {}
        self.dram = {}
        self.ev = {}

    def inc(self, inst, sem, eng, n=1):
        inst.then_inc(self.S[sem], n)
        self.c[sem] += n
        ev = (sem, self.c[sem])
        self.emitter[ev] = eng
        return ev

    def wait(self, eng, engname, *evs):
        for ev in evs:
            if ev is None:
                continue
            sem, val = ev
            if sem in ("pe", "dv", "ac", "po") and self.emitter.get(ev) == engname:
                continue
            w = self.waited.setdefault(engname, {})
            if w.get(sem, -1) >= val:
                continue
            eng.wait_ge(self.S[sem], val)
            w[sem] = val

    def din(self, name, shape, dt):
        self.dram[name] = self.nc.dram_tensor(name, shape, dt, kind="ExternalInput").ap()

    def dout(self, name, shape, dt):
        self.dram[name] = self.nc.dram_tensor(name, shape, dt, kind="ExternalOutput").ap()

    def dscr(self, name, shape, dt):
        kind = "ExternalOutput" if name in self.dev else "Internal"
        self.dram[name] = self.nc.dram_tensor(name, shape, dt, kind=kind).ap()

    def reg(self, off, nbytes, dt, parts=128):
        assert off % 4 == 0 and off + nbytes <= BPP, (off, nbytes)
        v = self.pool[0:parts, off:off + nbytes]
        if dt is not U8:
            v = v.bitcast(dt)
        return v

    def bank(self, i, n=1):
        return self.ps[:, i * 512:(i + n) * 512]

    def build(self):
        nc = self.nc
        self.din("u", [self.nt * 128, DM], F32)
        self.din("win_main", [66, 128, KC_IN, 128], F32)
        self.din("win_tail", [128, KC_IN, 64], F32)
        self.din("wout", [NOC_OUT, 128, KC_OUT, 128], F32)
        self.din("wg", [NOC_G, 128, KC_IN, 128], I32)
        self.din("wu", [NOC_G, 128, KC_IN, 128], I32)
        self.din("wd", [NOC_D, 128, NOC_G, 128], I32)
        self.din("convw", [128, 34, 4], F32)
        self.din("convb", [128, 34], F32)
        self.din("dtb", [64, 1], F32)
        self.din("alog", [64, 1], F32)
        self.din("dpar", [128, 32], F32)
        self.din("normw", [128, 32], F32)
        self.din("sg", [128, NOC_G], F32)
        self.din("su", [128, NOC_G], F32)
        self.din("sd", [128, NOC_D], F32)
        self.din("ident_bf", [128, 128], BF16)
        self.din("ident_f", [128, 128], F32)
        self.din("tri_bf", [128, 128], BF16)
        self.din("reset", [64, TTMAX], F32)
        self.din("ones_bf", [128, 1], BF16)
        self.din("cvals", [128, 2], F32)
        if self.np_:
            self.din("u2", [sum(self.pt) * 128, DM], F32)
            self.din("smask", [128, 1], F32)

        self.dout("out", [self.ntok_real, DM], F32)

        ns = self.nslots
        self.dscr("xn_sc", [ns, TTMAX, DM], BF16)
        self.dscr("z_sc", [ns, DI, TTMAX], BF16)
        self.dscr("xb_sc", [ns, DI, TTMAX], BF16)
        self.dscr("y_sc", [ns, DI, 1024], BF16)
        self.dscr("acs_sc", [ns, 8, 64, CH], F32)
        self.dscr("rstd_sc", [ns, 1, 1024], F32)
        self.dscr("attn_sc", [ns, DM, 1024], BF16)
        self.dscr("x1_sc", [ns, 1024, DM], F32)
        self.dscr("xn2_sc", [ns, 1024, DM], BF16)
        self.dscr("h_sc", [ns, HID, 1024], BF16)
        self.dscr("ffn_sc", [ns, DM, 1024], BF16)
        if self.dev:
            self.dout("dv_dt", [ns, 64, TTMAX], F32)
            self.dout("dv_acs", [ns, 64, TTMAX], F32)
            self.dout("dv_S", [ns, 128, DI], F32)
            self.dout("dv_Sp", [128, DI], F32)
            self.dout("dv_mt", [ns, 128, 8192], BF16)
            self.dout("dv_mt2", [ns, 128, 8192], BF16)
            self.dout("dv_xch", [ns, 128, 4096], BF16)
            self.dout("dv_cbt", [ns, 128, 128], BF16)

        import contextlib
        with contextlib.ExitStack() as st:
            pool_t = st.enter_context(nc.sbuf_tensor("pool", [128, BPP], U8))
            self.pool = pool_t[:]
            names = [("cw", [128, 34 * 4], F32), ("cb", [128, 34], F32),
                     ("s_dtb", [64, 1], F32), ("s_alog", [64, 1], F32),
                     ("acol", [64, 1], F32), ("s_dpar", [128, 32], F32),
                     ("nw", [128, 32], F32), ("s_sg", [128, NOC_G], F32),
                     ("s_su", [128, NOC_G], F32), ("s_sd", [128, NOC_D], F32),
                     ("idb", [128, 128], BF16), ("idf", [128, 128], F32),
                     ("trib", [128, 128], BF16), ("rmask", [64, TTMAX], F32),
                     ("onesb", [128, 1], BF16), ("scol", [128, 24], F32),
                     ("cval", [128, 2], F32), ("smk", [128, 1], F32)]
            self.sb = {}
            for nm, shape, dt in names:
                self.sb[nm] = st.enter_context(nc.sbuf_tensor(nm, shape, dt))[:]
            nc.const_aps.register(F32, EPS, self.sb["cval"][:, 0:1])
            nc.const_aps.register(F32, 1.0, self.sb["cval"][:, 1:2])
            self.ps = st.enter_context(nc.psum_tensor("ps", [128, 4096], F32))[:]
            self.S = {}
            for s in ("qs", "qg", "qc", "pe", "dv", "ac", "po"):
                self.S[s] = st.enter_context(nc.semaphore(s))
                self.c[s] = 0
            self.blk = st.enter_context(nc.Block())
            self.emit_all()
        return nc

    def emit_all(self):
        self.prologue()
        tb = 0
        for s in range(self.np_):
            nt = self.pt[s]
            gA = gW = gPE = gDVE = self.ev.get(("slot_done", s - 1))
            self.phaseA(s, tb, nt, nt * 128, uname="u2", gateA=gA)
            self.phaseB(s, nt * 128, prepass=True, gateW=gW, gatePE=gPE, gateDVE=gDVE)
            self.phaseC(s, nt * 128, 0, prepass=True)
            tb += nt
        if self.np_:
            self.mask_block()
        tb = 0
        for g in range(self.ng):
            s = self.np_ + g
            nt = self.gt[g]
            tt = nt * 128
            c0 = 1 if g == 0 else 0
            STRICT = True
            if g == 0:
                gA = self.ev.get(("slot_mask",)) if STRICT else None
                gW = self.ev.get(("slot_mask",)) if STRICT else self.ev.get(("b_mm", s - 1, 66))
                gPE = self.ev.get(("slot_mask",)) if STRICT else self.ev.get(("c_sadd", s - 1, 7, 1))
                gDVE = self.ev.get(("slot_mask",)) if STRICT else self.ev.get(("c_asum", s - 1, 7, 1))
            elif STRICT:
                gA = gW = gPE = gDVE = self.ev.get(("slot_done", s - 1))
            else:
                gA = self.ev.get(("f_umm", s - 1, NOC_G - 1))
                gW = self.ev.get(("f_umm", s - 1, NOC_G - 1))
                gPE = self.ev.get(("f_fev", s - 1, 1, NOC_D - 1))
                gDVE = self.ev.get(("f_out", s - 1, 7))
            self.phaseA(s, tb, nt, tt, uname="u", gateA=gA)
            self.phaseB(s, tt, gateW=gW, gatePE=gPE, gateDVE=gDVE)
            self.phaseC(s, tt, c0)
            self.phaseE(s, g, tb, c0)
            self.phaseF(s, g)
            tb += nt
        self.epilogue()

    def mask_block(self):
        S = self.reg(0, 16384, F32)
        Sbf = self.reg(16384, 8192, BF16)
        hist = self.reg(24576, 204, BF16)
        last_p = self.np_ - 1

        @self.blk.sync
        def _(sync):
            self.wait(sync, "sync", self.ev["consts"])
            ev = self.inc(sync.dma_start(out=self.sb["smk"], in_=self.dram["smask"]),
                          "qs", "sync", 16)
            self.ev["smk"] = ev

        if self.dev:
            @self.blk.sync
            def _(sync):
                self.wait(sync, "sync", self.ev[("c_sadd", last_p, 7, 1)])
                ev = self.inc(sync.dma_start(out=self.dram["dv_Sp"], in_=S),
                              "qs", "sync", 16)
                self.ev["sptap"] = ev

        @self.blk.vector
        def _(v):
            self.wait(v, "vec", self.ev["smk"],
                      self.ev[("c_sadd", last_p, 7, 1)],
                      self.ev.get("sptap"))
            v.tensor_scalar(out=S, in0=S, scalar1=self.sb["smk"][:],
                            scalar2=None, op0=ALU.mult)
            v.memset(hist, 0.0)
            ev = self.inc(v.tensor_copy(out=Sbf, in_=S), "dv", "vec")
            self.ev[("c_sbf", self.np_, -1)] = ev
            self.ev[("slot_mask",)] = ev

    def prologue(self):
        d, sb = self.dram, self.sb

        @self.blk.sync
        def _(sync):
            last = None
            for nm, dst in [("convw", sb["cw"]), ("convb", sb["cb"]),
                            ("dtb", sb["s_dtb"]), ("alog", sb["s_alog"]),
                            ("dpar", sb["s_dpar"]), ("normw", sb["nw"]),
                            ("sg", sb["s_sg"]), ("su", sb["s_su"]), ("sd", sb["s_sd"]),
                            ("ident_bf", sb["idb"]), ("ident_f", sb["idf"]),
                            ("tri_bf", sb["trib"]), ("reset", sb["rmask"]),
                            ("ones_bf", sb["onesb"]), ("cvals", sb["cval"])]:
                last = self.inc(sync.dma_start(out=dst, in_=d[nm]), "qs", "sync", 16)
            self.ev["consts"] = last

        S = self.reg(0, 16384, F32)
        Sbf = self.reg(16384, 8192, BF16)
        hist = self.reg(24576, 204, BF16)

        @self.blk.scalar
        def _(a):
            self.wait(a, "act", self.ev["consts"])
            ev = self.inc(a.activation(out=sb["acol"], in_=sb["s_alog"], func=AF.Exp),
                          "ac", "act")
            self.ev["aexp"] = ev

        @self.blk.vector
        def _(v):
            self.wait(v, "vec", self.ev["consts"], self.ev["aexp"])
            v.memset(S, 0.0)
            v.memset(Sbf, 0.0)
            v.memset(hist, 0.0)
            ev = self.inc(v.tensor_scalar(out=sb["acol"], in0=sb["acol"],
                                          scalar1=-1.0, scalar2=None, op0=ALU.mult),
                          "dv", "vec")
            self.ev["init"] = ev
            self.ev["acol"] = ev

    def phaseA(self, g, tb, nt, tt, uname="u", gateA=None):
        d, sb = self.dram, self.sb
        XFA = self.reg(24832, 36864, BF16).rearrange("p (kc t) -> p kc t", kc=16)
        u_t = [self.reg(self.RX + 20000, 8192, F32),
               self.reg(self.RX + 28192, 8192, F32)]
        xn_t = [self.reg(self.RX + 36384, 4096, BF16),
                self.reg(self.RX + 40480, 4096, BF16)]
        sq = self.reg(self.RX + 44576, 8192, F32)
        scol = sb["scol"]

        for i in range(nt):
            p = i % 2
            row0 = (tb + i) * 128

            @self.blk.sync
            def _(sync, i=i, p=p, row0=row0):
                gate = gateA if i < 2 else None
                gate2 = self.ev.get(("a_xfa", g - 1)) if i < 2 else None
                self.wait(sync, "sync", gate, gate2, self.ev.get(("a_xn", g, i - 2)))
                ev = self.inc(sync.dma_start(out=u_t[p], in_=d[uname][row0:row0 + 128, :]),
                              "qs", "sync", 16)
                self.ev[("a_u", g, i)] = ev

            @self.blk.scalar
            def _(a, i=i, p=p):
                self.wait(a, "act", self.ev[("a_u", g, i)])
                ev = self.inc(a.activation(out=sq, in_=u_t[p], func=AF.Square,
                                           accum_out=scol[:, 8 * p + 1:8 * p + 2]),
                              "ac", "act")
                self.ev[("a_sq", g, i)] = ev

            @self.blk.vector
            def _(v, i=i, p=p):
                self.wait(v, "vec", self.ev[("a_u", g, i)])
                rv = self.inc(v.tensor_reduce(out=scol[:, 8 * p:8 * p + 1], in_=u_t[p],
                                              axis=mybir.AxisListType.X, op=ALU.add),
                              "dv", "vec")
                v.wait_ge(self.S["dv"], rv[1])
                v.tensor_scalar(out=scol[:, 8 * p + 2:8 * p + 3],
                                in0=scol[:, 8 * p:8 * p + 1],
                                scalar1=1.0 / DM, scalar2=None, op0=ALU.mult)
                self.wait(v, "vec", self.ev[("a_sq", g, i)])
                v.tensor_scalar(out=scol[:, 8 * p + 1:8 * p + 2],
                                in0=scol[:, 8 * p + 1:8 * p + 2],
                                scalar1=1.0 / DM, scalar2=None, op0=ALU.mult)
                ev = self.inc(v.scalar_tensor_tensor(
                    out=scol[:, 8 * p + 3:8 * p + 4],
                    in0=scol[:, 8 * p + 2:8 * p + 3],
                    scalar=scol[:, 8 * p + 2:8 * p + 3],
                    in1=scol[:, 8 * p + 1:8 * p + 2],
                    op0=ALU.mult, op1=ALU.subtract), "dv", "vec")
                self.ev[("a_var", g, i)] = ev

            @self.blk.scalar
            def _(a, i=i, p=p):
                self.wait(a, "act", self.ev[("a_var", g, i)])
                a.activation(out=scol[:, 8 * p + 4:8 * p + 5],
                             in_=scol[:, 8 * p + 3:8 * p + 4],
                             func=AF.Ln, scale=-1.0, bias=EPS)
                ev = self.inc(a.activation(out=scol[:, 8 * p + 4:8 * p + 5],
                                           in_=scol[:, 8 * p + 4:8 * p + 5],
                                           func=AF.Exp, scale=-0.5), "ac", "act")
                self.ev[("a_stf", g, i)] = ev

            @self.blk.vector
            def _(v, i=i, p=p):
                self.wait(v, "vec", self.ev[("a_stf", g, i)],
                          self.ev.get(("a_xndma", g, i - 2)))
                ev = self.inc(v.tensor_scalar(out=xn_t[p], in0=u_t[p],
                                              scalar1=scol[:, 8 * p + 2:8 * p + 3],
                                              scalar2=scol[:, 8 * p + 4:8 * p + 5],
                                              op0=ALU.subtract, op1=ALU.mult),
                              "dv", "vec")
                self.ev[("a_xn", g, i)] = ev

            @self.blk.sync
            def _(sync, i=i, p=p):
                self.wait(sync, "sync", self.ev[("a_xn", g, i)])
                ev = self.inc(sync.dma_start(
                    out=d["xn_sc"][g, i * 128:(i + 1) * 128, :], in_=xn_t[p]),
                    "qs", "sync", 16)
                self.ev[("a_xndma", g, i)] = ev

        @self.blk.sync
        def _(sync):
            self.wait(sync, "sync", self.ev[("a_xndma", g, nt - 1)],
                      self.ev.get(("b_mm", g - 1, 66)))
            last = None
            for kc in range(16):
                last = self.inc(sync.dma_start(
                    out=XFA[:, kc, 0:tt],
                    in_=d["xn_sc"][g, 0:tt, kc * 128:(kc + 1) * 128],
                    transpose=True), "qs", "sync", 16)
            self.ev[("a_xfa", g)] = last

    def phaseB(self, g, tt, prepass=False, gateW=None, gatePE=None, gateDVE=None):
        d, sb = self.dram, self.sb
        XFA = self.reg(24832, 36864, BF16).rearrange("p (kc t) -> p kc t", kc=16)
        wbuf = [self.reg(61696, 11264, BF16), self.reg(61696 + 11264, 11264, BF16)]
        dtF = self.reg(84224, 4608, F32, parts=64)
        aF = self.reg(88832, 4608, F32, parts=64)
        acsF = self.reg(93440, 4608, F32, parts=64)
        BCF = self.reg(98048, 4608, BF16).rearrange("p (a t) -> p a t", a=2)
        zbuf = [self.reg(self.RX, 2304, BF16), self.reg(self.RX + 2304, 2304, BF16)]
        raw = [self.reg(self.RX + 4608, 2312, BF16), self.reg(self.RX + 6920, 2312, BF16)]
        acc = [self.reg(self.RX + 9232, 4608, F32), self.reg(self.RX + 13840, 4608, F32)]
        cout = [self.reg(self.RX + 18448, 2304, BF16),
                self.reg(self.RX + 18448 + 2304, 2304, BF16)]
        hist = self.reg(24576, 204, BF16).rearrange("p (a b) -> p a b", a=34)
        ftiles = _f_tiles(tt)
        cwv = sb["cw"].rearrange("p (a b) -> p a b", a=34)

        oclist = (list(range(32, 65)) + [66]) if prepass else list(range(NOC_IN))
        for oci, oc in enumerate(oclist):
            p = oc % 2
            mwid = 64 if oc == 66 else 128

            @self.blk.gpsimd
            def _(gp, oc=oc, p=p, mwid=mwid, oci=oci):
                gate = gateW if oci < 2 else None
                self.wait(gp, "gp", gate, self.ev.get(("b_mm", g, oc - 2)))
                src = d["win_main"][oc] if oc < 66 else d["win_tail"]
                dst = wbuf[p][:, 0:16 * mwid].rearrange("p (kc o) -> p kc o", kc=16)
                ev = self.inc(gp.dma_start(out=dst, in_=src), "qg", "gp", 16)
                self.ev[("b_w", g, oc)] = ev

            @self.blk.tensor
            def _(t, oc=oc, p=p, mwid=mwid, oci=oci):
                self.wait(t, "pe", self.ev[("b_w", g, oc)], self.ev[("a_xfa", g)],
                          gatePE if oci < 2 else None,
                          self.ev.get(("b_evac", g, oc - 2)))
                wv = wbuf[p][:, 0:16 * mwid].rearrange("p (kc o) -> p kc o", kc=16)
                last = None
                for fi, (fo, fw) in enumerate(ftiles):
                    pb = self.bank(3 * p + fi)[0:mwid, 0:fw]
                    for kc in range(16):
                        last = t.matmul(pb, lhsT=wv[:, kc, :], rhs=XFA[:, kc, fo:fo + fw],
                                        start=(kc == 0), stop=(kc == 15))
                ev = self.inc(last, "pe", "pe")
                self.ev[("b_mm", g, oc)] = ev

            if oc < 32:
                @self.blk.vector
                def _(v, oc=oc, p=p, oci=oci):
                    self.wait(v, "vec", self.ev[("b_mm", g, oc)],
                              gateDVE if oci < 2 else None,
                              self.ev.get(("b_zdma", g, oc - 2)))
                    last = None
                    for fi, (fo, fw) in enumerate(ftiles):
                        last = v.tensor_copy(out=zbuf[p][:, fo:fo + fw],
                                             in_=self.bank(3 * p + fi)[:, 0:fw])
                    ev = self.inc(last, "dv", "vec")
                    self.ev[("b_evac", g, oc)] = ev

                @self.blk.sync
                def _(sync, oc=oc, p=p):
                    self.wait(sync, "sync", self.ev[("b_evac", g, oc)])
                    ev = self.inc(sync.dma_start(
                        out=d["z_sc"][g, oc * 128:(oc + 1) * 128, 0:tt], in_=zbuf[p][:, 0:tt]),
                        "qs", "sync", 16)
                    self.ev[("b_zdma", g, oc)] = ev

            elif oc < 66:
                hp = oc - 32

                @self.blk.vector
                def _(v, oc=oc, p=p, hp=hp, oci=oci):
                    self.wait(v, "vec", self.ev[("b_mm", g, oc)],
                              gateDVE if oci < 2 else None)
                    v.tensor_copy(out=raw[p][:, 0:3], in_=hist[:, hp, :])
                    last = None
                    for fi, (fo, fw) in enumerate(ftiles):
                        last = v.tensor_copy(out=raw[p][:, 3 + fo:3 + fo + fw],
                                             in_=self.bank(3 * p + fi)[:, 0:fw])
                    bv = self.inc(last, "dv", "vec")
                    v.wait_ge(self.S["dv"], bv[1])
                    ev = self.inc(v.tensor_copy(out=hist[:, hp, :],
                                                in_=raw[p][:, tt:tt + 3]), "dv", "vec")
                    self.ev[("b_evac", g, oc)] = ev
                    self.wait(v, "vec", self.ev.get(("b_silu", g, oc - 2)))
                    v.tensor_scalar(out=acc[p][:, 0:tt], in0=raw[p][:, 0:tt],
                                    scalar1=cwv[:, hp, 0:1], scalar2=None, op0=ALU.mult)
                    last = None
                    for j in (1, 2, 3):
                        last = v.scalar_tensor_tensor(out=acc[p][:, 0:tt],
                                                      in0=raw[p][:, j:j + tt],
                                                      scalar=cwv[:, hp, j:j + 1],
                                                      in1=acc[p][:, 0:tt],
                                                      op0=ALU.mult, op1=ALU.add)
                    ev = self.inc(last, "dv", "vec")
                    self.ev[("b_conv", g, oc)] = ev

                @self.blk.scalar
                def _(a, oc=oc, p=p, hp=hp):
                    self.wait(a, "act", self.ev[("b_conv", g, oc)],
                              self.ev.get(("b_xdma", g, oc - 2)))
                    dst = cout[p][:, 0:tt] if hp < 32 else BCF[:, hp - 32, 0:tt]
                    ev = self.inc(a.activation(out=dst, in_=acc[p][:, 0:tt], func=AF.Silu,
                                               bias=sb["cb"][:, hp:hp + 1]), "ac", "act")
                    self.ev[("b_silu", g, oc)] = ev

                if hp < 32:
                    @self.blk.sync
                    def _(sync, oc=oc, p=p, hp=hp):
                        self.wait(sync, "sync", self.ev[("b_silu", g, oc)])
                        ev = self.inc(sync.dma_start(
                            out=d["xb_sc"][g, hp * 128:(hp + 1) * 128, 0:tt],
                            in_=cout[p][:, 0:tt]), "qs", "sync", 16)
                        self.ev[("b_xdma", g, oc)] = ev

            else:
                @self.blk.vector
                def _(v, p=p):
                    self.wait(v, "vec", self.ev[("b_mm", g, 66)])
                    for fi, (fo, fw) in enumerate(ftiles):
                        v.tensor_scalar(out=dtF[:, fo:fo + fw],
                                        in0=self.bank(3 * p + fi)[0:64, 0:fw],
                                        scalar1=sb["s_dtb"][:], scalar2=None, op0=ALU.add)
                    ev = self.inc(v.scalar_tensor_tensor(
                        out=aF[:, 0:tt], in0=dtF[:, 0:tt], scalar=-1.0,
                        in1=dtF[:, 0:tt], op0=ALU.mult, op1=ALU.max), "dv", "vec")
                    self.ev[("b_evac", g, 66)] = ev

                @self.blk.scalar
                def _(a):
                    self.wait(a, "act", self.ev[("b_evac", g, 66)])
                    a.activation(out=aF[:, 0:tt], in_=aF[:, 0:tt], func=AF.Exp, scale=-1.0)
                    ev = self.inc(a.activation(out=aF[:, 0:tt], in_=aF[:, 0:tt],
                                               func=AF.Ln, bias=1.0), "ac", "act")
                    self.ev[("b_sp1", g)] = ev

                @self.blk.vector
                def _(v):
                    self.wait(v, "vec", self.ev[("b_sp1", g)], self.ev["acol"])
                    v.tensor_scalar(out=dtF[:, 0:tt], in0=dtF[:, 0:tt],
                                    scalar1=0.0, scalar2=None, op0=ALU.max)
                    v.tensor_tensor(out=dtF[:, 0:tt], in0=dtF[:, 0:tt],
                                    in1=aF[:, 0:tt], op=ALU.add)
                    v.tensor_scalar(out=aF[:, 0:tt], in0=dtF[:, 0:tt],
                                    scalar1=sb["acol"][:], scalar2=None, op0=ALU.mult)
                    ev = self.inc(v.tensor_tensor_scan(
                        out=acsF[:, 0:tt], data0=sb["rmask"][:, 0:tt],
                        data1=aF[:, 0:tt], initial=0.0, op0=ALU.mult, op1=ALU.add),
                        "dv", "vec")
                    self.ev[("b_dt", g)] = ev

        if self.dev:
            @self.blk.sync
            def _(sync):
                self.wait(sync, "sync", self.ev[("b_dt", g)])
                self.inc(sync.dma_start(out=d["dv_dt"][g, :, 0:tt], in_=dtF[:, 0:tt]),
                         "qs", "sync", 16)
                self.inc(sync.dma_start(out=d["dv_acs"][g, :, 0:tt], in_=acsF[:, 0:tt]),
                         "qs", "sync", 16)

    def phaseC(self, g, tt, c0, prepass=False):
        d, sb = self.dram, self.sb
        S = self.reg(0, 16384, F32)
        Sbf = self.reg(16384, 8192, BF16)
        dtF = self.reg(84224, 4608, F32, parts=64)
        acsF = self.reg(93440, 4608, F32, parts=64)
        BCF = self.reg(98048, 4608, BF16).rearrange("p (a t) -> p a t", a=2)
        term1s = [self.reg(self.RX, 16384, F32),
                  self.reg(24832, 16384, F32)]
        MT = self.reg(self.RX + 16384, 16384, BF16)
        MT2 = self.reg(self.RX + 73728, 16384, BF16)
        MTv = MT.rearrange("p (h l) -> p h l", h=64)
        MT2v = MT2.rearrange("p (h l) -> p h l", h=64)
        xdr = [self.reg(self.RX + 32768, 8192, BF16),
               self.reg(61696, 8192, BF16)]
        xch = [self.reg(self.RX + 49152, 8192, BF16),
               self.reg(61696 + 8192, 8192, BF16)]
        xdd = self.reg(self.RX + 40960, 8192, BF16)
        ych = [self.reg(self.RX + 65536, 8192, BF16),
               self.reg(24832 + 24576, 8192, BF16)]
        zch = self.reg(self.RX + 57344, 8192, BF16)
        sqch = self.reg(24832 + 16384, 8192, BF16)
        smts = [self.reg(self.RX + 90112, 2048, F32),
                self.reg(61696 + 16384, 2048, F32)]
        CBts = [self.reg(self.RX + 92160, 256, BF16),
                self.reg(61696 + 18432, 256, BF16)]
        Btoks = [self.reg(self.RX + 92416, 256, BF16),
                 self.reg(61696 + 18688, 256, BF16)]
        rrepc = self.reg(self.RX + 92672, 512, F32)
        rstbf = self.reg(self.RX + 93184, 256, BF16)
        nwv = self.sb["nw"][:].unsqueeze(2)
        zoff = 128 if g == self.np_ else 0
        nch = tt // 128

        def smf(png):
            smt = smts[png]
            return dict(acs_tok=smt[:, 0:64], dt_tok=smt[:, 64:128],
                        asum=smt[:, 128:192], dd=smt[:, 192:256],
                        etile=smt[:, 256:320], ddt=smt[:, 320:384],
                        rstrow=smt[:, 384:512])

        def FEa(c):
            r = c - c0
            cs = c * 128
            png = r % 2
            sm = smf(png)
            CBt, Btok = CBts[png], Btoks[png]

            @self.blk.tensor
            def _(t, c=c, r=r, cs=cs):
                self.wait(t, "pe", self.ev[("b_dt", g)],
                          self.ev[("b_silu", g, 64 if prepass else 65)],
                          self.ev.get(("c_sm", g, r - 1)))
                p4 = self.bank(4)
                t.transpose(p4[0:128, 0:64], in_=acsF[:, cs:cs + 128],
                            identity=sb["idf"][0:64, 0:64])
                t.transpose(p4[0:128, 64:128], in_=dtF[:, cs:cs + 128],
                            identity=sb["idf"][0:64, 0:64])
                last = t.transpose(p4.bitcast(BF16)[:, 512:640],
                                   in_=BCF[:, 0, cs:cs + 128], identity=sb["idb"])
                if not prepass:
                    last = t.matmul(p4[:, 128:256], lhsT=BCF[:, 0, cs:cs + 128],
                                    rhs=BCF[:, 1, cs:cs + 128], start=True, stop=True)
                ev = self.inc(last, "pe", "pe")
                self.ev[("c_p4", g, r)] = ev

            @self.blk.vector
            def _(v, r=r, sm=sm, CBt=CBt, Btok=Btok):
                self.wait(v, "vec", self.ev[("c_p4", g, r)],
                          self.ev.get(("c_xd", g, r - 2)))
                p4 = self.bank(4)
                v.tensor_copy(out=sm["acs_tok"], in_=p4[:, 0:64])
                v.tensor_copy(out=sm["dt_tok"], in_=p4[:, 64:128])
                bv = self.inc(v.tensor_copy(out=Btok, in_=p4.bitcast(BF16)[:, 512:640]),
                              "dv", "vec")
                v.wait_ge(self.S["dv"], bv[1])
                if prepass:
                    self.ev[("c_sm", g, r)] = bv
                else:
                    ev = self.inc(v.tensor_tensor(out=CBt, in0=p4[:, 128:256],
                                                  in1=sb["trib"], op=ALU.mult), "dv", "vec")
                    self.ev[("c_sm", g, r)] = ev

            @self.blk.sync
            def _(sync, r=r, cs=cs):
                self.wait(sync, "sync", self.ev[("b_dt", g)])
                ev = self.inc(sync.dma_start(out=d["acs_sc"][g, r],
                                             in_=acsF[:, cs:cs + 128]), "qs", "sync", 16)
                self.ev[("c_acssc", g, r)] = ev

            @self.blk.sync
            def _(sync, r=r, cs=cs, png=png):
                war = self.ev.get(("c_xd", g, r - 1)) if prepass else \
                    self.ev.get(("c_ylast", g, r - 2))
                bgate = None if prepass else self.ev[("b_dt", g)]
                bgate2 = None if prepass else self.ev[("b_silu", g, 65)]
                self.wait(sync, "sync", self.ev[("b_xdma", g, 63)], war, bgate, bgate2)
                ev = self.inc(sync.dma_start(out=xdr[png],
                                             in_=d["xb_sc"][g, :, cs:cs + 128],
                                             transpose=True), "qs", "sync", 16)
                self.ev[("c_xdr", g, r)] = ev

            @self.blk.gpsimd
            def _(gp, r=r, cs=cs, png=png):
                if prepass:
                    return
                self.wait(gp, "gp", self.ev[("b_xdma", g, 63)],
                          self.ev.get(("c_yev", g, r - 2, 15)),
                          self.ev[("b_dt", g)], self.ev[("b_silu", g, 65)])
                src = d["xb_sc"][g].rearrange("(hp p) t -> p hp t", p=128)[:, :, cs:cs + 128]
                ev = self.inc(gp.dma_start(
                    out=xch[png].rearrange("p (hp t) -> p hp t", hp=32), in_=src),
                    "qg", "gp", 16)
                self.ev[("c_xch", g, r)] = ev

            for H in (0, 1):
                @self.blk.gpsimd
                def _(gp, r=r, H=H):
                    if prepass:
                        free = self.ev.get(("c_asum", g, r - 1, H))
                    else:
                        free = self.ev.get(("c_exp", g, r - 1, H))
                    self.wait(gp, "gp", self.ev[("c_acssc", g, r)], free)
                    srcp = d["acs_sc"][g, r, 32 * H:32 * (H + 1), :]
                    srcp = srcp.rearrange("h l -> (h l)").unsqueeze(0).broadcast_to([128, 4096])
                    ev = self.inc(gp.dma_start(out=term1s[H], in_=srcp), "qg", "gp", 16)
                    self.ev[("c_rep", g, r, H)] = ev

        def FEb(c):
            r = c - c0
            cs = c * 128
            png = r % 2
            sm = smf(png)
            CBt = CBts[png]

            if prepass:
                for H in (0, 1):
                    @self.blk.vector
                    def _(v, r=r, H=H, sm=sm):
                        self.wait(v, "vec", self.ev[("c_rep", g, r, H)],
                                  self.ev[("c_sm", g, r)])
                        t1v = term1s[H].rearrange("p (h l) -> p h l", h=32)
                        ev = self.inc(v.tensor_copy(
                            out=sm["asum"][:, 32 * H:32 * (H + 1)].unsqueeze(2),
                            in_=t1v[:, :, 127:128]), "dv", "vec")
                        self.ev[("c_asum", g, r, H)] = ev

                @self.blk.vector
                def _(v, r=r, sm=sm):
                    self.wait(v, "vec", self.ev[("c_asum", g, r, 1)])
                    ev = self.inc(v.tensor_tensor(out=sm["dd"], in0=sm["asum"],
                                                  in1=sm["acs_tok"],
                                                  op=ALU.subtract), "dv", "vec")
                    self.ev[("c_ddraw", g, r)] = ev

                @self.blk.scalar
                def _(a, r=r, sm=sm):
                    self.wait(a, "act", self.ev[("c_ddraw", g, r)])
                    a.activation(out=sm["dd"], in_=sm["dd"], func=AF.Exp)
                    ev = self.inc(a.activation(out=sm["etile"], in_=sm["asum"],
                                               func=AF.Exp), "ac", "act")
                    self.ev[("c_dde", g, r)] = ev

                @self.blk.vector
                def _(v, r=r, sm=sm, png=png):
                    self.wait(v, "vec", self.ev[("c_dde", g, r)],
                              self.ev[("c_xdr", g, r)],
                              self.ev.get(("c_stmm", g, r - 1, 1)))
                    bv = self.inc(v.tensor_tensor(out=sm["ddt"], in0=sm["dd"],
                                                  in1=sm["dt_tok"], op=ALU.mult),
                                  "dv", "vec")
                    v.wait_ge(self.S["dv"], bv[1])
                    xv = xdr[png].rearrange("p (h q) -> p h q", h=64)
                    ev = self.inc(v.tensor_tensor(
                        out=xdd.rearrange("p (h q) -> p h q", h=64), in0=xv,
                        in1=sm["ddt"].unsqueeze(2).broadcast_to([128, 64, 64]),
                        op=ALU.mult), "dv", "vec")
                    self.ev[("c_xd", g, r)] = ev
                return

            @self.blk.gpsimd
            def _(gp, r=r, sm=sm):
                self.wait(gp, "gp", self.ev[("c_rep", g, r, 0)],
                          self.ev[("c_rep", g, r, 1)], self.ev[("c_sm", g, r)],
                          self.ev.get(("c_dde", g, r - 2)))
                for H in (0, 1):
                    t1v = term1s[H].rearrange("p (h l) -> p h l", h=32)
                    gp.tensor_copy(out=sm["asum"][:, 32 * H:32 * (H + 1)].unsqueeze(2),
                                   in_=t1v[:, :, 127:128])
                ev = self.inc(gp.tensor_tensor(out=sm["dd"], in0=sm["asum"],
                                               in1=sm["acs_tok"], op=ALU.subtract),
                              "po", "gp")
                self.ev[("c_ddraw", g, r)] = ev

            @self.blk.scalar
            def _(a, r=r, sm=sm):
                self.wait(a, "act", self.ev[("c_ddraw", g, r)])
                a.activation(out=sm["dd"], in_=sm["dd"], func=AF.Exp)
                ev = self.inc(a.activation(out=sm["etile"], in_=sm["asum"],
                                           func=AF.Exp), "ac", "act")
                self.ev[("c_dde", g, r)] = ev

            @self.blk.gpsimd
            def _(gp, r=r, sm=sm, png=png):
                self.wait(gp, "gp", self.ev[("c_dde", g, r)], self.ev[("c_xdr", g, r)],
                          self.ev.get(("c_stmm", g, r - 1, 1)))
                bv = self.inc(gp.tensor_tensor(out=sm["ddt"], in0=sm["dd"],
                                               in1=sm["dt_tok"], op=ALU.mult),
                              "po", "gp")
                gp.wait_ge(self.S["po"], bv[1])
                xv = xdr[png].rearrange("p (h q) -> p h q", h=64)
                gp.tensor_tensor(out=xdd.rearrange("p (h q) -> p h q", h=64),
                                 in0=xv,
                                 in1=sm["ddt"].unsqueeze(2).broadcast_to([128, 64, 64]),
                                 op=ALU.mult)
                ev = self.inc(gp.tensor_tensor(
                    out=xv, in0=xv,
                    in1=sm["dt_tok"].unsqueeze(2).broadcast_to([128, 64, 64]),
                    op=ALU.mult), "po", "gp")
                self.ev[("c_xd", g, r)] = ev

            for H in (0, 1):
                @self.blk.scalar
                def _(a, r=r, H=H):
                    war = self.ev.get(("c_y", g, r - 1, 7)) if H == 0 else \
                        self.ev.get(("c_ylast", g, r - 1))
                    self.wait(a, "act", self.ev[("c_rep", g, r, H)], war,
                              self.ev.get(("c_devtap", g)) if (self.dev and r == 2) else None)
                    ev = self.inc(a.activation(out=MT2[:, 4096 * H:4096 * (H + 1)],
                                               in_=term1s[H], func=AF.Exp), "ac", "act")
                    self.ev[("c_e2", g, r, H)] = ev

            for H in (0, 1):
                @self.blk.gpsimd
                def _(gp, r=r, H=H, sm=sm):
                    self.wait(gp, "gp", self.ev[("c_e2", g, r, H)],
                              self.ev[("c_sm", g, r)])
                    t1v = term1s[H].rearrange("p (h l) -> p h l", h=32)
                    view = sm["acs_tok"][:, 32 * H:32 * (H + 1)].unsqueeze(2) \
                        .broadcast_to([128, 32, 128])
                    ev = self.inc(gp.tensor_tensor(out=t1v, in0=view, in1=t1v,
                                                   op=ALU.subtract), "po", "gp")
                    self.ev[("c_dneg", g, r, H)] = ev

            for H in (0, 1):
                @self.blk.scalar
                def _(a, r=r, H=H):
                    war = self.ev.get(("c_y", g, r - 1, 7)) if H == 0 else \
                        self.ev.get(("c_ylast", g, r - 1))
                    self.wait(a, "act", self.ev[("c_dneg", g, r, H)], war)
                    a.activation(out=term1s[H], in_=term1s[H], func=AF.Relu)
                    ev = self.inc(a.activation(out=MT[:, 4096 * H:4096 * (H + 1)],
                                               in_=term1s[H], func=AF.Exp, scale=-1.0),
                                  "ac", "act")
                    self.ev[("c_exp", g, r, H)] = ev

            for H in (0, 1):
                @self.blk.vector
                def _(v, r=r, H=H, cs=cs):
                    self.wait(v, "vec", self.ev[("c_e2", g, r, H)],
                              self.ev[("b_silu", g, 65)])
                    m2v = MT2v[:, 32 * H:32 * (H + 1), :]
                    cview = BCF[:, 1, cs:cs + 128].unsqueeze(1).broadcast_to([128, 32, 128])
                    ev = self.inc(v.tensor_tensor(out=m2v, in0=m2v, in1=cview,
                                                  op=ALU.mult), "dv", "vec")
                    self.ev[("c_mt2", g, r, H)] = ev

            for H in (0, 1):
                @self.blk.gpsimd
                def _(gp, r=r, H=H, CBt=CBt):
                    self.wait(gp, "gp", self.ev[("c_exp", g, r, H)])
                    mtv_ = MTv[:, 32 * H:32 * (H + 1), :]
                    cbv = CBt.unsqueeze(1).broadcast_to([128, 32, 128])
                    ev = self.inc(gp.tensor_tensor(out=mtv_, in0=mtv_, in1=cbv,
                                                   op=ALU.mult), "po", "gp")
                    self.ev[("c_mt", g, r, H)] = ev

        def BE1(c):
            r = c - c0
            png = r % 2
            sm = smf(png)
            Btok = Btoks[png]

            @self.blk.tensor
            def _(t, r=r, Btok=Btok):
                self.wait(t, "pe", self.ev[("c_xd", g, r)], self.ev[("c_sm", g, r)],
                          self.ev.get(("c_sadd", g, r - 1, 1)))
                last = None
                for i in range(4):
                    last = t.matmul(self.bank(i), lhsT=Btok,
                                    rhs=xdd[:, i * 512:(i + 1) * 512],
                                    start=True, stop=True)
                ev = self.inc(last, "pe", "pe")
                self.ev[("c_stmm", g, r, 0)] = ev

            @self.blk.vector
            def _(v, r=r, sm=sm):
                self.wait(v, "vec", self.ev[("c_dde", g, r)])
                v.tensor_tensor(out=S.rearrange("p (h q) -> p h q", h=64),
                                in0=S.rearrange("p (h q) -> p h q", h=64),
                                in1=sm["etile"].unsqueeze(2).broadcast_to([128, 64, 64]),
                                op=ALU.mult)
                self.wait(v, "vec", self.ev[("c_stmm", g, r, 0)])
                last = None
                for i in range(4):
                    last = v.tensor_tensor(out=S[:, i * 512:(i + 1) * 512],
                                           in0=S[:, i * 512:(i + 1) * 512],
                                           in1=self.bank(i), op=ALU.add)
                ev = self.inc(last, "dv", "vec")
                self.ev[("c_sadd", g, r, 0)] = ev

            @self.blk.tensor
            def _(t, r=r, Btok=Btok):
                self.wait(t, "pe", self.ev[("c_sadd", g, r, 0)])
                last = None
                for i in range(4):
                    last = t.matmul(self.bank(i), lhsT=Btok,
                                    rhs=xdd[:, 2048 + i * 512:2048 + (i + 1) * 512],
                                    start=True, stop=True)
                ev = self.inc(last, "pe", "pe")
                self.ev[("c_stmm", g, r, 1)] = ev

            @self.blk.vector
            def _(v, r=r):
                self.wait(v, "vec", self.ev[("c_stmm", g, r, 1)])
                last = None
                for i in range(4):
                    last = v.tensor_tensor(out=S[:, 2048 + i * 512:2048 + (i + 1) * 512],
                                           in0=S[:, 2048 + i * 512:2048 + (i + 1) * 512],
                                           in1=self.bank(i), op=ALU.add)
                ev = self.inc(last, "dv", "vec")
                self.ev[("c_sadd", g, r, 1)] = ev

        def BE2(c):
            r = c - c0
            png = r % 2
            sm = smf(png)
            for j in range(16):
                q = j % 2
                bk = 5 + q
                heads = [8 * (j // 2) + 2 * i + q for i in range(4)]

                @self.blk.tensor
                def _(t, r=r, j=j, q=q, bk=bk, heads=heads, png=png):
                    bank_war = self.ev.get(("c_yev", g, r - 1, 14 + q)) if j < 2 \
                        else self.ev[("c_yev", g, r, j - 2)]
                    self.wait(t, "pe",
                              self.ev[("c_mt", g, r, 0 if j < 8 else 1)],
                              self.ev[("c_mt2", g, r, 0 if j < 8 else 1)],
                              self.ev[("c_xd", g, r)],
                              self.ev.get(("c_sbf", g, r - 1)),
                              self.ev.get(("c_rcp", g, r - 1)) if bk == 6 else None,
                              bank_war)
                    pb = self.bank(bk)
                    first = True
                    last = None
                    for i, h in enumerate(heads):
                        outp = pb[0:64, i * 128:(i + 1) * 128]
                        last = t.matmul(outp, lhsT=xdr[png][:, h * 64:h * 64 + 64],
                                        rhs=MTv[:, h, :], start=first, stop=False)
                        first = False
                        last = t.matmul(outp, lhsT=Sbf[:, h * 64:h * 64 + 64],
                                        rhs=MT2v[:, h, :],
                                        start=False, stop=(i == 3))
                    ev = self.inc(last, "pe", "pe")
                    self.ev[("c_y", g, r, j)] = ev
                    if j == 15:
                        self.ev[("c_ylast", g, r)] = ev

                @self.blk.vector
                def _(v, r=r, j=j, q=q, bk=bk, heads=heads, png=png):
                    self.wait(v, "vec", self.ev[("c_y", g, r, j)],
                              self.ev[("c_xch", g, r)],
                              self.ev.get(("c_ydma", g, r - 2)))
                    pb = self.bank(bk)
                    b0 = q * 64
                    xv = xch[png].rearrange("p (hp t) -> p hp t", hp=32)
                    yv = ych[png].rearrange("p (hp t) -> p hp t", hp=32)
                    last = None
                    for i, h in enumerate(heads):
                        hp = h // 2
                        last = v.scalar_tensor_tensor(
                            out=yv[b0:b0 + 64, hp, :], in0=xv[b0:b0 + 64, hp, :],
                            scalar=sb["s_dpar"][b0:b0 + 64, hp:hp + 1],
                            in1=pb[0:64, i * 128:(i + 1) * 128],
                            op0=ALU.mult, op1=ALU.add)
                    ev = self.inc(last, "dv", "vec")
                    self.ev[("c_yev", g, r, j)] = ev

            @self.blk.scalar
            def _(a, r=r):
                self.wait(a, "act", self.ev[("c_ylast", g, r)],
                          self.ev[("c_sadd", g, r, 1)])
                ev = self.inc(a.activation(out=Sbf, in_=S, func=AF.Copy), "ac", "act")
                self.ev[("c_sbf", g, r)] = ev

            @self.blk.sync
            def _(sync, r=r):
                self.wait(sync, "sync", self.ev[("b_zdma", g, 31)],
                          self.ev.get(("c_yg", g, r - 1)))
                ev = self.inc(sync.dma_start(
                    out=zch.rearrange("p (hp t) -> p hp t", hp=32),
                    in_=d["z_sc"][g].rearrange("(hp p) t -> p hp t", p=128)[:, :, zoff + r * 128:zoff + (r + 1) * 128]),
                    "qs", "sync", 16)
                self.ev[("c_zld", g, r)] = ev

            @self.blk.scalar
            def _(a, r=r):
                self.wait(a, "act", self.ev[("c_zld", g, r)])
                ev = self.inc(a.activation(out=zch, in_=zch, func=AF.Silu), "ac", "act")
                self.ev[("c_zsilu", g, r)] = ev

            @self.blk.vector
            def _(v, r=r, png=png):
                self.wait(v, "vec", self.ev[("c_zsilu", g, r)],
                          self.ev[("c_yev", g, r, 15)])
                bv = self.inc(v.tensor_tensor(out=ych[png], in0=ych[png], in1=zch,
                                              op=ALU.mult), "dv", "vec")
                self.ev[("c_yg", g, r)] = bv
                v.wait_ge(self.S["dv"], bv[1])
                ev = self.inc(v.tensor_tensor(out=sqch, in0=ych[png], in1=ych[png],
                                              op=ALU.mult), "dv", "vec")
                self.ev[("c_sq", g, r)] = ev

            @self.blk.tensor
            def _(t, r=r):
                self.wait(t, "pe", self.ev[("c_sq", g, r)],
                          self.ev.get(("c_rstd", g, r - 1)))
                sv = sqch.rearrange("p (hp t) -> p hp t", hp=32)
                last = None
                for hp in range(32):
                    last = t.matmul(self.bank(7)[0:1, 0:128], lhsT=sb["onesb"][:],
                                    rhs=sv[:, hp, :], start=(hp == 0), stop=(hp == 31))
                ev = self.inc(last, "pe", "pe")
                self.ev[("c_rms", g, r)] = ev

            @self.blk.scalar
            def _(a, r=r, sm=sm):
                self.wait(a, "act", self.ev[("c_rms", g, r)])
                a.activation(out=sm["rstrow"][0:1, 0:128], in_=self.bank(7)[0:1, 0:128],
                             func=AF.Ln, scale=1.0 / DI, bias=EPS)
                a.activation(out=sm["rstrow"][0:1, 0:128], in_=sm["rstrow"][0:1, 0:128],
                             func=AF.Exp, scale=-0.5)
                ev = self.inc(a.activation(out=rstbf[0:1, 0:128],
                                           in_=sm["rstrow"][0:1, 0:128], func=AF.Copy),
                              "ac", "act")
                self.ev[("c_rstd", g, r)] = ev

            @self.blk.tensor
            def _(t, r=r):
                self.wait(t, "pe", self.ev[("c_rstd", g, r)],
                          self.ev.get(("c_rcp", g, r - 1)))
                ev = self.inc(t.matmul(self.bank(6)[:, 0:128],
                                       lhsT=sb["trib"][0:1, :],
                                       rhs=rstbf[0:1, 0:128],
                                       start=True, stop=True), "pe", "pe")
                self.ev[("c_rrep", g, r)] = ev

            @self.blk.vector
            def _(v, r=r, png=png):
                self.wait(v, "vec", self.ev[("c_rrep", g, r)])
                bv = self.inc(v.tensor_copy(out=rrepc, in_=self.bank(6)[:, 0:128]),
                              "dv", "vec")
                self.ev[("c_rcp", g, r)] = bv
                v.wait_ge(self.S["dv"], bv[1])
                y3 = ych[png].rearrange("p (hp t) -> p hp t", hp=32)
                v.tensor_tensor(out=y3, in0=y3, in1=nwv.broadcast_to([128, 32, 128]),
                                op=ALU.mult)
                ev = self.inc(v.tensor_tensor(
                    out=y3, in0=y3,
                    in1=rrepc.unsqueeze(1).broadcast_to([128, 32, 128]),
                    op=ALU.mult), "dv", "vec")
                self.ev[("c_yn", g, r)] = ev

            @self.blk.sync
            def _(sync, r=r, png=png):
                self.wait(sync, "sync", self.ev[("c_yn", g, r)])
                dst = d["y_sc"][g].rearrange("(hp p) t -> p hp t", p=128)[:, :, r * 128:(r + 1) * 128]
                ev = self.inc(sync.dma_start(
                    out=dst, in_=ych[png].rearrange("p (hp t) -> p hp t", hp=32)),
                    "qs", "sync", 16)
                self.ev[("c_ydma", g, r)] = ev

            if self.dev and r == 1:
                @self.blk.sync
                def _(sync, r=r, png=png):
                    self.wait(sync, "sync", self.ev[("c_ylast", g, r)],
                              self.ev[("c_yev", g, r, 15)])
                    self.inc(sync.dma_start(out=d["dv_mt"][g], in_=MT), "qs", "sync", 16)
                    self.inc(sync.dma_start(out=d["dv_mt2"][g], in_=MT2), "qs", "sync", 16)
                    self.inc(sync.dma_start(out=d["dv_xch"][g], in_=xch[png]), "qs", "sync", 16)
                    ev = self.inc(sync.dma_start(out=d["dv_cbt"][g], in_=CBts[png]), "qs", "sync", 16)
                    self.ev[("c_devtap", g)] = ev

        if prepass:
            for c in range(c0, nch):
                FEa(c)
                FEb(c)
                BE1(c)
            self.ev[("slot_done", g)] = self.ev[("c_sadd", g, nch - c0 - 1, 1)]
            return

        FEa(c0)
        FEb(c0)
        for c in range(c0, nch):
            BE1(c)
            if c + 1 < nch:
                FEa(c + 1)
            BE2(c)
            if c + 1 < nch:
                FEb(c + 1)

        self.ev[("c_done", g)] = self.ev[("c_ydma", g, nch - c0 - 1)]
        self.ev[("c_yevlast", g)] = self.ev[("c_yev", g, nch - c0 - 1, 15)]
        if self.dev:
            @self.blk.sync
            def _(sync):
                self.wait(sync, "sync", self.ev[("c_sbf", g, nch - c0 - 1)])
                self.inc(sync.dma_start(out=d["dv_S"][g], in_=S), "qs", "sync", 16)

    def phaseD(self, g):
        d, sb = self.dram, self.sb
        YN = self.reg(self.RX, 65536, BF16)
        ysl = self.reg(self.RX + 65536, 8192, BF16)
        zsl = self.reg(self.RX + 73728, 8192, BF16)
        qsl = self.reg(self.RX + 81920, 8192, BF16)
        rrep = self.reg(self.RX + 90112, 4096, F32)
        rsum = self.reg(self.RX + 94208, 4096, F32)
        zoff = 128 if g == self.np_ else 0

        for s8 in range(8):
            @self.blk.sync
            def _(sync, s8=s8):
                self.wait(sync, "sync", self.ev[("c_done", g)],
                          self.ev.get(("d_mv", g, s8 - 1)))
                sync.dma_start(
                    out=ysl.rearrange("p (hp t) -> p hp t", hp=4),
                    in_=d["y_sc"][g].rearrange("(hp p) t -> p hp t", p=128)[:, 4 * s8:4 * s8 + 4, :]
                ).then_inc(self.S["qs"], 16)
                self.c["qs"] += 16
                self.wait(sync, "sync", self.ev.get(("d_yg", g, s8 - 1)))
                ev = self.inc(sync.dma_start(
                    out=zsl.rearrange("p (hp t) -> p hp t", hp=4),
                    in_=d["z_sc"][g].rearrange("(hp p) t -> p hp t", p=128)[:, 4 * s8:4 * s8 + 4, zoff:zoff + 1024]),
                    "qs", "sync", 16)
                self.ev[("d_ld", g, s8)] = ev

            @self.blk.scalar
            def _(a, s8=s8):
                self.wait(a, "act", self.ev[("d_ld", g, s8)])
                ev = self.inc(a.activation(out=zsl, in_=zsl, func=AF.Silu), "ac", "act")
                self.ev[("d_silu", g, s8)] = ev

            @self.blk.vector
            def _(v, s8=s8):
                self.wait(v, "vec", self.ev[("d_silu", g, s8)])
                ev = self.inc(v.tensor_tensor(out=ysl, in0=ysl, in1=zsl, op=ALU.mult),
                              "dv", "vec")
                self.ev[("d_yg", g, s8)] = ev

            @self.blk.scalar
            def _(a, s8=s8):
                self.wait(a, "act", self.ev[("d_yg", g, s8)],
                          self.ev.get(("d_rmsmm", g, s8 - 1)))
                ev = self.inc(a.activation(out=qsl, in_=ysl, func=AF.Square), "ac", "act")
                self.ev[("d_sq", g, s8)] = ev

            @self.blk.vector
            def _(v, s8=s8):
                self.wait(v, "vec", self.ev[("d_sq", g, s8)])
                ev = self.inc(v.tensor_copy(out=YN[:, s8 * 4096:(s8 + 1) * 4096],
                                            in_=ysl), "dv", "vec")
                self.ev[("d_mv", g, s8)] = ev

            @self.blk.tensor
            def _(t, s8=s8):
                self.wait(t, "pe", self.ev[("d_sq", g, s8)],
                          self.ev.get(("d_racc", g, s8 - 1)))
                qv = qsl.rearrange("p (hp t) -> p hp t", hp=4)
                last = None
                for fi in range(2):
                    for hp in range(4):
                        last = t.matmul(self.bank(7 - fi)[0:1, 0:512],
                                        lhsT=sb["onesb"][:],
                                        rhs=qv[:, hp, fi * 512:(fi + 1) * 512],
                                        start=(hp == 0), stop=(hp == 3))
                ev = self.inc(last, "pe", "pe")
                self.ev[("d_rmsmm", g, s8)] = ev

            @self.blk.vector
            def _(v, s8=s8):
                self.wait(v, "vec", self.ev[("d_rmsmm", g, s8)])
                last = None
                for fi in range(2):
                    if s8 == 0:
                        last = v.tensor_copy(out=rsum[0:1, fi * 512:(fi + 1) * 512],
                                             in_=self.bank(7 - fi)[0:1, 0:512])
                    else:
                        last = v.tensor_tensor(out=rsum[0:1, fi * 512:(fi + 1) * 512],
                                               in0=rsum[0:1, fi * 512:(fi + 1) * 512],
                                               in1=self.bank(7 - fi)[0:1, 0:512],
                                               op=ALU.add)
                ev = self.inc(last, "dv", "vec")
                self.ev[("d_racc", g, s8)] = ev

        @self.blk.scalar
        def _(a):
            self.wait(a, "act", self.ev[("d_racc", g, 7)])
            rr = rrep[0:1, 0:1024]
            a.activation(out=rr, in_=rsum[0:1, 0:1024], func=AF.Ln,
                         scale=1.0 / DI, bias=EPS)
            ev = self.inc(a.activation(out=rr, in_=rr, func=AF.Exp, scale=-0.5),
                          "ac", "act")
            self.ev[("d_rsb", g)] = ev

        @self.blk.sync
        def _(sync):
            self.wait(sync, "sync", self.ev[("d_rsb", g)])
            ev = self.inc(sync.dma_start(out=d["rstd_sc"][g], in_=rrep[0:1, :]),
                          "qs", "sync", 16)
            self.ev[("d_rdma", g)] = ev

        @self.blk.gpsimd
        def _(gp):
            self.wait(gp, "gp", self.ev[("d_rdma", g)])
            src = d["rstd_sc"][g, 0].unsqueeze(0).broadcast_to([128, 1024])
            ev = self.inc(gp.dma_start(out=rrep, in_=src), "qg", "gp", 16)
            self.ev[("d_rrep", g)] = ev

        @self.blk.vector
        def _(v):
            self.wait(v, "vec", self.ev[("d_rrep", g)], self.ev[("d_mv", g, 7)])
            ynv = YN.rearrange("p (hp t) -> p hp t", hp=32)
            last = None
            for hp in range(32):
                last = v.scalar_tensor_tensor(out=ynv[:, hp, :], in0=ynv[:, hp, :],
                                              scalar=sb["nw"][:, hp:hp + 1], in1=rrep,
                                              op0=ALU.mult, op1=ALU.mult)
            ev = self.inc(last, "dv", "vec")
            self.ev[("d_yn", g)] = ev

    def phaseE(self, g, gmain, tb, c0):
        d, sb = self.dram, self.sb
        YN = self.reg(self.RX, 65536, BF16).rearrange("p (kc t) -> p kc t", kc=32)
        wbuf = [self.reg(61696, 8192, BF16), self.reg(61696 + 8192, 8192, BF16)]
        at_t = [self.reg(self.RX + 65536, 2048, BF16),
                self.reg(self.RX + 67584, 2048, BF16)]

        @self.blk.sync
        def _(sync):
            self.wait(sync, "sync", self.ev[("c_done", g)])
            ev = self.inc(sync.dma_start(
                out=YN, in_=d["y_sc"][g].rearrange("(kc p) t -> p kc t", p=128)),
                "qs", "sync", 16)
            self.ev[("e_yn", g)] = ev

        for oc in range(NOC_OUT):
            p = oc % 2

            @self.blk.gpsimd
            def _(gp, oc=oc, p=p):
                gate = self.ev[("c_yevlast", g)] if oc < 2 else None
                self.wait(gp, "gp", gate, self.ev.get(("e_mm", g, oc - 2)))
                dst = wbuf[p].rearrange("p (kc o) -> p kc o", kc=KC_OUT)
                ev = self.inc(gp.dma_start(out=dst, in_=d["wout"][oc]), "qg", "gp", 16)
                self.ev[("e_w", g, oc)] = ev

            @self.blk.tensor
            def _(t, oc=oc, p=p):
                self.wait(t, "pe", self.ev[("e_w", g, oc)], self.ev[("e_yn", g)],
                          self.ev.get(("e_evac", g, oc - 2)))
                wv = wbuf[p].rearrange("p (kc o) -> p kc o", kc=KC_OUT)
                last = None
                for fi in range(2):
                    pb = self.bank(2 * p + fi)
                    for kc in range(KC_OUT):
                        last = t.matmul(pb, lhsT=wv[:, kc, :],
                                        rhs=YN[:, kc, fi * 512:(fi + 1) * 512],
                                        start=(kc == 0), stop=(kc == KC_OUT - 1))
                ev = self.inc(last, "pe", "pe")
                self.ev[("e_mm", g, oc)] = ev

            @self.blk.vector
            def _(v, oc=oc, p=p):
                self.wait(v, "vec", self.ev[("e_mm", g, oc)],
                          self.ev.get(("e_atdma", g, oc - 2)))
                last = None
                for fi in range(2):
                    last = v.tensor_copy(out=at_t[p][:, fi * 512:(fi + 1) * 512],
                                         in_=self.bank(2 * p + fi))
                ev = self.inc(last, "dv", "vec")
                self.ev[("e_evac", g, oc)] = ev

            @self.blk.sync
            def _(sync, oc=oc, p=p):
                self.wait(sync, "sync", self.ev[("e_evac", g, oc)])
                ev = self.inc(sync.dma_start(
                    out=d["attn_sc"][g, oc * 128:(oc + 1) * 128, :], in_=at_t[p]),
                    "qs", "sync", 16)
                self.ev[("e_atdma", g, oc)] = ev

        at_tok = [self.reg(self.RX, 4096, BF16), self.reg(self.RX + 4096, 4096, BF16),
                  self.reg(61696, 4096, BF16)]
        u2 = [self.reg(self.RX + 8192, 8192, F32), self.reg(self.RX + 16384, 8192, F32),
              self.reg(61696 + 4096, 8192, F32)]
        x1 = [self.reg(self.RX + 24576, 8192, F32), self.reg(self.RX + 32768, 8192, F32),
              self.reg(61696 + 12288, 8192, F32)]
        xn2 = [self.reg(self.RX + 40960, 4096, BF16), self.reg(self.RX + 45056, 4096, BF16),
               self.reg(self.RX + 57344, 4096, BF16)]
        sq2 = self.reg(self.RX + 49152, 8192, F32)
        scol = sb["scol"]
        XFB = self.reg(165888, 32768, BF16)

        for i in range(8):
            p = i % 3
            urow = (tb + c0 + i) * 128

            @self.blk.sync
            def _(sync, i=i, p=p, urow=urow):
                self.wait(sync, "sync", self.ev[("e_atdma", g, NOC_OUT - 1)],
                          self.ev.get(("e_ln2", g, i - 3)))
                sync.dma_start(out=at_tok[p],
                               in_=d["attn_sc"][g, :, i * 128:(i + 1) * 128],
                               transpose=True).then_inc(self.S["qs"], 16)
                self.c["qs"] += 16
                self.wait(sync, "sync", self.ev.get(("e_sq", g, i - 3)))
                ev = self.inc(sync.dma_start(out=u2[p], in_=d["u"][urow:urow + 128, :]),
                              "qs", "sync", 16)
                self.ev[("e_ld", g, i)] = ev

            @self.blk.vector
            def _(v, i=i, p=p):
                self.wait(v, "vec", self.ev[("e_ld", g, i)],
                          self.ev.get(("e_x1dma", g, i - 3)))
                ev = self.inc(v.tensor_tensor(out=x1[p], in0=u2[p], in1=at_tok[p],
                                              op=ALU.add), "dv", "vec")
                self.ev[("e_x1", g, i)] = ev

            @self.blk.sync
            def _(sync, i=i, p=p):
                self.wait(sync, "sync", self.ev[("e_x1", g, i)])
                ev = self.inc(sync.dma_start(out=d["x1_sc"][g, i * 128:(i + 1) * 128, :],
                                             in_=x1[p]), "qs", "sync", 16)
                self.ev[("e_x1dma", g, i)] = ev

            @self.blk.scalar
            def _(a, i=i, p=p):
                self.wait(a, "act", self.ev[("e_x1", g, i)])
                ev = self.inc(a.activation(out=sq2, in_=x1[p], func=AF.Square,
                                           accum_out=scol[:, 8 * p + 1:8 * p + 2]),
                              "ac", "act")
                self.ev[("e_sq", g, i)] = ev

            @self.blk.vector
            def _(v, i=i, p=p):
                self.wait(v, "vec", self.ev[("e_sq", g, i)])
                rv = self.inc(v.tensor_reduce(out=scol[:, 8 * p:8 * p + 1], in_=x1[p],
                                              axis=mybir.AxisListType.X, op=ALU.add),
                              "dv", "vec")
                v.wait_ge(self.S["dv"], rv[1])
                v.tensor_scalar(out=scol[:, 8 * p + 2:8 * p + 3],
                                in0=scol[:, 8 * p:8 * p + 1],
                                scalar1=1.0 / DM, scalar2=None, op0=ALU.mult)
                v.tensor_scalar(out=scol[:, 8 * p + 1:8 * p + 2],
                                in0=scol[:, 8 * p + 1:8 * p + 2],
                                scalar1=1.0 / DM, scalar2=None, op0=ALU.mult)
                ev = self.inc(v.scalar_tensor_tensor(
                    out=scol[:, 8 * p + 3:8 * p + 4],
                    in0=scol[:, 8 * p + 2:8 * p + 3],
                    scalar=scol[:, 8 * p + 2:8 * p + 3],
                    in1=scol[:, 8 * p + 1:8 * p + 2],
                    op0=ALU.mult, op1=ALU.subtract), "dv", "vec")
                self.ev[("e_var", g, i)] = ev

            @self.blk.scalar
            def _(a, i=i, p=p):
                self.wait(a, "act", self.ev[("e_var", g, i)])
                a.activation(out=scol[:, 8 * p + 4:8 * p + 5],
                             in_=scol[:, 8 * p + 3:8 * p + 4],
                             func=AF.Ln, scale=-1.0, bias=EPS)
                ev = self.inc(a.activation(out=scol[:, 8 * p + 4:8 * p + 5],
                                           in_=scol[:, 8 * p + 4:8 * p + 5],
                                           func=AF.Exp, scale=-0.5), "ac", "act")
                self.ev[("e_stf", g, i)] = ev

            @self.blk.vector
            def _(v, i=i, p=p):
                self.wait(v, "vec", self.ev[("e_stf", g, i)],
                          self.ev.get(("e_xn2dma", g, i - 3)))
                ev = self.inc(v.tensor_scalar(out=xn2[p], in0=x1[p],
                                              scalar1=scol[:, 8 * p + 2:8 * p + 3],
                                              scalar2=scol[:, 8 * p + 4:8 * p + 5],
                                              op0=ALU.subtract, op1=ALU.mult),
                              "dv", "vec")
                self.ev[("e_ln2", g, i)] = ev

            @self.blk.sync
            def _(sync, i=i, p=p):
                self.wait(sync, "sync", self.ev[("e_ln2", g, i)])
                ev = self.inc(sync.dma_start(out=d["xn2_sc"][g, i * 128:(i + 1) * 128, :],
                                             in_=xn2[p]), "qs", "sync", 16)
                self.ev[("e_xn2dma", g, i)] = ev

        for hh in (0, 1):
            @self.blk.scalar
            def _(a, hh=hh):
                self.wait(a, "act", self.ev[("e_xn2dma", g, 4 * hh + 3)])
                XFBv = XFB.rearrange("p (kc t) -> p kc t", kc=16)
                last = None
                for kc in range(16):
                    last = self.inc(a.dma_start(
                        out=XFBv[:, kc, hh * 512:(hh + 1) * 512],
                        in_=d["xn2_sc"][g, hh * 512:(hh + 1) * 512,
                                        kc * 128:(kc + 1) * 128], transpose=True),
                        "qc", "act", 16)
                if hh == 1:
                    self.ev[("e_xfb", g)] = last

    def phaseF(self, g, gmain):
        d, sb = self.dram, self.sb
        XFB = self.reg(165888, 32768, BF16).rearrange("p (kc t) -> p kc t", kc=16)
        wbuf = [self.reg(61696 + 4096 * j, 4096, BF16) for j in range(4)]
        wtouch = self.reg(61696 + 16384, 16, BF16)
        gate_t = [self.reg(self.RX, 2048, BF16), self.reg(self.RX + 2048, 2048, BF16)]
        h_t = [self.reg(self.RX + 4096, 2048, BF16), self.reg(self.RX + 6144, 2048, BF16)]

        for oc in range(NOC_G):
            p = oc % 2

            @self.blk.gpsimd
            def _(gp, oc=oc, p=p):
                gate = self.ev[("e_xfb", g)] if oc < 2 else None
                self.wait(gp, "gp", gate, self.ev.get(("f_gmm", g, oc - 2)))
                ev = self.inc(gp.dma_start(
                    out=wbuf[2 * p].rearrange("p (kc o) -> p kc o", kc=16),
                    in_=d["wg"][oc]), "qg", "gp", 16)
                self.ev[("f_wgld", g, oc)] = ev
                self.wait(gp, "gp", self.ev.get(("f_umm", g, oc - 2)))
                ev2 = self.inc(gp.dma_start(
                    out=wbuf[2 * p + 1].rearrange("p (kc o) -> p kc o", kc=16),
                    in_=d["wu"][oc]), "qg", "gp", 16)
                self.ev[("f_wuld", g, oc)] = ev2

            @self.blk.vector
            def _(v, oc=oc, p=p):
                self.wait(v, "vec", self.ev[("f_wgld", g, oc)])
                v.tensor_scalar(out=wbuf[2 * p], in0=wbuf[2 * p],
                                scalar1=-128.0, scalar2=None, op0=ALU.add)
                ev = self.inc(v.tensor_copy(out=wtouch[:, 0:8],
                                            in_=wbuf[2 * p][:, 2040:2048]), "dv", "vec")
                self.ev[("f_wgs", g, oc)] = ev
                self.wait(v, "vec", self.ev[("f_wuld", g, oc)])
                v.tensor_scalar(out=wbuf[2 * p + 1], in0=wbuf[2 * p + 1],
                                scalar1=-128.0, scalar2=None, op0=ALU.add)
                ev2 = self.inc(v.tensor_copy(out=wtouch[:, 0:8],
                                             in_=wbuf[2 * p + 1][:, 2040:2048]), "dv", "vec")
                self.ev[("f_wus", g, oc)] = ev2

            @self.blk.tensor
            def _(t, oc=oc, p=p):
                self.wait(t, "pe", self.ev[("f_wgs", g, oc)], self.ev[("e_xfb", g)],
                          self.ev.get(("f_gev", g, oc - 1)))
                wv = wbuf[2 * p].rearrange("p (kc o) -> p kc o", kc=16)
                last = None
                for fi in range(2):
                    for kc in range(16):
                        last = t.matmul(self.bank(fi), lhsT=wv[:, kc, :],
                                        rhs=XFB[:, kc, fi * 512:(fi + 1) * 512],
                                        start=(kc == 0), stop=(kc == 15))
                ev = self.inc(last, "pe", "pe")
                self.ev[("f_gmm", g, oc)] = ev

            @self.blk.scalar
            def _(a, oc=oc, p=p):
                self.wait(a, "act", self.ev[("f_gmm", g, oc)],
                          self.ev.get(("f_h", g, oc - 2)))
                last = None
                for fi in range(2):
                    last = a.activation(out=gate_t[p][:, fi * 512:(fi + 1) * 512],
                                        in_=self.bank(fi), func=AF.Silu,
                                        scale=sb["s_sg"][:, oc:oc + 1])
                ev = self.inc(last, "ac", "act")
                self.ev[("f_gev", g, oc)] = ev

            @self.blk.tensor
            def _(t, oc=oc, p=p):
                self.wait(t, "pe", self.ev[("f_wus", g, oc)],
                          self.ev.get(("f_h", g, oc - 1)))
                wv = wbuf[2 * p + 1].rearrange("p (kc o) -> p kc o", kc=16)
                last = None
                for fi in range(2):
                    for kc in range(16):
                        last = t.matmul(self.bank(2 + fi), lhsT=wv[:, kc, :],
                                        rhs=XFB[:, kc, fi * 512:(fi + 1) * 512],
                                        start=(kc == 0), stop=(kc == 15))
                ev = self.inc(last, "pe", "pe")
                self.ev[("f_umm", g, oc)] = ev

            @self.blk.vector
            def _(v, oc=oc, p=p):
                self.wait(v, "vec", self.ev[("f_umm", g, oc)], self.ev[("f_gev", g, oc)],
                          self.ev.get(("f_hdma", g, oc - 2)))
                last = None
                for fi in range(2):
                    last = v.scalar_tensor_tensor(
                        out=h_t[p][:, fi * 512:(fi + 1) * 512],
                        in0=self.bank(2 + fi), scalar=sb["s_su"][:, oc:oc + 1],
                        in1=gate_t[p][:, fi * 512:(fi + 1) * 512],
                        op0=ALU.mult, op1=ALU.mult)
                ev = self.inc(last, "dv", "vec")
                self.ev[("f_h", g, oc)] = ev

            @self.blk.sync
            def _(sync, oc=oc, p=p):
                self.wait(sync, "sync", self.ev[("f_h", g, oc)])
                ev = self.inc(sync.dma_start(out=d["h_sc"][g, oc * 128:(oc + 1) * 128, :],
                                             in_=h_t[p]), "qs", "sync", 16)
                self.ev[("f_hdma", g, oc)] = ev

        self.ev[("f_gu", g)] = self.ev[("f_hdma", g, NOC_G - 1)]

        Hb = [self.reg(self.RX + 8192, 45056, BF16)]
        wdb = [self.reg(self.RX + 53248, 11264, BF16),
               self.reg(self.RX + 64512, 11264, BF16)]
        fb = [self.reg(self.RX + 75776, 1024, BF16),
              self.reg(self.RX + 76800, 1024, BF16)]

        for fh in range(2):
            hp_ = 0

            @self.blk.sync
            def _(sync, fh=fh, hp_=hp_):
                self.wait(sync, "sync", self.ev[("f_gu", g)],
                          self.ev.get(("f_dmm", g, fh - 1, NOC_D - 1)))
                e = self.inc(sync.dma_start(
                    out=Hb[hp_].rearrange("p (kc t) -> p kc t", kc=NOC_G),
                    in_=d["h_sc"][g].rearrange("(kc p) t -> p kc t", p=128)[:, :, fh * 512:(fh + 1) * 512]),
                    "qs", "sync", 16)
                self.ev[("f_hload", g, fh)] = e

            for oc in range(NOC_D):
                p = oc % 2

                @self.blk.gpsimd
                def _(gp, fh=fh, oc=oc, p=p):
                    self.wait(gp, "gp", self.ev.get(("f_dmm", g, fh, oc - 2)),
                              self.ev.get(("f_dmm", g, fh - 1, 14 + oc)) if (oc < 2 and fh > 0) else None,
                              self.ev[("f_gu", g)] if (oc < 2 and fh == 0) else None)
                    e = self.inc(gp.dma_start(
                        out=wdb[p].rearrange("p (kc o) -> p kc o", kc=NOC_G),
                        in_=d["wd"][oc]), "qg", "gp", 16)
                    self.ev[("f_wdld", g, fh, oc)] = e

                @self.blk.vector
                def _(v, fh=fh, oc=oc, p=p):
                    self.wait(v, "vec", self.ev[("f_wdld", g, fh, oc)])
                    e = self.inc(v.tensor_scalar(out=wdb[p], in0=wdb[p], scalar1=-128.0,
                                                 scalar2=None, op0=ALU.add), "dv", "vec")
                    self.ev[("f_wds", g, fh, oc)] = e

                @self.blk.tensor
                def _(t, fh=fh, oc=oc, p=p, hp_=hp_):
                    self.wait(t, "pe", self.ev[("f_wds", g, fh, oc)],
                              self.ev[("f_hload", g, fh)],
                              self.ev[("f_gev", g, NOC_G - 1)] if (fh == 0 and oc == 0) else None,
                              self.ev.get(("f_fev", g, fh, oc - 2)))
                    wv = wdb[p].rearrange("p (kc o) -> p kc o", kc=NOC_G)
                    Hv = Hb[hp_].rearrange("p (kc t) -> p kc t", kc=NOC_G)
                    last = None
                    for kc in range(NOC_G):
                        last = t.matmul(self.bank(p), lhsT=wv[:, kc, :], rhs=Hv[:, kc, :],
                                        start=(kc == 0), stop=(kc == NOC_G - 1))
                    e = self.inc(last, "pe", "pe")
                    self.ev[("f_dmm", g, fh, oc)] = e

                @self.blk.vector
                def _(v, fh=fh, oc=oc, p=p):
                    self.wait(v, "vec", self.ev[("f_dmm", g, fh, oc)],
                              self.ev.get(("f_fdma", g, fh, oc - 2)))
                    e = self.inc(v.tensor_scalar(out=fb[p], in0=self.bank(p),
                                                 scalar1=sb["s_sd"][:, oc:oc + 1],
                                                 scalar2=None, op0=ALU.mult), "dv", "vec")
                    self.ev[("f_fev", g, fh, oc)] = e

                @self.blk.sync
                def _(sync, fh=fh, oc=oc, p=p):
                    self.wait(sync, "sync", self.ev[("f_fev", g, fh, oc)])
                    e = self.inc(sync.dma_start(
                        out=d["ffn_sc"][g, oc * 128:(oc + 1) * 128,
                                        fh * 512:(fh + 1) * 512], in_=fb[p]),
                        "qs", "sync", 16)
                    self.ev[("f_fdma", g, fh, oc)] = e

        f_tok = [self.reg(self.RX + 8192, 4096, BF16),
                 self.reg(self.RX + 12288, 4096, BF16)]
        x1b = [self.reg(self.RX + 16384, 8192, F32), self.reg(self.RX + 24576, 8192, F32)]
        ot = [self.reg(self.RX + 32768, 8192, F32), self.reg(self.RX + 40960, 8192, F32)]
        f_tokE = [self.reg(61696, 4096, BF16), self.reg(61696 + 4096, 4096, BF16)]
        x1bE = [self.reg(61696 + 8192, 8192, F32), self.reg(61696 + 16384, 8192, F32)]
        otE = [self.reg(86272, 8192, F32), self.reg(86272 + 8192, 8192, F32)]

        for i in range(8):
            p = i % 2
            orow = gmain * 1024 + i * 128
            ft = f_tokE if i < 4 else f_tok
            xb = x1bE if i < 4 else x1b
            oo = otE if i < 4 else ot
            ghalf = i // 4

            @self.blk.sync
            def _(sync, i=i, p=p, ft=ft, xb=xb, ghalf=ghalf):
                self.wait(sync, "sync", self.ev[("f_fdma", g, ghalf, NOC_D - 1)],
                          self.ev.get(("f_o", g, i - 2)))
                sync.dma_start(out=ft[p], in_=d["ffn_sc"][g, :, i * 128:(i + 1) * 128],
                               transpose=True).then_inc(self.S["qs"], 16)
                self.c["qs"] += 16
                ev = self.inc(sync.dma_start(out=xb[p],
                                             in_=d["x1_sc"][g, i * 128:(i + 1) * 128, :]),
                              "qs", "sync", 16)
                self.ev[("f_old", g, i)] = ev

            @self.blk.vector
            def _(v, i=i, p=p, ft=ft, xb=xb, oo=oo):
                self.wait(v, "vec", self.ev[("f_old", g, i)],
                          self.ev.get(("f_out", g, i - 2)))
                ev = self.inc(v.tensor_tensor(out=oo[p], in0=xb[p], in1=ft[p],
                                              op=ALU.add), "dv", "vec")
                self.ev[("f_o", g, i)] = ev

            @self.blk.sync
            def _(sync, i=i, p=p, orow=orow, oo=oo):
                self.wait(sync, "sync", self.ev[("f_o", g, i)])
                ev = self.inc(sync.dma_start(out=d["out"][orow:orow + 128, :], in_=oo[p]),
                              "qs", "sync", 16)
                self.ev[("f_out", g, i)] = ev

        self.ev[("slot_done", g)] = self.ev[("f_out", g, 7)]

    def epilogue(self):
        @self.blk.sync
        def _(sync):
            self.wait(sync, "sync", self.ev[("slot_done", self.nslots - 1)])


def host_prep(inputs):
    f32 = np.float32
    W_in = inputs["L0_attn_W_in"]
    A = np.ascontiguousarray(W_in.T).reshape(KC_IN, 128, DIP)
    win_main = np.ascontiguousarray(
        A[:, :, :66 * 128].reshape(KC_IN, 128, 66, 128).transpose(2, 1, 0, 3))
    win_tail = np.ascontiguousarray(A[:, :, 66 * 128:].transpose(1, 0, 2))
    W_out = inputs["L0_attn_W_out"]
    B = np.ascontiguousarray(W_out.T).reshape(KC_OUT, 128, DM)
    wout = np.ascontiguousarray(
        B.reshape(KC_OUT, 128, NOC_OUT, 128).transpose(2, 1, 0, 3))

    def blk_gate(wq):
        T = np.ascontiguousarray(wq.T).reshape(KC_IN, 128, HID)
        return np.ascontiguousarray(
            T.reshape(KC_IN, 128, NOC_G, 128).transpose(2, 1, 0, 3))

    wg = blk_gate(inputs["L0_ffn_wq_gate"])
    wu = blk_gate(inputs["L0_ffn_wq_up"])
    Td = np.ascontiguousarray(inputs["L0_ffn_wq_down"].T)
    wd = np.ascontiguousarray(
        Td.reshape(NOC_G, 128, NOC_D, 128).transpose(2, 1, 0, 3))

    convw = np.ascontiguousarray(
        inputs["L0_attn_conv_w"][:, 0, :].reshape(34, 128, 4).transpose(1, 0, 2))
    convb = np.ascontiguousarray(inputs["L0_attn_conv_b"].reshape(34, 128).T)
    shared = dict(
        win_main=win_main.astype(f32), win_tail=win_tail.astype(f32),
        wout=wout.astype(f32), wg=wg.astype(np.int32), wu=wu.astype(np.int32),
        wd=wd.astype(np.int32),
        convw=convw.reshape(128, 34, 4).astype(f32), convb=convb.astype(f32),
        dtb=inputs["L0_attn_dt_bias"].reshape(64, 1).astype(f32),
        alog=inputs["L0_attn_A_log"].reshape(64, 1).astype(f32),
        dpar=np.ascontiguousarray(
            np.repeat(inputs["L0_attn_D_param"], 64).reshape(32, 128).T).astype(f32),
        normw=np.ascontiguousarray(
            inputs["L0_attn_norm_w"].reshape(32, 128).T).astype(f32),
        sg=np.ascontiguousarray(inputs["L0_ffn_s_gate"].reshape(NOC_G, 128).T).astype(f32),
        su=np.ascontiguousarray(inputs["L0_ffn_s_up"].reshape(NOC_G, 128).T).astype(f32),
        sd=np.ascontiguousarray(inputs["L0_ffn_s_down"].reshape(NOC_D, 128).T).astype(f32),
        ident_bf=np.eye(128, dtype=f32).astype(ml_dtypes.bfloat16),
        ident_f=np.eye(128, dtype=f32),
        tri_bf=np.triu(np.ones((128, 128), f32)).astype(ml_dtypes.bfloat16),
        reset=np.ones((64, TTMAX), f32),
        ones_bf=np.ones((128, 1), f32).astype(ml_dtypes.bfloat16),
        cvals=np.concatenate([np.full((128, 1), EPS, f32), np.ones((128, 1), f32)], 1),
    )
    shared["reset"][:, 0::128] = 0.0
    return shared


def kernel(**inputs):
    u_full = np.asarray(inputs["L0_attn_u"])
    shared = host_prep(inputs)
    nc = bass.Bass("TRN2", target_bir_lowering=False)
    mk = Mk(nc, [9, 8], prepass_tiles=(8, 8))
    nc = mk.build()
    in_maps = []
    for c in range(8):
        b, h = c // 2, c % 2
        pad = np.zeros((128, DM), np.float32)
        if h == 1:
            pad[125:128] = u_full[b][2045:2048]
        m = dict(shared)
        m["u"] = np.ascontiguousarray(
            np.vstack([pad, u_full[b][h * 2048:(h + 1) * 2048]]))
        m["u2"] = np.ascontiguousarray(u_full[b][0:2048])
        m["smask"] = np.full((128, 1), float(h), np.float32)
        in_maps.append(m)
    res = run_bass_kernel_spmd(nc, in_maps, core_ids=list(range(8)))
    out = np.stack(
        [np.vstack([np.asarray(res.results[2 * b]["out"]),
                    np.asarray(res.results[2 * b + 1]["out"])]) for b in range(4)],
        axis=0)
    return out.astype(np.float32)


# version 15
# speedup vs baseline: 1.1741x; 1.0053x over previous
import numpy as np
import ml_dtypes
import concourse.bass as bass
import concourse.mybir as mybir
from concourse.bass_utils import run_bass_kernel_spmd

F32 = mybir.dt.float32
BF16 = mybir.dt.bfloat16
I32 = mybir.dt.int32
U8 = mybir.dt.uint8
AF = mybir.ActivationFunctionType
ALU = mybir.AluOpType

DM = 2048
DI = 4096
NH = 64
HD = 64
DIP = 8512
CH = 128
KC_IN = 16
NOC_IN = 67
KC_OUT = 32
NOC_OUT = 16
HID = 5632
NOC_G = 44
NOC_D = 16
EPS = 1e-5
BPP = 202752
TTMAX = 1152


def _f_tiles(tt):
    out, o = [], 0
    while o < tt:
        w = min(512, tt - o)
        out.append((o, w))
        o += w
    return out


class Mk:
    RX = 102656

    def __init__(self, nc, group_tiles, dev_taps=(), prepass_tiles=()):
        self.nc = nc
        self.gt = group_tiles
        self.ng = len(group_tiles)
        self.pt = list(prepass_tiles)
        self.np_ = len(self.pt)
        self.nslots = self.np_ + self.ng
        self.nt = sum(group_tiles)
        self.ntok_real = (self.nt - 1) * 128
        self.dev = set(dev_taps)
        self.c = {}
        self.emitter = {}
        self.waited = {}
        self.dram = {}
        self.ev = {}

    def inc(self, inst, sem, eng, n=1):
        inst.then_inc(self.S[sem], n)
        self.c[sem] += n
        ev = (sem, self.c[sem])
        self.emitter[ev] = eng
        return ev

    def wait(self, eng, engname, *evs):
        for ev in evs:
            if ev is None:
                continue
            sem, val = ev
            if sem in ("pe", "dv", "ac", "po") and self.emitter.get(ev) == engname:
                continue
            w = self.waited.setdefault(engname, {})
            if w.get(sem, -1) >= val:
                continue
            eng.wait_ge(self.S[sem], val)
            w[sem] = val

    def din(self, name, shape, dt):
        self.dram[name] = self.nc.dram_tensor(name, shape, dt, kind="ExternalInput").ap()

    def dout(self, name, shape, dt):
        self.dram[name] = self.nc.dram_tensor(name, shape, dt, kind="ExternalOutput").ap()

    def dscr(self, name, shape, dt):
        kind = "ExternalOutput" if name in self.dev else "Internal"
        self.dram[name] = self.nc.dram_tensor(name, shape, dt, kind=kind).ap()

    def reg(self, off, nbytes, dt, parts=128):
        assert off % 4 == 0 and off + nbytes <= BPP, (off, nbytes)
        v = self.pool[0:parts, off:off + nbytes]
        if dt is not U8:
            v = v.bitcast(dt)
        return v

    def bank(self, i, n=1):
        return self.ps[:, i * 512:(i + n) * 512]

    def build(self):
        nc = self.nc
        self.din("u", [self.nt * 128, DM], F32)
        self.din("win_main", [66, 128, KC_IN, 128], F32)
        self.din("win_tail", [128, KC_IN, 64], F32)
        self.din("wout", [NOC_OUT, 128, KC_OUT, 128], F32)
        self.din("wg", [NOC_G, 128, KC_IN, 128], I32)
        self.din("wu", [NOC_G, 128, KC_IN, 128], I32)
        self.din("wd", [NOC_D, 128, NOC_G, 128], I32)
        self.din("convw", [128, 34, 4], F32)
        self.din("convb", [128, 34], F32)
        self.din("dtb", [64, 1], F32)
        self.din("alog", [64, 1], F32)
        self.din("dpar", [128, 32], F32)
        self.din("normw", [128, 32], F32)
        self.din("sg", [128, NOC_G], F32)
        self.din("su", [128, NOC_G], F32)
        self.din("sd", [128, NOC_D], F32)
        self.din("ident_bf", [128, 128], BF16)
        self.din("ident_f", [128, 128], F32)
        self.din("tri_bf", [128, 128], BF16)
        self.din("reset", [64, TTMAX], F32)
        self.din("ones_bf", [128, 1], BF16)
        self.din("cvals", [128, 2], F32)
        if self.np_:
            self.din("u2", [sum(self.pt) * 128, DM], F32)
            self.din("smask", [128, 1], F32)

        self.dout("out", [self.ntok_real, DM], F32)

        ns = self.nslots
        self.dscr("xn_sc", [ns, TTMAX, DM], BF16)
        self.dscr("z_sc", [ns, DI, TTMAX], BF16)
        self.dscr("xb_sc", [ns, DI, TTMAX], BF16)
        self.dscr("y_sc", [ns, DI, 1024], BF16)
        self.dscr("acs_sc", [ns, 8, 64, CH], F32)
        self.dscr("rstd_sc", [ns, 1, 1024], F32)
        self.dscr("attn_sc", [ns, DM, 1024], BF16)
        self.dscr("x1_sc", [ns, 1024, DM], F32)
        self.dscr("xn2_sc", [ns, 1024, DM], BF16)
        self.dscr("h_sc", [ns, HID, 1024], BF16)
        self.dscr("ffn_sc", [ns, DM, 1024], BF16)
        if self.dev:
            self.dout("dv_dt", [ns, 64, TTMAX], F32)
            self.dout("dv_acs", [ns, 64, TTMAX], F32)
            self.dout("dv_S", [ns, 128, DI], F32)
            self.dout("dv_Sp", [128, DI], F32)
            self.dout("dv_mt", [ns, 128, 8192], BF16)
            self.dout("dv_mt2", [ns, 128, 8192], BF16)
            self.dout("dv_xch", [ns, 128, 4096], BF16)
            self.dout("dv_cbt", [ns, 128, 128], BF16)

        import contextlib
        with contextlib.ExitStack() as st:
            pool_t = st.enter_context(nc.sbuf_tensor("pool", [128, BPP], U8))
            self.pool = pool_t[:]
            names = [("cw", [128, 34 * 4], F32), ("cb", [128, 34], F32),
                     ("s_dtb", [64, 1], F32), ("s_alog", [64, 1], F32),
                     ("acol", [64, 1], F32), ("s_dpar", [128, 32], F32),
                     ("nw", [128, 32], F32), ("s_sg", [128, NOC_G], F32),
                     ("s_su", [128, NOC_G], F32), ("s_sd", [128, NOC_D], F32),
                     ("idb", [128, 128], BF16), ("idf", [128, 128], F32),
                     ("trib", [128, 128], BF16), ("rmask", [64, TTMAX], F32),
                     ("onesb", [128, 1], BF16), ("scol", [128, 24], F32),
                     ("cval", [128, 2], F32), ("smk", [128, 1], F32)]
            self.sb = {}
            for nm, shape, dt in names:
                self.sb[nm] = st.enter_context(nc.sbuf_tensor(nm, shape, dt))[:]
            nc.const_aps.register(F32, EPS, self.sb["cval"][:, 0:1])
            nc.const_aps.register(F32, 1.0, self.sb["cval"][:, 1:2])
            self.ps = st.enter_context(nc.psum_tensor("ps", [128, 4096], F32))[:]
            self.S = {}
            for s in ("qs", "qg", "qc", "pe", "dv", "ac", "po"):
                self.S[s] = st.enter_context(nc.semaphore(s))
                self.c[s] = 0
            self.blk = st.enter_context(nc.Block())
            self.emit_all()
        return nc

    def emit_all(self):
        self.prologue()
        tb = 0
        for s in range(self.np_):
            nt = self.pt[s]
            gA = gW = gPE = gDVE = self.ev.get(("slot_done", s - 1))
            self.phaseA(s, tb, nt, nt * 128, uname="u2", gateA=gA)
            self.phaseB(s, nt * 128, prepass=True, gateW=gW, gatePE=gPE, gateDVE=gDVE)
            self.phaseC(s, nt * 128, 0, prepass=True)
            tb += nt
        if self.np_:
            self.mask_block()
        tb = 0
        for g in range(self.ng):
            s = self.np_ + g
            nt = self.gt[g]
            tt = nt * 128
            c0 = 1 if g == 0 else 0
            STRICT = True
            if g == 0:
                gA = self.ev.get(("slot_mask",)) if STRICT else None
                gW = self.ev.get(("slot_mask",)) if STRICT else self.ev.get(("b_mm", s - 1, 66))
                gPE = self.ev.get(("slot_mask",)) if STRICT else self.ev.get(("c_sadd", s - 1, 7, 1))
                gDVE = self.ev.get(("slot_mask",)) if STRICT else self.ev.get(("c_asum", s - 1, 7, 1))
            elif STRICT:
                gA = gW = gPE = gDVE = self.ev.get(("slot_done", s - 1))
            else:
                gA = self.ev.get(("f_umm", s - 1, NOC_G - 1))
                gW = self.ev.get(("f_umm", s - 1, NOC_G - 1))
                gPE = self.ev.get(("f_fev", s - 1, 1, NOC_D - 1))
                gDVE = self.ev.get(("f_out", s - 1, 7))
            self.phaseA(s, tb, nt, tt, uname="u", gateA=gA)
            self.phaseB(s, tt, gateW=gW, gatePE=gPE, gateDVE=gDVE)
            self.phaseC(s, tt, c0)
            self.phaseE(s, g, tb, c0)
            self.phaseF(s, g)
            tb += nt
        self.epilogue()

    def mask_block(self):
        S = self.reg(0, 16384, F32)
        Sbf = self.reg(16384, 8192, BF16)
        hist = self.reg(24576, 204, BF16)
        last_p = self.np_ - 1

        @self.blk.sync
        def _(sync):
            self.wait(sync, "sync", self.ev["consts"])
            ev = self.inc(sync.dma_start(out=self.sb["smk"], in_=self.dram["smask"]),
                          "qs", "sync", 16)
            self.ev["smk"] = ev

        if self.dev:
            @self.blk.sync
            def _(sync):
                self.wait(sync, "sync", self.ev[("c_sadd", last_p, 7, 1)])
                ev = self.inc(sync.dma_start(out=self.dram["dv_Sp"], in_=S),
                              "qs", "sync", 16)
                self.ev["sptap"] = ev

        @self.blk.vector
        def _(v):
            self.wait(v, "vec", self.ev["smk"],
                      self.ev[("c_sadd", last_p, 7, 1)],
                      self.ev.get("sptap"))
            v.tensor_scalar(out=S, in0=S, scalar1=self.sb["smk"][:],
                            scalar2=None, op0=ALU.mult)
            v.memset(hist, 0.0)
            ev = self.inc(v.tensor_copy(out=Sbf, in_=S), "dv", "vec")
            self.ev[("c_sbf", self.np_, -1)] = ev
            self.ev[("slot_mask",)] = ev

    def prologue(self):
        d, sb = self.dram, self.sb

        @self.blk.sync
        def _(sync):
            last = None
            for nm, dst in [("convw", sb["cw"]), ("convb", sb["cb"]),
                            ("dtb", sb["s_dtb"]), ("alog", sb["s_alog"]),
                            ("dpar", sb["s_dpar"]), ("normw", sb["nw"]),
                            ("sg", sb["s_sg"]), ("su", sb["s_su"]), ("sd", sb["s_sd"]),
                            ("ident_bf", sb["idb"]), ("ident_f", sb["idf"]),
                            ("tri_bf", sb["trib"]), ("reset", sb["rmask"]),
                            ("ones_bf", sb["onesb"]), ("cvals", sb["cval"])]:
                last = self.inc(sync.dma_start(out=dst, in_=d[nm]), "qs", "sync", 16)
            self.ev["consts"] = last

        S = self.reg(0, 16384, F32)
        Sbf = self.reg(16384, 8192, BF16)
        hist = self.reg(24576, 204, BF16)

        @self.blk.scalar
        def _(a):
            self.wait(a, "act", self.ev["consts"])
            ev = self.inc(a.activation(out=sb["acol"], in_=sb["s_alog"], func=AF.Exp),
                          "ac", "act")
            self.ev["aexp"] = ev

        @self.blk.vector
        def _(v):
            self.wait(v, "vec", self.ev["consts"], self.ev["aexp"])
            v.memset(S, 0.0)
            v.memset(Sbf, 0.0)
            v.memset(hist, 0.0)
            ev = self.inc(v.tensor_scalar(out=sb["acol"], in0=sb["acol"],
                                          scalar1=-1.0, scalar2=None, op0=ALU.mult),
                          "dv", "vec")
            self.ev["init"] = ev
            self.ev["acol"] = ev

    def phaseA(self, g, tb, nt, tt, uname="u", gateA=None):
        d, sb = self.dram, self.sb
        XFA = self.reg(24832, 36864, BF16).rearrange("p (kc t) -> p kc t", kc=16)
        u_t = [self.reg(self.RX + 20000, 8192, F32),
               self.reg(self.RX + 28192, 8192, F32)]
        xn_t = [self.reg(self.RX + 36384, 4096, BF16),
                self.reg(self.RX + 40480, 4096, BF16)]
        sq = self.reg(self.RX + 44576, 8192, F32)
        scol = sb["scol"]

        for i in range(nt):
            p = i % 2
            row0 = (tb + i) * 128

            @self.blk.sync
            def _(sync, i=i, p=p, row0=row0):
                gate = gateA if i < 2 else None
                gate2 = self.ev.get(("a_xfa", g - 1)) if i < 2 else None
                self.wait(sync, "sync", gate, gate2, self.ev.get(("a_xn", g, i - 2)))
                ev = self.inc(sync.dma_start(out=u_t[p], in_=d[uname][row0:row0 + 128, :]),
                              "qs", "sync", 16)
                self.ev[("a_u", g, i)] = ev

            @self.blk.scalar
            def _(a, i=i, p=p):
                self.wait(a, "act", self.ev[("a_u", g, i)])
                ev = self.inc(a.activation(out=sq, in_=u_t[p], func=AF.Square,
                                           accum_out=scol[:, 8 * p + 1:8 * p + 2]),
                              "ac", "act")
                self.ev[("a_sq", g, i)] = ev

            @self.blk.vector
            def _(v, i=i, p=p):
                self.wait(v, "vec", self.ev[("a_u", g, i)])
                rv = self.inc(v.tensor_reduce(out=scol[:, 8 * p:8 * p + 1], in_=u_t[p],
                                              axis=mybir.AxisListType.X, op=ALU.add),
                              "dv", "vec")
                v.wait_ge(self.S["dv"], rv[1])
                v.tensor_scalar(out=scol[:, 8 * p + 2:8 * p + 3],
                                in0=scol[:, 8 * p:8 * p + 1],
                                scalar1=1.0 / DM, scalar2=None, op0=ALU.mult)
                self.wait(v, "vec", self.ev[("a_sq", g, i)])
                v.tensor_scalar(out=scol[:, 8 * p + 1:8 * p + 2],
                                in0=scol[:, 8 * p + 1:8 * p + 2],
                                scalar1=1.0 / DM, scalar2=None, op0=ALU.mult)
                ev = self.inc(v.scalar_tensor_tensor(
                    out=scol[:, 8 * p + 3:8 * p + 4],
                    in0=scol[:, 8 * p + 2:8 * p + 3],
                    scalar=scol[:, 8 * p + 2:8 * p + 3],
                    in1=scol[:, 8 * p + 1:8 * p + 2],
                    op0=ALU.mult, op1=ALU.subtract), "dv", "vec")
                self.ev[("a_var", g, i)] = ev

            @self.blk.scalar
            def _(a, i=i, p=p):
                self.wait(a, "act", self.ev[("a_var", g, i)])
                a.activation(out=scol[:, 8 * p + 4:8 * p + 5],
                             in_=scol[:, 8 * p + 3:8 * p + 4],
                             func=AF.Ln, scale=-1.0, bias=EPS)
                ev = self.inc(a.activation(out=scol[:, 8 * p + 4:8 * p + 5],
                                           in_=scol[:, 8 * p + 4:8 * p + 5],
                                           func=AF.Exp, scale=-0.5), "ac", "act")
                self.ev[("a_stf", g, i)] = ev

            @self.blk.vector
            def _(v, i=i, p=p):
                self.wait(v, "vec", self.ev[("a_stf", g, i)],
                          self.ev.get(("a_xndma", g, i - 2)))
                ev = self.inc(v.tensor_scalar(out=xn_t[p], in0=u_t[p],
                                              scalar1=scol[:, 8 * p + 2:8 * p + 3],
                                              scalar2=scol[:, 8 * p + 4:8 * p + 5],
                                              op0=ALU.subtract, op1=ALU.mult),
                              "dv", "vec")
                self.ev[("a_xn", g, i)] = ev

            @self.blk.sync
            def _(sync, i=i, p=p):
                self.wait(sync, "sync", self.ev[("a_xn", g, i)])
                ev = self.inc(sync.dma_start(
                    out=d["xn_sc"][g, i * 128:(i + 1) * 128, :], in_=xn_t[p]),
                    "qs", "sync", 16)
                self.ev[("a_xndma", g, i)] = ev

        @self.blk.sync
        def _(sync):
            self.wait(sync, "sync", self.ev[("a_xndma", g, nt - 1)],
                      self.ev.get(("b_mm", g - 1, 66)))
            last = None
            for kc in range(16):
                last = self.inc(sync.dma_start(
                    out=XFA[:, kc, 0:tt],
                    in_=d["xn_sc"][g, 0:tt, kc * 128:(kc + 1) * 128],
                    transpose=True), "qs", "sync", 16)
            self.ev[("a_xfa", g)] = last

    def phaseB(self, g, tt, prepass=False, gateW=None, gatePE=None, gateDVE=None):
        d, sb = self.dram, self.sb
        XFA = self.reg(24832, 36864, BF16).rearrange("p (kc t) -> p kc t", kc=16)
        wbuf = [self.reg(61696, 11264, BF16), self.reg(61696 + 11264, 11264, BF16)]
        dtF = self.reg(84224, 4608, F32, parts=64)
        aF = self.reg(88832, 4608, F32, parts=64)
        acsF = self.reg(93440, 4608, F32, parts=64)
        BCF = self.reg(98048, 4608, BF16).rearrange("p (a t) -> p a t", a=2)
        zbuf = [self.reg(self.RX, 2304, BF16), self.reg(self.RX + 2304, 2304, BF16)]
        raw = [self.reg(self.RX + 4608, 2312, BF16), self.reg(self.RX + 6920, 2312, BF16)]
        acc = [self.reg(self.RX + 9232, 4608, F32), self.reg(self.RX + 13840, 4608, F32)]
        cout = [self.reg(self.RX + 18448, 2304, BF16),
                self.reg(self.RX + 18448 + 2304, 2304, BF16)]
        hist = self.reg(24576, 204, BF16).rearrange("p (a b) -> p a b", a=34)
        ftiles = _f_tiles(tt)
        cwv = sb["cw"].rearrange("p (a b) -> p a b", a=34)

        oclist = (list(range(32, 65)) + [66]) if prepass else list(range(NOC_IN))
        for oci, oc in enumerate(oclist):
            p = oc % 2
            mwid = 64 if oc == 66 else 128

            @self.blk.gpsimd
            def _(gp, oc=oc, p=p, mwid=mwid, oci=oci):
                gate = gateW if oci < 2 else None
                self.wait(gp, "gp", gate, self.ev.get(("b_mm", g, oc - 2)))
                src = d["win_main"][oc] if oc < 66 else d["win_tail"]
                dst = wbuf[p][:, 0:16 * mwid].rearrange("p (kc o) -> p kc o", kc=16)
                ev = self.inc(gp.dma_start(out=dst, in_=src), "qg", "gp", 16)
                self.ev[("b_w", g, oc)] = ev

            @self.blk.tensor
            def _(t, oc=oc, p=p, mwid=mwid, oci=oci):
                self.wait(t, "pe", self.ev[("b_w", g, oc)], self.ev[("a_xfa", g)],
                          gatePE if oci < 2 else None,
                          self.ev.get(("b_evac", g, oc - 2)))
                wv = wbuf[p][:, 0:16 * mwid].rearrange("p (kc o) -> p kc o", kc=16)
                last = None
                for fi, (fo, fw) in enumerate(ftiles):
                    pb = self.bank(3 * p + fi)[0:mwid, 0:fw]
                    for kc in range(16):
                        last = t.matmul(pb, lhsT=wv[:, kc, :], rhs=XFA[:, kc, fo:fo + fw],
                                        start=(kc == 0), stop=(kc == 15))
                ev = self.inc(last, "pe", "pe")
                self.ev[("b_mm", g, oc)] = ev

            if oc < 32:
                @self.blk.vector
                def _(v, oc=oc, p=p, oci=oci):
                    self.wait(v, "vec", self.ev[("b_mm", g, oc)],
                              gateDVE if oci < 2 else None,
                              self.ev.get(("b_zdma", g, oc - 2)))
                    last = None
                    for fi, (fo, fw) in enumerate(ftiles):
                        last = v.tensor_copy(out=zbuf[p][:, fo:fo + fw],
                                             in_=self.bank(3 * p + fi)[:, 0:fw])
                    ev = self.inc(last, "dv", "vec")
                    self.ev[("b_evac", g, oc)] = ev

                @self.blk.sync
                def _(sync, oc=oc, p=p):
                    self.wait(sync, "sync", self.ev[("b_evac", g, oc)])
                    ev = self.inc(sync.dma_start(
                        out=d["z_sc"][g, oc * 128:(oc + 1) * 128, 0:tt], in_=zbuf[p][:, 0:tt]),
                        "qs", "sync", 16)
                    self.ev[("b_zdma", g, oc)] = ev

            elif oc < 66:
                hp = oc - 32

                @self.blk.vector
                def _(v, oc=oc, p=p, hp=hp, oci=oci):
                    self.wait(v, "vec", self.ev[("b_mm", g, oc)],
                              gateDVE if oci < 2 else None)
                    v.tensor_copy(out=raw[p][:, 0:3], in_=hist[:, hp, :])
                    last = None
                    for fi, (fo, fw) in enumerate(ftiles):
                        last = v.tensor_copy(out=raw[p][:, 3 + fo:3 + fo + fw],
                                             in_=self.bank(3 * p + fi)[:, 0:fw])
                    bv = self.inc(last, "dv", "vec")
                    v.wait_ge(self.S["dv"], bv[1])
                    ev = self.inc(v.tensor_copy(out=hist[:, hp, :],
                                                in_=raw[p][:, tt:tt + 3]), "dv", "vec")
                    self.ev[("b_evac", g, oc)] = ev
                    self.wait(v, "vec", self.ev.get(("b_silu", g, oc - 2)))
                    v.tensor_scalar(out=acc[p][:, 0:tt], in0=raw[p][:, 0:tt],
                                    scalar1=cwv[:, hp, 0:1], scalar2=None, op0=ALU.mult)
                    last = None
                    for j in (1, 2, 3):
                        last = v.scalar_tensor_tensor(out=acc[p][:, 0:tt],
                                                      in0=raw[p][:, j:j + tt],
                                                      scalar=cwv[:, hp, j:j + 1],
                                                      in1=acc[p][:, 0:tt],
                                                      op0=ALU.mult, op1=ALU.add)
                    ev = self.inc(last, "dv", "vec")
                    self.ev[("b_conv", g, oc)] = ev

                @self.blk.scalar
                def _(a, oc=oc, p=p, hp=hp):
                    self.wait(a, "act", self.ev[("b_conv", g, oc)],
                              self.ev.get(("b_xdma", g, oc - 2)))
                    dst = cout[p][:, 0:tt] if hp < 32 else BCF[:, hp - 32, 0:tt]
                    ev = self.inc(a.activation(out=dst, in_=acc[p][:, 0:tt], func=AF.Silu,
                                               bias=sb["cb"][:, hp:hp + 1]), "ac", "act")
                    self.ev[("b_silu", g, oc)] = ev

                if hp < 32:
                    @self.blk.sync
                    def _(sync, oc=oc, p=p, hp=hp):
                        self.wait(sync, "sync", self.ev[("b_silu", g, oc)])
                        ev = self.inc(sync.dma_start(
                            out=d["xb_sc"][g, hp * 128:(hp + 1) * 128, 0:tt],
                            in_=cout[p][:, 0:tt]), "qs", "sync", 16)
                        self.ev[("b_xdma", g, oc)] = ev

            else:
                @self.blk.vector
                def _(v, p=p):
                    self.wait(v, "vec", self.ev[("b_mm", g, 66)])
                    for fi, (fo, fw) in enumerate(ftiles):
                        v.tensor_scalar(out=dtF[:, fo:fo + fw],
                                        in0=self.bank(3 * p + fi)[0:64, 0:fw],
                                        scalar1=sb["s_dtb"][:], scalar2=None, op0=ALU.add)
                    ev = self.inc(v.scalar_tensor_tensor(
                        out=aF[:, 0:tt], in0=dtF[:, 0:tt], scalar=-1.0,
                        in1=dtF[:, 0:tt], op0=ALU.mult, op1=ALU.max), "dv", "vec")
                    self.ev[("b_evac", g, 66)] = ev

                @self.blk.scalar
                def _(a):
                    self.wait(a, "act", self.ev[("b_evac", g, 66)])
                    a.activation(out=aF[:, 0:tt], in_=aF[:, 0:tt], func=AF.Exp, scale=-1.0)
                    ev = self.inc(a.activation(out=aF[:, 0:tt], in_=aF[:, 0:tt],
                                               func=AF.Ln, bias=1.0), "ac", "act")
                    self.ev[("b_sp1", g)] = ev

                @self.blk.vector
                def _(v):
                    self.wait(v, "vec", self.ev[("b_sp1", g)], self.ev["acol"])
                    v.tensor_scalar(out=dtF[:, 0:tt], in0=dtF[:, 0:tt],
                                    scalar1=0.0, scalar2=None, op0=ALU.max)
                    v.tensor_tensor(out=dtF[:, 0:tt], in0=dtF[:, 0:tt],
                                    in1=aF[:, 0:tt], op=ALU.add)
                    v.tensor_scalar(out=aF[:, 0:tt], in0=dtF[:, 0:tt],
                                    scalar1=sb["acol"][:], scalar2=None, op0=ALU.mult)
                    ev = self.inc(v.tensor_tensor_scan(
                        out=acsF[:, 0:tt], data0=sb["rmask"][:, 0:tt],
                        data1=aF[:, 0:tt], initial=0.0, op0=ALU.mult, op1=ALU.add),
                        "dv", "vec")
                    self.ev[("b_dt", g)] = ev

        if self.dev:
            @self.blk.sync
            def _(sync):
                self.wait(sync, "sync", self.ev[("b_dt", g)])
                self.inc(sync.dma_start(out=d["dv_dt"][g, :, 0:tt], in_=dtF[:, 0:tt]),
                         "qs", "sync", 16)
                self.inc(sync.dma_start(out=d["dv_acs"][g, :, 0:tt], in_=acsF[:, 0:tt]),
                         "qs", "sync", 16)

    def phaseC(self, g, tt, c0, prepass=False):
        d, sb = self.dram, self.sb
        S = self.reg(0, 16384, F32)
        Sbf = self.reg(16384, 8192, BF16)
        dtF = self.reg(84224, 4608, F32, parts=64)
        acsF = self.reg(93440, 4608, F32, parts=64)
        BCF = self.reg(98048, 4608, BF16).rearrange("p (a t) -> p a t", a=2)
        term1s = [self.reg(self.RX, 16384, F32),
                  self.reg(24832, 16384, F32)]
        MT = self.reg(self.RX + 16384, 16384, BF16)
        MT2 = self.reg(self.RX + 73728, 16384, BF16)
        MTv = MT.rearrange("p (h l) -> p h l", h=64)
        MT2v = MT2.rearrange("p (h l) -> p h l", h=64)
        xdr = [self.reg(self.RX + 32768, 8192, BF16),
               self.reg(61696, 8192, BF16)]
        xch = [self.reg(self.RX + 49152, 8192, BF16),
               self.reg(61696 + 8192, 8192, BF16)]
        xdd = self.reg(self.RX + 40960, 8192, BF16)
        ych = [self.reg(self.RX + 65536, 8192, BF16),
               self.reg(24832 + 24576, 8192, BF16)]
        zch = self.reg(self.RX + 57344, 8192, BF16)
        sqch = self.reg(24832 + 16384, 8192, BF16)
        smts = [self.reg(self.RX + 90112, 2048, F32),
                self.reg(61696 + 16384, 2048, F32)]
        CBts = [self.reg(self.RX + 92160, 256, BF16),
                self.reg(61696 + 18432, 256, BF16)]
        Btoks = [self.reg(self.RX + 92416, 256, BF16),
                 self.reg(61696 + 18688, 256, BF16)]
        rrepc = self.reg(self.RX + 92672, 512, F32)
        rstbf = self.reg(self.RX + 93184, 256, BF16)
        nwv = self.sb["nw"][:].unsqueeze(2)
        zoff = 128 if g == self.np_ else 0
        nch = tt // 128

        def smf(png):
            smt = smts[png]
            return dict(acs_tok=smt[:, 0:64], dt_tok=smt[:, 64:128],
                        asum=smt[:, 128:192], dd=smt[:, 192:256],
                        etile=smt[:, 256:320], ddt=smt[:, 320:384],
                        rstrow=smt[:, 384:512])

        def FEa(c):
            r = c - c0
            cs = c * 128
            png = r % 2
            sm = smf(png)
            CBt, Btok = CBts[png], Btoks[png]

            @self.blk.tensor
            def _(t, c=c, r=r, cs=cs):
                self.wait(t, "pe", self.ev[("b_dt", g)],
                          self.ev[("b_silu", g, 64 if prepass else 65)],
                          self.ev.get(("c_sm", g, r - 1)))
                p4 = self.bank(4)
                t.transpose(p4[0:128, 0:64], in_=acsF[:, cs:cs + 128],
                            identity=sb["idf"][0:64, 0:64])
                t.transpose(p4[0:128, 64:128], in_=dtF[:, cs:cs + 128],
                            identity=sb["idf"][0:64, 0:64])
                last = t.transpose(p4.bitcast(BF16)[:, 512:640],
                                   in_=BCF[:, 0, cs:cs + 128], identity=sb["idb"])
                if not prepass:
                    last = t.matmul(p4[:, 128:256], lhsT=BCF[:, 0, cs:cs + 128],
                                    rhs=BCF[:, 1, cs:cs + 128], start=True, stop=True)
                ev = self.inc(last, "pe", "pe")
                self.ev[("c_p4", g, r)] = ev

            @self.blk.vector
            def _(v, r=r, sm=sm, CBt=CBt, Btok=Btok):
                self.wait(v, "vec", self.ev[("c_p4", g, r)],
                          self.ev.get(("c_xd", g, r - 2)))
                p4 = self.bank(4)
                v.tensor_copy(out=sm["acs_tok"], in_=p4[:, 0:64])
                v.tensor_copy(out=sm["dt_tok"], in_=p4[:, 64:128])
                bv = self.inc(v.tensor_copy(out=Btok, in_=p4.bitcast(BF16)[:, 512:640]),
                              "dv", "vec")
                v.wait_ge(self.S["dv"], bv[1])
                if prepass:
                    self.ev[("c_sm", g, r)] = bv
                else:
                    ev = self.inc(v.tensor_tensor(out=CBt, in0=p4[:, 128:256],
                                                  in1=sb["trib"], op=ALU.mult), "dv", "vec")
                    self.ev[("c_sm", g, r)] = ev

            @self.blk.sync
            def _(sync, r=r, cs=cs):
                self.wait(sync, "sync", self.ev[("b_dt", g)])
                ev = self.inc(sync.dma_start(out=d["acs_sc"][g, r],
                                             in_=acsF[:, cs:cs + 128]), "qs", "sync", 16)
                self.ev[("c_acssc", g, r)] = ev

            @self.blk.sync
            def _(sync, r=r, cs=cs, png=png):
                war = self.ev.get(("c_xd", g, r - 1)) if prepass else \
                    self.ev.get(("c_ylast", g, r - 2))
                bgate = None if prepass else self.ev[("b_dt", g)]
                bgate2 = None if prepass else self.ev[("b_silu", g, 65)]
                self.wait(sync, "sync", self.ev[("b_xdma", g, 63)], war, bgate, bgate2)
                ev = self.inc(sync.dma_start(out=xdr[png],
                                             in_=d["xb_sc"][g, :, cs:cs + 128],
                                             transpose=True), "qs", "sync", 16)
                self.ev[("c_xdr", g, r)] = ev

            @self.blk.gpsimd
            def _(gp, r=r, cs=cs, png=png):
                if prepass:
                    return
                self.wait(gp, "gp", self.ev[("b_xdma", g, 63)],
                          self.ev.get(("c_yev", g, r - 2, 15)),
                          self.ev[("b_dt", g)], self.ev[("b_silu", g, 65)])
                src = d["xb_sc"][g].rearrange("(hp p) t -> p hp t", p=128)[:, :, cs:cs + 128]
                ev = self.inc(gp.dma_start(
                    out=xch[png].rearrange("p (hp t) -> p hp t", hp=32), in_=src),
                    "qg", "gp", 16)
                self.ev[("c_xch", g, r)] = ev

            for H in (0, 1):
                @self.blk.gpsimd
                def _(gp, r=r, H=H):
                    if prepass:
                        free = self.ev.get(("c_asum", g, r - 1, H))
                    else:
                        free = self.ev.get(("c_exp", g, r - 1, H))
                    self.wait(gp, "gp", self.ev[("c_acssc", g, r)], free)
                    srcp = d["acs_sc"][g, r, 32 * H:32 * (H + 1), :]
                    srcp = srcp.rearrange("h l -> (h l)").unsqueeze(0).broadcast_to([128, 4096])
                    ev = self.inc(gp.dma_start(out=term1s[H], in_=srcp), "qg", "gp", 16)
                    self.ev[("c_rep", g, r, H)] = ev

        def FEb(c):
            r = c - c0
            cs = c * 128
            png = r % 2
            sm = smf(png)
            CBt = CBts[png]

            if prepass:
                for H in (0, 1):
                    @self.blk.vector
                    def _(v, r=r, H=H, sm=sm):
                        self.wait(v, "vec", self.ev[("c_rep", g, r, H)],
                                  self.ev[("c_sm", g, r)])
                        t1v = term1s[H].rearrange("p (h l) -> p h l", h=32)
                        ev = self.inc(v.tensor_copy(
                            out=sm["asum"][:, 32 * H:32 * (H + 1)].unsqueeze(2),
                            in_=t1v[:, :, 127:128]), "dv", "vec")
                        self.ev[("c_asum", g, r, H)] = ev

                @self.blk.vector
                def _(v, r=r, sm=sm):
                    self.wait(v, "vec", self.ev[("c_asum", g, r, 1)])
                    ev = self.inc(v.tensor_tensor(out=sm["dd"], in0=sm["asum"],
                                                  in1=sm["acs_tok"],
                                                  op=ALU.subtract), "dv", "vec")
                    self.ev[("c_ddraw", g, r)] = ev

                @self.blk.scalar
                def _(a, r=r, sm=sm):
                    self.wait(a, "act", self.ev[("c_ddraw", g, r)])
                    a.activation(out=sm["dd"], in_=sm["dd"], func=AF.Exp)
                    ev = self.inc(a.activation(out=sm["etile"], in_=sm["asum"],
                                               func=AF.Exp), "ac", "act")
                    self.ev[("c_dde", g, r)] = ev

                @self.blk.vector
                def _(v, r=r, sm=sm, png=png):
                    self.wait(v, "vec", self.ev[("c_dde", g, r)],
                              self.ev[("c_xdr", g, r)],
                              self.ev.get(("c_stmm", g, r - 1, 1)))
                    bv = self.inc(v.tensor_tensor(out=sm["ddt"], in0=sm["dd"],
                                                  in1=sm["dt_tok"], op=ALU.mult),
                                  "dv", "vec")
                    v.wait_ge(self.S["dv"], bv[1])
                    xv = xdr[png].rearrange("p (h q) -> p h q", h=64)
                    ev = self.inc(v.tensor_tensor(
                        out=xdd.rearrange("p (h q) -> p h q", h=64), in0=xv,
                        in1=sm["ddt"].unsqueeze(2).broadcast_to([128, 64, 64]),
                        op=ALU.mult), "dv", "vec")
                    self.ev[("c_xd", g, r)] = ev
                return

            @self.blk.gpsimd
            def _(gp, r=r, sm=sm):
                self.wait(gp, "gp", self.ev[("c_rep", g, r, 0)],
                          self.ev[("c_rep", g, r, 1)], self.ev[("c_sm", g, r)],
                          self.ev.get(("c_dde", g, r - 2)))
                for H in (0, 1):
                    t1v = term1s[H].rearrange("p (h l) -> p h l", h=32)
                    gp.tensor_copy(out=sm["asum"][:, 32 * H:32 * (H + 1)].unsqueeze(2),
                                   in_=t1v[:, :, 127:128])
                ev = self.inc(gp.tensor_tensor(out=sm["dd"], in0=sm["asum"],
                                               in1=sm["acs_tok"], op=ALU.subtract),
                              "po", "gp")
                self.ev[("c_ddraw", g, r)] = ev

            @self.blk.scalar
            def _(a, r=r, sm=sm):
                self.wait(a, "act", self.ev[("c_ddraw", g, r)])
                a.activation(out=sm["dd"], in_=sm["dd"], func=AF.Exp)
                ev = self.inc(a.activation(out=sm["etile"], in_=sm["asum"],
                                           func=AF.Exp), "ac", "act")
                self.ev[("c_dde", g, r)] = ev

            @self.blk.gpsimd
            def _(gp, r=r, sm=sm, png=png):
                self.wait(gp, "gp", self.ev[("c_dde", g, r)], self.ev[("c_xdr", g, r)],
                          self.ev.get(("c_stmm", g, r - 1, 1)))
                bv = self.inc(gp.tensor_tensor(out=sm["ddt"], in0=sm["dd"],
                                               in1=sm["dt_tok"], op=ALU.mult),
                              "po", "gp")
                gp.wait_ge(self.S["po"], bv[1])
                xv = xdr[png].rearrange("p (h q) -> p h q", h=64)
                gp.tensor_tensor(out=xdd.rearrange("p (h q) -> p h q", h=64),
                                 in0=xv,
                                 in1=sm["ddt"].unsqueeze(2).broadcast_to([128, 64, 64]),
                                 op=ALU.mult)
                ev = self.inc(gp.tensor_tensor(
                    out=xv, in0=xv,
                    in1=sm["dt_tok"].unsqueeze(2).broadcast_to([128, 64, 64]),
                    op=ALU.mult), "po", "gp")
                self.ev[("c_xd", g, r)] = ev

            for H in (0, 1):
                @self.blk.scalar
                def _(a, r=r, H=H):
                    war = self.ev.get(("c_y", g, r - 1, 7)) if H == 0 else \
                        self.ev.get(("c_ylast", g, r - 1))
                    self.wait(a, "act", self.ev[("c_rep", g, r, H)], war,
                              self.ev.get(("c_devtap", g)) if (self.dev and r == 2) else None)
                    ev = self.inc(a.activation(out=MT2[:, 4096 * H:4096 * (H + 1)],
                                               in_=term1s[H], func=AF.Exp), "ac", "act")
                    self.ev[("c_e2", g, r, H)] = ev

            for H in (0, 1):
                @self.blk.gpsimd
                def _(gp, r=r, H=H, sm=sm):
                    self.wait(gp, "gp", self.ev[("c_e2", g, r, H)],
                              self.ev[("c_sm", g, r)])
                    t1v = term1s[H].rearrange("p (h l) -> p h l", h=32)
                    view = sm["acs_tok"][:, 32 * H:32 * (H + 1)].unsqueeze(2) \
                        .broadcast_to([128, 32, 128])
                    ev = self.inc(gp.tensor_tensor(out=t1v, in0=view, in1=t1v,
                                                   op=ALU.subtract), "po", "gp")
                    self.ev[("c_dneg", g, r, H)] = ev

            for H in (0, 1):
                @self.blk.scalar
                def _(a, r=r, H=H):
                    war = self.ev.get(("c_y", g, r - 1, 7)) if H == 0 else \
                        self.ev.get(("c_ylast", g, r - 1))
                    self.wait(a, "act", self.ev[("c_dneg", g, r, H)], war)
                    a.activation(out=term1s[H], in_=term1s[H], func=AF.Relu)
                    ev = self.inc(a.activation(out=MT[:, 4096 * H:4096 * (H + 1)],
                                               in_=term1s[H], func=AF.Exp, scale=-1.0),
                                  "ac", "act")
                    self.ev[("c_exp", g, r, H)] = ev

            for H in (0, 1):
                @self.blk.vector
                def _(v, r=r, H=H, cs=cs):
                    self.wait(v, "vec", self.ev[("c_e2", g, r, H)],
                              self.ev[("b_silu", g, 65)])
                    m2v = MT2v[:, 32 * H:32 * (H + 1), :]
                    cview = BCF[:, 1, cs:cs + 128].unsqueeze(1).broadcast_to([128, 32, 128])
                    ev = self.inc(v.tensor_tensor(out=m2v, in0=m2v, in1=cview,
                                                  op=ALU.mult), "dv", "vec")
                    self.ev[("c_mt2", g, r, H)] = ev

            for H in (0, 1):
                @self.blk.gpsimd
                def _(gp, r=r, H=H, CBt=CBt):
                    self.wait(gp, "gp", self.ev[("c_exp", g, r, H)])
                    mtv_ = MTv[:, 32 * H:32 * (H + 1), :]
                    cbv = CBt.unsqueeze(1).broadcast_to([128, 32, 128])
                    ev = self.inc(gp.tensor_tensor(out=mtv_, in0=mtv_, in1=cbv,
                                                   op=ALU.mult), "po", "gp")
                    self.ev[("c_mt", g, r, H)] = ev

        def BE1(c):
            r = c - c0
            png = r % 2
            sm = smf(png)
            Btok = Btoks[png]

            @self.blk.tensor
            def _(t, r=r, Btok=Btok):
                self.wait(t, "pe", self.ev[("c_xd", g, r)], self.ev[("c_sm", g, r)],
                          self.ev.get(("c_sadd", g, r - 1, 1)))
                last = None
                for i in range(4):
                    last = t.matmul(self.bank(i), lhsT=Btok,
                                    rhs=xdd[:, i * 512:(i + 1) * 512],
                                    start=True, stop=True)
                ev = self.inc(last, "pe", "pe")
                self.ev[("c_stmm", g, r, 0)] = ev

            @self.blk.vector
            def _(v, r=r, sm=sm):
                self.wait(v, "vec", self.ev[("c_dde", g, r)])
                v.tensor_tensor(out=S.rearrange("p (h q) -> p h q", h=64),
                                in0=S.rearrange("p (h q) -> p h q", h=64),
                                in1=sm["etile"].unsqueeze(2).broadcast_to([128, 64, 64]),
                                op=ALU.mult)
                self.wait(v, "vec", self.ev[("c_stmm", g, r, 0)])
                last = None
                for i in range(4):
                    last = v.tensor_tensor(out=S[:, i * 512:(i + 1) * 512],
                                           in0=S[:, i * 512:(i + 1) * 512],
                                           in1=self.bank(i), op=ALU.add)
                ev = self.inc(last, "dv", "vec")
                self.ev[("c_sadd", g, r, 0)] = ev

            @self.blk.tensor
            def _(t, r=r, Btok=Btok):
                self.wait(t, "pe", self.ev[("c_sadd", g, r, 0)])
                last = None
                for i in range(4):
                    last = t.matmul(self.bank(i), lhsT=Btok,
                                    rhs=xdd[:, 2048 + i * 512:2048 + (i + 1) * 512],
                                    start=True, stop=True)
                ev = self.inc(last, "pe", "pe")
                self.ev[("c_stmm", g, r, 1)] = ev

            @self.blk.vector
            def _(v, r=r):
                self.wait(v, "vec", self.ev[("c_stmm", g, r, 1)])
                last = None
                for i in range(4):
                    last = v.tensor_tensor(out=S[:, 2048 + i * 512:2048 + (i + 1) * 512],
                                           in0=S[:, 2048 + i * 512:2048 + (i + 1) * 512],
                                           in1=self.bank(i), op=ALU.add)
                ev = self.inc(last, "dv", "vec")
                self.ev[("c_sadd", g, r, 1)] = ev

        def BE2(c):
            r = c - c0
            png = r % 2
            sm = smf(png)
            for j in range(16):
                q = j % 2
                bk = 5 + q
                heads = [8 * (j // 2) + 2 * i + q for i in range(4)]

                @self.blk.tensor
                def _(t, r=r, j=j, q=q, bk=bk, heads=heads, png=png):
                    bank_war = self.ev.get(("c_yev", g, r - 1, 14 + q)) if j < 2 \
                        else self.ev[("c_yev", g, r, j - 2)]
                    self.wait(t, "pe",
                              self.ev[("c_mt", g, r, 0 if j < 8 else 1)],
                              self.ev[("c_mt2", g, r, 0 if j < 8 else 1)],
                              self.ev[("c_xd", g, r)],
                              self.ev.get(("c_sbf", g, r - 1)),
                              self.ev.get(("c_rcp", g, r - 1)) if bk == 6 else None,
                              bank_war)
                    pb = self.bank(bk)
                    first = True
                    last = None
                    for i, h in enumerate(heads):
                        outp = pb[0:64, i * 128:(i + 1) * 128]
                        last = t.matmul(outp, lhsT=xdr[png][:, h * 64:h * 64 + 64],
                                        rhs=MTv[:, h, :], start=first, stop=False)
                        first = False
                        last = t.matmul(outp, lhsT=Sbf[:, h * 64:h * 64 + 64],
                                        rhs=MT2v[:, h, :],
                                        start=False, stop=(i == 3))
                    ev = self.inc(last, "pe", "pe")
                    self.ev[("c_y", g, r, j)] = ev
                    if j == 15:
                        self.ev[("c_ylast", g, r)] = ev

                @self.blk.vector
                def _(v, r=r, j=j, q=q, bk=bk, heads=heads, png=png):
                    self.wait(v, "vec", self.ev[("c_y", g, r, j)],
                              self.ev[("c_xch", g, r)],
                              self.ev.get(("c_ydma", g, r - 2)))
                    pb = self.bank(bk)
                    b0 = q * 64
                    xv = xch[png].rearrange("p (hp t) -> p hp t", hp=32)
                    yv = ych[png].rearrange("p (hp t) -> p hp t", hp=32)
                    last = None
                    for i, h in enumerate(heads):
                        hp = h // 2
                        last = v.scalar_tensor_tensor(
                            out=yv[b0:b0 + 64, hp, :], in0=xv[b0:b0 + 64, hp, :],
                            scalar=sb["s_dpar"][b0:b0 + 64, hp:hp + 1],
                            in1=pb[0:64, i * 128:(i + 1) * 128],
                            op0=ALU.mult, op1=ALU.add)
                    ev = self.inc(last, "dv", "vec")
                    self.ev[("c_yev", g, r, j)] = ev

            @self.blk.scalar
            def _(a, r=r):
                self.wait(a, "act", self.ev[("c_ylast", g, r)],
                          self.ev[("c_sadd", g, r, 1)])
                ev = self.inc(a.activation(out=Sbf, in_=S, func=AF.Copy), "ac", "act")
                self.ev[("c_sbf", g, r)] = ev

            @self.blk.sync
            def _(sync, r=r):
                self.wait(sync, "sync", self.ev[("b_zdma", g, 31)],
                          self.ev.get(("c_yg", g, r - 1)))
                ev = self.inc(sync.dma_start(
                    out=zch.rearrange("p (hp t) -> p hp t", hp=32),
                    in_=d["z_sc"][g].rearrange("(hp p) t -> p hp t", p=128)[:, :, zoff + r * 128:zoff + (r + 1) * 128]),
                    "qs", "sync", 16)
                self.ev[("c_zld", g, r)] = ev

            @self.blk.scalar
            def _(a, r=r):
                self.wait(a, "act", self.ev[("c_zld", g, r)])
                ev = self.inc(a.activation(out=zch, in_=zch, func=AF.Silu), "ac", "act")
                self.ev[("c_zsilu", g, r)] = ev

            @self.blk.vector
            def _(v, r=r, png=png):
                self.wait(v, "vec", self.ev[("c_zsilu", g, r)],
                          self.ev[("c_yev", g, r, 15)])
                bv = self.inc(v.tensor_tensor(out=ych[png], in0=ych[png], in1=zch,
                                              op=ALU.mult), "dv", "vec")
                self.ev[("c_yg", g, r)] = bv
                v.wait_ge(self.S["dv"], bv[1])
                ev = self.inc(v.tensor_tensor(out=sqch, in0=ych[png], in1=ych[png],
                                              op=ALU.mult), "dv", "vec")
                self.ev[("c_sq", g, r)] = ev

            @self.blk.tensor
            def _(t, r=r):
                self.wait(t, "pe", self.ev[("c_sq", g, r)],
                          self.ev.get(("c_rstd", g, r - 1)))
                sv = sqch.rearrange("p (hp t) -> p hp t", hp=32)
                last = None
                for hp in range(32):
                    last = t.matmul(self.bank(7)[0:1, 0:128], lhsT=sb["onesb"][:],
                                    rhs=sv[:, hp, :], start=(hp == 0), stop=(hp == 31))
                ev = self.inc(last, "pe", "pe")
                self.ev[("c_rms", g, r)] = ev

            @self.blk.scalar
            def _(a, r=r, sm=sm):
                self.wait(a, "act", self.ev[("c_rms", g, r)])
                a.activation(out=sm["rstrow"][0:1, 0:128], in_=self.bank(7)[0:1, 0:128],
                             func=AF.Ln, scale=1.0 / DI, bias=EPS)
                a.activation(out=sm["rstrow"][0:1, 0:128], in_=sm["rstrow"][0:1, 0:128],
                             func=AF.Exp, scale=-0.5)
                ev = self.inc(a.activation(out=rstbf[0:1, 0:128],
                                           in_=sm["rstrow"][0:1, 0:128], func=AF.Copy),
                              "ac", "act")
                self.ev[("c_rstd", g, r)] = ev

            @self.blk.tensor
            def _(t, r=r):
                self.wait(t, "pe", self.ev[("c_rstd", g, r)],
                          self.ev.get(("c_rcp", g, r - 1)))
                ev = self.inc(t.matmul(self.bank(6)[:, 0:128],
                                       lhsT=sb["trib"][0:1, :],
                                       rhs=rstbf[0:1, 0:128],
                                       start=True, stop=True), "pe", "pe")
                self.ev[("c_rrep", g, r)] = ev

            @self.blk.vector
            def _(v, r=r, png=png):
                self.wait(v, "vec", self.ev[("c_rrep", g, r)])
                bv = self.inc(v.tensor_copy(out=rrepc, in_=self.bank(6)[:, 0:128]),
                              "dv", "vec")
                self.ev[("c_rcp", g, r)] = bv
                v.wait_ge(self.S["dv"], bv[1])
                y3 = ych[png].rearrange("p (hp t) -> p hp t", hp=32)
                v.tensor_tensor(out=y3, in0=y3, in1=nwv.broadcast_to([128, 32, 128]),
                                op=ALU.mult)
                ev = self.inc(v.tensor_tensor(
                    out=y3, in0=y3,
                    in1=rrepc.unsqueeze(1).broadcast_to([128, 32, 128]),
                    op=ALU.mult), "dv", "vec")
                self.ev[("c_yn", g, r)] = ev

            @self.blk.sync
            def _(sync, r=r, png=png):
                self.wait(sync, "sync", self.ev[("c_yn", g, r)])
                dst = d["y_sc"][g].rearrange("(hp p) t -> p hp t", p=128)[:, :, r * 128:(r + 1) * 128]
                ev = self.inc(sync.dma_start(
                    out=dst, in_=ych[png].rearrange("p (hp t) -> p hp t", hp=32)),
                    "qs", "sync", 16)
                self.ev[("c_ydma", g, r)] = ev

            if self.dev and r == 1:
                @self.blk.sync
                def _(sync, r=r, png=png):
                    self.wait(sync, "sync", self.ev[("c_ylast", g, r)],
                              self.ev[("c_yev", g, r, 15)])
                    self.inc(sync.dma_start(out=d["dv_mt"][g], in_=MT), "qs", "sync", 16)
                    self.inc(sync.dma_start(out=d["dv_mt2"][g], in_=MT2), "qs", "sync", 16)
                    self.inc(sync.dma_start(out=d["dv_xch"][g], in_=xch[png]), "qs", "sync", 16)
                    ev = self.inc(sync.dma_start(out=d["dv_cbt"][g], in_=CBts[png]), "qs", "sync", 16)
                    self.ev[("c_devtap", g)] = ev

        if prepass:
            for c in range(c0, nch):
                FEa(c)
                FEb(c)
                BE1(c)
            self.ev[("slot_done", g)] = self.ev[("c_sadd", g, nch - c0 - 1, 1)]
            return

        FEa(c0)
        FEb(c0)
        for c in range(c0, nch):
            BE1(c)
            if c + 1 < nch:
                FEa(c + 1)
            BE2(c)
            if c + 1 < nch:
                FEb(c + 1)

        self.ev[("c_done", g)] = self.ev[("c_ydma", g, nch - c0 - 1)]
        self.ev[("c_yevlast", g)] = self.ev[("c_yev", g, nch - c0 - 1, 15)]
        if self.dev:
            @self.blk.sync
            def _(sync):
                self.wait(sync, "sync", self.ev[("c_sbf", g, nch - c0 - 1)])
                self.inc(sync.dma_start(out=d["dv_S"][g], in_=S), "qs", "sync", 16)

    def phaseD(self, g):
        d, sb = self.dram, self.sb
        YN = self.reg(self.RX, 65536, BF16)
        ysl = self.reg(self.RX + 65536, 8192, BF16)
        zsl = self.reg(self.RX + 73728, 8192, BF16)
        qsl = self.reg(self.RX + 81920, 8192, BF16)
        rrep = self.reg(self.RX + 90112, 4096, F32)
        rsum = self.reg(self.RX + 94208, 4096, F32)
        zoff = 128 if g == self.np_ else 0

        for s8 in range(8):
            @self.blk.sync
            def _(sync, s8=s8):
                self.wait(sync, "sync", self.ev[("c_done", g)],
                          self.ev.get(("d_mv", g, s8 - 1)))
                sync.dma_start(
                    out=ysl.rearrange("p (hp t) -> p hp t", hp=4),
                    in_=d["y_sc"][g].rearrange("(hp p) t -> p hp t", p=128)[:, 4 * s8:4 * s8 + 4, :]
                ).then_inc(self.S["qs"], 16)
                self.c["qs"] += 16
                self.wait(sync, "sync", self.ev.get(("d_yg", g, s8 - 1)))
                ev = self.inc(sync.dma_start(
                    out=zsl.rearrange("p (hp t) -> p hp t", hp=4),
                    in_=d["z_sc"][g].rearrange("(hp p) t -> p hp t", p=128)[:, 4 * s8:4 * s8 + 4, zoff:zoff + 1024]),
                    "qs", "sync", 16)
                self.ev[("d_ld", g, s8)] = ev

            @self.blk.scalar
            def _(a, s8=s8):
                self.wait(a, "act", self.ev[("d_ld", g, s8)])
                ev = self.inc(a.activation(out=zsl, in_=zsl, func=AF.Silu), "ac", "act")
                self.ev[("d_silu", g, s8)] = ev

            @self.blk.vector
            def _(v, s8=s8):
                self.wait(v, "vec", self.ev[("d_silu", g, s8)])
                ev = self.inc(v.tensor_tensor(out=ysl, in0=ysl, in1=zsl, op=ALU.mult),
                              "dv", "vec")
                self.ev[("d_yg", g, s8)] = ev

            @self.blk.scalar
            def _(a, s8=s8):
                self.wait(a, "act", self.ev[("d_yg", g, s8)],
                          self.ev.get(("d_rmsmm", g, s8 - 1)))
                ev = self.inc(a.activation(out=qsl, in_=ysl, func=AF.Square), "ac", "act")
                self.ev[("d_sq", g, s8)] = ev

            @self.blk.vector
            def _(v, s8=s8):
                self.wait(v, "vec", self.ev[("d_sq", g, s8)])
                ev = self.inc(v.tensor_copy(out=YN[:, s8 * 4096:(s8 + 1) * 4096],
                                            in_=ysl), "dv", "vec")
                self.ev[("d_mv", g, s8)] = ev

            @self.blk.tensor
            def _(t, s8=s8):
                self.wait(t, "pe", self.ev[("d_sq", g, s8)],
                          self.ev.get(("d_racc", g, s8 - 1)))
                qv = qsl.rearrange("p (hp t) -> p hp t", hp=4)
                last = None
                for fi in range(2):
                    for hp in range(4):
                        last = t.matmul(self.bank(7 - fi)[0:1, 0:512],
                                        lhsT=sb["onesb"][:],
                                        rhs=qv[:, hp, fi * 512:(fi + 1) * 512],
                                        start=(hp == 0), stop=(hp == 3))
                ev = self.inc(last, "pe", "pe")
                self.ev[("d_rmsmm", g, s8)] = ev

            @self.blk.vector
            def _(v, s8=s8):
                self.wait(v, "vec", self.ev[("d_rmsmm", g, s8)])
                last = None
                for fi in range(2):
                    if s8 == 0:
                        last = v.tensor_copy(out=rsum[0:1, fi * 512:(fi + 1) * 512],
                                             in_=self.bank(7 - fi)[0:1, 0:512])
                    else:
                        last = v.tensor_tensor(out=rsum[0:1, fi * 512:(fi + 1) * 512],
                                               in0=rsum[0:1, fi * 512:(fi + 1) * 512],
                                               in1=self.bank(7 - fi)[0:1, 0:512],
                                               op=ALU.add)
                ev = self.inc(last, "dv", "vec")
                self.ev[("d_racc", g, s8)] = ev

        @self.blk.scalar
        def _(a):
            self.wait(a, "act", self.ev[("d_racc", g, 7)])
            rr = rrep[0:1, 0:1024]
            a.activation(out=rr, in_=rsum[0:1, 0:1024], func=AF.Ln,
                         scale=1.0 / DI, bias=EPS)
            ev = self.inc(a.activation(out=rr, in_=rr, func=AF.Exp, scale=-0.5),
                          "ac", "act")
            self.ev[("d_rsb", g)] = ev

        @self.blk.sync
        def _(sync):
            self.wait(sync, "sync", self.ev[("d_rsb", g)])
            ev = self.inc(sync.dma_start(out=d["rstd_sc"][g], in_=rrep[0:1, :]),
                          "qs", "sync", 16)
            self.ev[("d_rdma", g)] = ev

        @self.blk.gpsimd
        def _(gp):
            self.wait(gp, "gp", self.ev[("d_rdma", g)])
            src = d["rstd_sc"][g, 0].unsqueeze(0).broadcast_to([128, 1024])
            ev = self.inc(gp.dma_start(out=rrep, in_=src), "qg", "gp", 16)
            self.ev[("d_rrep", g)] = ev

        @self.blk.vector
        def _(v):
            self.wait(v, "vec", self.ev[("d_rrep", g)], self.ev[("d_mv", g, 7)])
            ynv = YN.rearrange("p (hp t) -> p hp t", hp=32)
            last = None
            for hp in range(32):
                last = v.scalar_tensor_tensor(out=ynv[:, hp, :], in0=ynv[:, hp, :],
                                              scalar=sb["nw"][:, hp:hp + 1], in1=rrep,
                                              op0=ALU.mult, op1=ALU.mult)
            ev = self.inc(last, "dv", "vec")
            self.ev[("d_yn", g)] = ev

    def phaseE(self, g, gmain, tb, c0):
        d, sb = self.dram, self.sb
        YN = self.reg(self.RX, 65536, BF16).rearrange("p (kc t) -> p kc t", kc=32)
        wbuf = [self.reg(61696, 8192, BF16), self.reg(61696 + 8192, 8192, BF16)]
        at_t = [self.reg(self.RX + 65536, 2048, BF16),
                self.reg(self.RX + 67584, 2048, BF16)]

        @self.blk.sync
        def _(sync):
            self.wait(sync, "sync", self.ev[("c_done", g)])
            ev = self.inc(sync.dma_start(
                out=YN, in_=d["y_sc"][g].rearrange("(kc p) t -> p kc t", p=128)),
                "qs", "sync", 16)
            self.ev[("e_yn", g)] = ev

        for oc in range(NOC_OUT):
            p = oc % 2

            @self.blk.gpsimd
            def _(gp, oc=oc, p=p):
                gate = self.ev[("c_yevlast", g)] if oc < 2 else None
                self.wait(gp, "gp", gate, self.ev.get(("e_mm", g, oc - 2)))
                dst = wbuf[p].rearrange("p (kc o) -> p kc o", kc=KC_OUT)
                ev = self.inc(gp.dma_start(out=dst, in_=d["wout"][oc]), "qg", "gp", 16)
                self.ev[("e_w", g, oc)] = ev

            @self.blk.tensor
            def _(t, oc=oc, p=p):
                self.wait(t, "pe", self.ev[("e_w", g, oc)], self.ev[("e_yn", g)],
                          self.ev.get(("e_evac", g, oc - 2)))
                wv = wbuf[p].rearrange("p (kc o) -> p kc o", kc=KC_OUT)
                last = None
                for fi in range(2):
                    pb = self.bank(2 * p + fi)
                    for kc in range(KC_OUT):
                        last = t.matmul(pb, lhsT=wv[:, kc, :],
                                        rhs=YN[:, kc, fi * 512:(fi + 1) * 512],
                                        start=(kc == 0), stop=(kc == KC_OUT - 1))
                ev = self.inc(last, "pe", "pe")
                self.ev[("e_mm", g, oc)] = ev

            @self.blk.vector
            def _(v, oc=oc, p=p):
                self.wait(v, "vec", self.ev[("e_mm", g, oc)],
                          self.ev.get(("e_atdma", g, oc - 2)))
                last = None
                for fi in range(2):
                    last = v.tensor_copy(out=at_t[p][:, fi * 512:(fi + 1) * 512],
                                         in_=self.bank(2 * p + fi))
                ev = self.inc(last, "dv", "vec")
                self.ev[("e_evac", g, oc)] = ev

            @self.blk.sync
            def _(sync, oc=oc, p=p):
                self.wait(sync, "sync", self.ev[("e_evac", g, oc)])
                ev = self.inc(sync.dma_start(
                    out=d["attn_sc"][g, oc * 128:(oc + 1) * 128, :], in_=at_t[p]),
                    "qs", "sync", 16)
                self.ev[("e_atdma", g, oc)] = ev

        at_tok = [self.reg(self.RX, 4096, BF16), self.reg(self.RX + 4096, 4096, BF16),
                  self.reg(61696, 4096, BF16)]
        u2 = [self.reg(self.RX + 8192, 8192, F32), self.reg(self.RX + 16384, 8192, F32),
              self.reg(61696 + 4096, 8192, F32)]
        x1 = [self.reg(self.RX + 24576, 8192, F32), self.reg(self.RX + 32768, 8192, F32),
              self.reg(61696 + 12288, 8192, F32)]
        xn2 = [self.reg(self.RX + 40960, 4096, BF16), self.reg(self.RX + 45056, 4096, BF16),
               self.reg(self.RX + 57344, 4096, BF16)]
        sq2 = self.reg(self.RX + 49152, 8192, F32)
        scol = sb["scol"]
        XFB = self.reg(165888, 32768, BF16)

        for i in range(8):
            p = i % 3
            urow = (tb + c0 + i) * 128

            @self.blk.sync
            def _(sync, i=i, p=p, urow=urow):
                self.wait(sync, "sync", self.ev[("e_atdma", g, NOC_OUT - 1)],
                          self.ev.get(("e_ln2", g, i - 3)))
                sync.dma_start(out=at_tok[p],
                               in_=d["attn_sc"][g, :, i * 128:(i + 1) * 128],
                               transpose=True).then_inc(self.S["qs"], 16)
                self.c["qs"] += 16
                self.wait(sync, "sync", self.ev.get(("e_sq", g, i - 3)))
                ev = self.inc(sync.dma_start(out=u2[p], in_=d["u"][urow:urow + 128, :]),
                              "qs", "sync", 16)
                self.ev[("e_ld", g, i)] = ev

            @self.blk.vector
            def _(v, i=i, p=p):
                self.wait(v, "vec", self.ev[("e_ld", g, i)],
                          self.ev.get(("e_x1dma", g, i - 3)))
                ev = self.inc(v.tensor_tensor(out=x1[p], in0=u2[p], in1=at_tok[p],
                                              op=ALU.add), "dv", "vec")
                self.ev[("e_x1", g, i)] = ev

            @self.blk.sync
            def _(sync, i=i, p=p):
                self.wait(sync, "sync", self.ev[("e_x1", g, i)])
                ev = self.inc(sync.dma_start(out=d["x1_sc"][g, i * 128:(i + 1) * 128, :],
                                             in_=x1[p]), "qs", "sync", 16)
                self.ev[("e_x1dma", g, i)] = ev

            @self.blk.scalar
            def _(a, i=i, p=p):
                self.wait(a, "act", self.ev[("e_x1", g, i)])
                ev = self.inc(a.activation(out=sq2, in_=x1[p], func=AF.Square,
                                           accum_out=scol[:, 8 * p + 1:8 * p + 2]),
                              "ac", "act")
                self.ev[("e_sq", g, i)] = ev

            @self.blk.vector
            def _(v, i=i, p=p):
                self.wait(v, "vec", self.ev[("e_sq", g, i)])
                rv = self.inc(v.tensor_reduce(out=scol[:, 8 * p:8 * p + 1], in_=x1[p],
                                              axis=mybir.AxisListType.X, op=ALU.add),
                              "dv", "vec")
                v.wait_ge(self.S["dv"], rv[1])
                v.tensor_scalar(out=scol[:, 8 * p + 2:8 * p + 3],
                                in0=scol[:, 8 * p:8 * p + 1],
                                scalar1=1.0 / DM, scalar2=None, op0=ALU.mult)
                v.tensor_scalar(out=scol[:, 8 * p + 1:8 * p + 2],
                                in0=scol[:, 8 * p + 1:8 * p + 2],
                                scalar1=1.0 / DM, scalar2=None, op0=ALU.mult)
                ev = self.inc(v.scalar_tensor_tensor(
                    out=scol[:, 8 * p + 3:8 * p + 4],
                    in0=scol[:, 8 * p + 2:8 * p + 3],
                    scalar=scol[:, 8 * p + 2:8 * p + 3],
                    in1=scol[:, 8 * p + 1:8 * p + 2],
                    op0=ALU.mult, op1=ALU.subtract), "dv", "vec")
                self.ev[("e_var", g, i)] = ev

            @self.blk.scalar
            def _(a, i=i, p=p):
                self.wait(a, "act", self.ev[("e_var", g, i)])
                a.activation(out=scol[:, 8 * p + 4:8 * p + 5],
                             in_=scol[:, 8 * p + 3:8 * p + 4],
                             func=AF.Ln, scale=-1.0, bias=EPS)
                ev = self.inc(a.activation(out=scol[:, 8 * p + 4:8 * p + 5],
                                           in_=scol[:, 8 * p + 4:8 * p + 5],
                                           func=AF.Exp, scale=-0.5), "ac", "act")
                self.ev[("e_stf", g, i)] = ev

            @self.blk.vector
            def _(v, i=i, p=p):
                self.wait(v, "vec", self.ev[("e_stf", g, i)],
                          self.ev.get(("e_xn2dma", g, i - 3)))
                ev = self.inc(v.tensor_scalar(out=xn2[p], in0=x1[p],
                                              scalar1=scol[:, 8 * p + 2:8 * p + 3],
                                              scalar2=scol[:, 8 * p + 4:8 * p + 5],
                                              op0=ALU.subtract, op1=ALU.mult),
                              "dv", "vec")
                self.ev[("e_ln2", g, i)] = ev

            @self.blk.sync
            def _(sync, i=i, p=p):
                self.wait(sync, "sync", self.ev[("e_ln2", g, i)])
                ev = self.inc(sync.dma_start(out=d["xn2_sc"][g, i * 128:(i + 1) * 128, :],
                                             in_=xn2[p]), "qs", "sync", 16)
                self.ev[("e_xn2dma", g, i)] = ev

        for hh in (0, 1):
            @self.blk.scalar
            def _(a, hh=hh):
                self.wait(a, "act", self.ev[("e_xn2dma", g, 4 * hh + 3)])
                XFBv = XFB.rearrange("p (kc t) -> p kc t", kc=16)
                last = None
                for kc in range(16):
                    last = self.inc(a.dma_start(
                        out=XFBv[:, kc, hh * 512:(hh + 1) * 512],
                        in_=d["xn2_sc"][g, hh * 512:(hh + 1) * 512,
                                        kc * 128:(kc + 1) * 128], transpose=True),
                        "qc", "act", 16)
                if hh == 1:
                    self.ev[("e_xfb", g)] = last

    def phaseF(self, g, gmain):
        d, sb = self.dram, self.sb
        XFB = self.reg(165888, 32768, BF16).rearrange("p (kc t) -> p kc t", kc=16)
        wbuf = [self.reg(61696 + 4096 * j, 4096, BF16) for j in range(4)]
        wtouch = self.reg(61696 + 16384, 16, BF16)
        gate_t = [self.reg(self.RX, 2048, BF16), self.reg(self.RX + 2048, 2048, BF16)]
        h_t = [self.reg(self.RX + 4096, 2048, BF16), self.reg(self.RX + 6144, 2048, BF16)]

        for oc in range(NOC_G):
            p = oc % 2

            @self.blk.gpsimd
            def _(gp, oc=oc, p=p):
                gate = self.ev[("e_xfb", g)] if oc < 2 else None
                self.wait(gp, "gp", gate, self.ev.get(("f_gmm", g, oc - 2)))
                ev = self.inc(gp.dma_start(
                    out=wbuf[2 * p].rearrange("p (kc o) -> p kc o", kc=16),
                    in_=d["wg"][oc]), "qg", "gp", 16)
                self.ev[("f_wgld", g, oc)] = ev
                self.wait(gp, "gp", self.ev.get(("f_umm", g, oc - 2)))
                ev2 = self.inc(gp.dma_start(
                    out=wbuf[2 * p + 1].rearrange("p (kc o) -> p kc o", kc=16),
                    in_=d["wu"][oc]), "qg", "gp", 16)
                self.ev[("f_wuld", g, oc)] = ev2

            @self.blk.vector
            def _(v, oc=oc, p=p):
                self.wait(v, "vec", self.ev[("f_wgld", g, oc)])
                v.tensor_scalar(out=wbuf[2 * p], in0=wbuf[2 * p],
                                scalar1=-128.0, scalar2=None, op0=ALU.add)
                ev = self.inc(v.tensor_copy(out=wtouch[:, 0:8],
                                            in_=wbuf[2 * p][:, 2040:2048]), "dv", "vec")
                self.ev[("f_wgs", g, oc)] = ev
                self.wait(v, "vec", self.ev[("f_wuld", g, oc)])
                v.tensor_scalar(out=wbuf[2 * p + 1], in0=wbuf[2 * p + 1],
                                scalar1=-128.0, scalar2=None, op0=ALU.add)
                ev2 = self.inc(v.tensor_copy(out=wtouch[:, 0:8],
                                             in_=wbuf[2 * p + 1][:, 2040:2048]), "dv", "vec")
                self.ev[("f_wus", g, oc)] = ev2

            @self.blk.tensor
            def _(t, oc=oc, p=p):
                self.wait(t, "pe", self.ev[("f_wgs", g, oc)], self.ev[("e_xfb", g)],
                          self.ev.get(("f_gev", g, oc - 1)))
                wv = wbuf[2 * p].rearrange("p (kc o) -> p kc o", kc=16)
                last = None
                for fi in range(2):
                    for kc in range(16):
                        last = t.matmul(self.bank(fi), lhsT=wv[:, kc, :],
                                        rhs=XFB[:, kc, fi * 512:(fi + 1) * 512],
                                        start=(kc == 0), stop=(kc == 15))
                ev = self.inc(last, "pe", "pe")
                self.ev[("f_gmm", g, oc)] = ev

            @self.blk.scalar
            def _(a, oc=oc, p=p):
                self.wait(a, "act", self.ev[("f_gmm", g, oc)],
                          self.ev.get(("f_h", g, oc - 2)))
                last = None
                for fi in range(2):
                    last = a.activation(out=gate_t[p][:, fi * 512:(fi + 1) * 512],
                                        in_=self.bank(fi), func=AF.Silu,
                                        scale=sb["s_sg"][:, oc:oc + 1])
                ev = self.inc(last, "ac", "act")
                self.ev[("f_gev", g, oc)] = ev

            @self.blk.tensor
            def _(t, oc=oc, p=p):
                self.wait(t, "pe", self.ev[("f_wus", g, oc)],
                          self.ev.get(("f_h", g, oc - 1)))
                wv = wbuf[2 * p + 1].rearrange("p (kc o) -> p kc o", kc=16)
                last = None
                for fi in range(2):
                    for kc in range(16):
                        last = t.matmul(self.bank(2 + fi), lhsT=wv[:, kc, :],
                                        rhs=XFB[:, kc, fi * 512:(fi + 1) * 512],
                                        start=(kc == 0), stop=(kc == 15))
                ev = self.inc(last, "pe", "pe")
                self.ev[("f_umm", g, oc)] = ev

            @self.blk.vector
            def _(v, oc=oc, p=p):
                self.wait(v, "vec", self.ev[("f_umm", g, oc)], self.ev[("f_gev", g, oc)],
                          self.ev.get(("f_hdma", g, oc - 2)))
                last = None
                for fi in range(2):
                    last = v.scalar_tensor_tensor(
                        out=h_t[p][:, fi * 512:(fi + 1) * 512],
                        in0=self.bank(2 + fi), scalar=sb["s_su"][:, oc:oc + 1],
                        in1=gate_t[p][:, fi * 512:(fi + 1) * 512],
                        op0=ALU.mult, op1=ALU.mult)
                ev = self.inc(last, "dv", "vec")
                self.ev[("f_h", g, oc)] = ev

            @self.blk.sync
            def _(sync, oc=oc, p=p):
                self.wait(sync, "sync", self.ev[("f_h", g, oc)])
                ev = self.inc(sync.dma_start(out=d["h_sc"][g, oc * 128:(oc + 1) * 128, :],
                                             in_=h_t[p]), "qs", "sync", 16)
                self.ev[("f_hdma", g, oc)] = ev

        self.ev[("f_gu", g)] = self.ev[("f_hdma", g, NOC_G - 1)]

        Hb = [self.reg(self.RX + 8192, 45056, BF16),
              self.reg(57600, 45056, BF16)]
        wdb = [self.reg(self.RX + 53248, 11264, BF16),
               self.reg(self.RX + 64512, 11264, BF16)]
        fb = [self.reg(self.RX + 75776, 1024, BF16),
              self.reg(self.RX + 76800, 1024, BF16)]

        for fh in range(2):
            @self.blk.sync
            def _(sync, fh=fh):
                self.wait(sync, "sync", self.ev[("f_gu", g)],
                          self.ev[("f_gev", g, NOC_G - 1)] if fh == 1 else None)
                e = self.inc(sync.dma_start(
                    out=Hb[fh].rearrange("p (kc t) -> p kc t", kc=NOC_G),
                    in_=d["h_sc"][g].rearrange("(kc p) t -> p kc t", p=128)[:, :, fh * 512:(fh + 1) * 512]),
                    "qs", "sync", 16)
                self.ev[("f_hload", g, fh)] = e

        for fh in range(2):
            hp_ = fh
            for oc in range(NOC_D):
                p = oc % 2

                @self.blk.gpsimd
                def _(gp, fh=fh, oc=oc, p=p):
                    self.wait(gp, "gp", self.ev.get(("f_dmm", g, fh, oc - 2)),
                              self.ev.get(("f_dmm", g, fh - 1, 14 + oc)) if (oc < 2 and fh > 0) else None,
                              self.ev[("f_gu", g)] if (oc < 2 and fh == 0) else None)
                    e = self.inc(gp.dma_start(
                        out=wdb[p].rearrange("p (kc o) -> p kc o", kc=NOC_G),
                        in_=d["wd"][oc]), "qg", "gp", 16)
                    self.ev[("f_wdld", g, fh, oc)] = e

                @self.blk.vector
                def _(v, fh=fh, oc=oc, p=p):
                    self.wait(v, "vec", self.ev[("f_wdld", g, fh, oc)])
                    e = self.inc(v.tensor_scalar(out=wdb[p], in0=wdb[p], scalar1=-128.0,
                                                 scalar2=None, op0=ALU.add), "dv", "vec")
                    self.ev[("f_wds", g, fh, oc)] = e

                @self.blk.tensor
                def _(t, fh=fh, oc=oc, p=p, hp_=hp_):
                    self.wait(t, "pe", self.ev[("f_wds", g, fh, oc)],
                              self.ev[("f_hload", g, fh)],
                              self.ev[("f_gev", g, NOC_G - 1)] if (fh == 0 and oc == 0) else None,
                              self.ev.get(("f_fev", g, fh, oc - 2)))
                    wv = wdb[p].rearrange("p (kc o) -> p kc o", kc=NOC_G)
                    Hv = Hb[hp_].rearrange("p (kc t) -> p kc t", kc=NOC_G)
                    last = None
                    for kc in range(NOC_G):
                        last = t.matmul(self.bank(p), lhsT=wv[:, kc, :], rhs=Hv[:, kc, :],
                                        start=(kc == 0), stop=(kc == NOC_G - 1))
                    e = self.inc(last, "pe", "pe")
                    self.ev[("f_dmm", g, fh, oc)] = e

                @self.blk.vector
                def _(v, fh=fh, oc=oc, p=p):
                    self.wait(v, "vec", self.ev[("f_dmm", g, fh, oc)],
                              self.ev.get(("f_fdma", g, fh, oc - 2)))
                    e = self.inc(v.tensor_scalar(out=fb[p], in0=self.bank(p),
                                                 scalar1=sb["s_sd"][:, oc:oc + 1],
                                                 scalar2=None, op0=ALU.mult), "dv", "vec")
                    self.ev[("f_fev", g, fh, oc)] = e

                @self.blk.sync
                def _(sync, fh=fh, oc=oc, p=p):
                    self.wait(sync, "sync", self.ev[("f_fev", g, fh, oc)])
                    e = self.inc(sync.dma_start(
                        out=d["ffn_sc"][g, oc * 128:(oc + 1) * 128,
                                        fh * 512:(fh + 1) * 512], in_=fb[p]),
                        "qs", "sync", 16)
                    self.ev[("f_fdma", g, fh, oc)] = e

        f_tok = [self.reg(self.RX + 8192, 4096, BF16),
                 self.reg(self.RX + 12288, 4096, BF16)]
        x1b = [self.reg(self.RX + 16384, 8192, F32), self.reg(self.RX + 24576, 8192, F32)]
        ot = [self.reg(self.RX + 32768, 8192, F32), self.reg(self.RX + 40960, 8192, F32)]

        for i in range(8):
            p = i % 2
            orow = gmain * 1024 + i * 128

            @self.blk.sync
            def _(sync, i=i, p=p):
                self.wait(sync, "sync", self.ev[("f_fdma", g, 1, NOC_D - 1)],
                          self.ev.get(("f_o", g, i - 2)))
                sync.dma_start(out=f_tok[p], in_=d["ffn_sc"][g, :, i * 128:(i + 1) * 128],
                               transpose=True).then_inc(self.S["qs"], 16)
                self.c["qs"] += 16
                ev = self.inc(sync.dma_start(out=x1b[p],
                                             in_=d["x1_sc"][g, i * 128:(i + 1) * 128, :]),
                              "qs", "sync", 16)
                self.ev[("f_old", g, i)] = ev

            @self.blk.vector
            def _(v, i=i, p=p):
                self.wait(v, "vec", self.ev[("f_old", g, i)],
                          self.ev.get(("f_out", g, i - 2)))
                ev = self.inc(v.tensor_tensor(out=ot[p], in0=x1b[p], in1=f_tok[p],
                                              op=ALU.add), "dv", "vec")
                self.ev[("f_o", g, i)] = ev

            @self.blk.sync
            def _(sync, i=i, p=p, orow=orow):
                self.wait(sync, "sync", self.ev[("f_o", g, i)])
                ev = self.inc(sync.dma_start(out=d["out"][orow:orow + 128, :], in_=ot[p]),
                              "qs", "sync", 16)
                self.ev[("f_out", g, i)] = ev

        self.ev[("slot_done", g)] = self.ev[("f_out", g, 7)]

    def epilogue(self):
        @self.blk.sync
        def _(sync):
            self.wait(sync, "sync", self.ev[("slot_done", self.nslots - 1)])


def host_prep(inputs):
    f32 = np.float32
    W_in = inputs["L0_attn_W_in"]
    A = np.ascontiguousarray(W_in.T).reshape(KC_IN, 128, DIP)
    win_main = np.ascontiguousarray(
        A[:, :, :66 * 128].reshape(KC_IN, 128, 66, 128).transpose(2, 1, 0, 3))
    win_tail = np.ascontiguousarray(A[:, :, 66 * 128:].transpose(1, 0, 2))
    W_out = inputs["L0_attn_W_out"]
    B = np.ascontiguousarray(W_out.T).reshape(KC_OUT, 128, DM)
    wout = np.ascontiguousarray(
        B.reshape(KC_OUT, 128, NOC_OUT, 128).transpose(2, 1, 0, 3))

    def blk_gate(wq):
        T = np.ascontiguousarray(wq.T).reshape(KC_IN, 128, HID)
        return np.ascontiguousarray(
            T.reshape(KC_IN, 128, NOC_G, 128).transpose(2, 1, 0, 3))

    wg = blk_gate(inputs["L0_ffn_wq_gate"])
    wu = blk_gate(inputs["L0_ffn_wq_up"])
    Td = np.ascontiguousarray(inputs["L0_ffn_wq_down"].T)
    wd = np.ascontiguousarray(
        Td.reshape(NOC_G, 128, NOC_D, 128).transpose(2, 1, 0, 3))

    convw = np.ascontiguousarray(
        inputs["L0_attn_conv_w"][:, 0, :].reshape(34, 128, 4).transpose(1, 0, 2))
    convb = np.ascontiguousarray(inputs["L0_attn_conv_b"].reshape(34, 128).T)
    shared = dict(
        win_main=win_main.astype(f32), win_tail=win_tail.astype(f32),
        wout=wout.astype(f32), wg=wg.astype(np.int32), wu=wu.astype(np.int32),
        wd=wd.astype(np.int32),
        convw=convw.reshape(128, 34, 4).astype(f32), convb=convb.astype(f32),
        dtb=inputs["L0_attn_dt_bias"].reshape(64, 1).astype(f32),
        alog=inputs["L0_attn_A_log"].reshape(64, 1).astype(f32),
        dpar=np.ascontiguousarray(
            np.repeat(inputs["L0_attn_D_param"], 64).reshape(32, 128).T).astype(f32),
        normw=np.ascontiguousarray(
            inputs["L0_attn_norm_w"].reshape(32, 128).T).astype(f32),
        sg=np.ascontiguousarray(inputs["L0_ffn_s_gate"].reshape(NOC_G, 128).T).astype(f32),
        su=np.ascontiguousarray(inputs["L0_ffn_s_up"].reshape(NOC_G, 128).T).astype(f32),
        sd=np.ascontiguousarray(inputs["L0_ffn_s_down"].reshape(NOC_D, 128).T).astype(f32),
        ident_bf=np.eye(128, dtype=f32).astype(ml_dtypes.bfloat16),
        ident_f=np.eye(128, dtype=f32),
        tri_bf=np.triu(np.ones((128, 128), f32)).astype(ml_dtypes.bfloat16),
        reset=np.ones((64, TTMAX), f32),
        ones_bf=np.ones((128, 1), f32).astype(ml_dtypes.bfloat16),
        cvals=np.concatenate([np.full((128, 1), EPS, f32), np.ones((128, 1), f32)], 1),
    )
    shared["reset"][:, 0::128] = 0.0
    return shared


def kernel(**inputs):
    u_full = np.asarray(inputs["L0_attn_u"])
    shared = host_prep(inputs)
    nc = bass.Bass("TRN2", target_bir_lowering=False)
    mk = Mk(nc, [9, 8], prepass_tiles=(8, 8))
    nc = mk.build()
    in_maps = []
    for c in range(8):
        b, h = c // 2, c % 2
        pad = np.zeros((128, DM), np.float32)
        if h == 1:
            pad[125:128] = u_full[b][2045:2048]
        m = dict(shared)
        m["u"] = np.ascontiguousarray(
            np.vstack([pad, u_full[b][h * 2048:(h + 1) * 2048]]))
        m["u2"] = np.ascontiguousarray(u_full[b][0:2048])
        m["smask"] = np.full((128, 1), float(h), np.float32)
        in_maps.append(m)
    res = run_bass_kernel_spmd(nc, in_maps, core_ids=list(range(8)))
    out = np.stack(
        [np.vstack([np.asarray(res.results[2 * b]["out"]),
                    np.asarray(res.results[2 * b + 1]["out"])]) for b in range(4)],
        axis=0)
    return out.astype(np.float32)
